# version 12
# baseline (speedup 1.0000x reference)
import sys
sys.path.insert(0, "/opt/trn_rl_repo")
import numpy as np
import ml_dtypes
import concourse.bass as bass
import concourse.mybir as mybir
from concourse.bass_utils import run_bass_kernel_spmd

F32 = mybir.dt.float32
BF16 = mybir.dt.bfloat16
I32 = mybir.dt.int32
U8 = mybir.dt.uint8
AF = mybir.ActivationFunctionType
ALU = mybir.AluOpType

D = 2048
NC = 16
NOWN = 2048
NFULL = 4096
NB = 4
HID0 = 5632
NH0 = 44
HID1 = 8192
NH1 = 64
EPS_LN = 1e-5
EPS_Z = 1e-6
EPS_B = 1e-6
CQK = float(1.0 / np.sqrt(128.0))
GROUPS = [[0, 1], [2, 3], [4, 5], [6, 7]]
DEBUG_L0 = False
DEBUG_L1 = False
DEBUG_FUSED = False
N_W_FC = float(HID1 * D)


class Sched:
    ENGS = ("t", "v", "s", "g", "q")

    def __init__(self, nc):
        self.nc = nc
        self.ops = {e: [] for e in self.ENGS}
        self.cnt = {}
        self.sems = {}
        self.lastw = {e: {} for e in self.ENGS}
        self._ctx = []

    def sem(self, name):
        if name not in self.sems:
            g = self.nc.semaphore(name)
            self.sems[name] = g.__enter__()
            self._ctx.append(g)
            self.cnt[name] = 0
        return self.sems[name]

    def on(self, e, fn, waits=(), inc=None, incn=1):
        wl = []
        for sname, v in waits:
            if v <= 0:
                continue
            self.sem(sname)
            if self.lastw[e].get(sname, -1) >= v:
                continue
            self.lastw[e][sname] = v
            wl.append((self.sems[sname], v))
        ih = self.sem(inc) if inc else None

        def run(eng, fn=fn, wl=tuple(wl), ih=ih, incn=incn):
            for s, v in wl:
                eng.wait_ge(s, v)
            r = fn(eng)
            if ih is not None:
                r.then_inc(ih, incn)
        self.ops[e].append(run)
        if inc:
            self.cnt[inc] += incn
            return self.cnt[inc]
        return None

    def dma(self, e, out, in_, waits=()):
        sem = "dg" if e == "g" else "dq"
        self.sem(sem)
        return self.on(e, lambda eng, o=out, i=in_: eng.dma_start(out=o, in_=i),
                       waits, inc=sem, incn=16)

    def c(self, name):
        return self.cnt.get(name, 0)

    def fence(self, engines="tvsgq"):
        vals = [(s, self.cnt[s]) for s in list(self.cnt)]
        for e in engines:
            self.on(e, lambda eng: eng.nop(), waits=vals)

    def finalize(self):
        nc = self.nc
        with nc.Block() as blk:
            @blk.tensor
            def _(t):
                for f in self.ops["t"]:
                    f(t)

            @blk.vector
            def _(v):
                for f in self.ops["v"]:
                    f(v)

            @blk.scalar
            def _(s):
                for f in self.ops["s"]:
                    f(s)

            @blk.gpsimd
            def _(g):
                for f in self.ops["g"]:
                    f(g)

            @blk.sync
            def _(q):
                for f in self.ops["q"]:
                    f(q)


def colload_half(wT, t, half):
    K = wT.shape[0]
    return wT[half * K // 2:(half + 1) * K // 2,
              t * 128:(t + 1) * 128].rearrange("(c p) o -> p c o", p=128)


def colload_ap(wT, t, w=128):
    return wT[:, t * w:(t + 1) * w].rearrange("(c p) o -> p c o", p=128)


def slab_ap(wT, c0, c1):
    return wT[:, c0:c1].rearrange("(c p) o -> p c o", p=128)


class FM:

    def __init__(self, ap, nchunk, w):
        self.ap, self.nchunk, self.w = ap, nchunk, w

    def ch(self, c, a=0, b=None):
        b = self.w if b is None else b
        return self.ap[:, c * self.w + a: c * self.w + b]

    def all3(self):
        return self.ap[:, 0:self.nchunk * self.w].rearrange("p (c n) -> p c n", n=self.w)


class Buf:

    def __init__(self, nc):
        A = lambda name, shape, dt: nc.alloc_sbuf_tensor(name, shape, dt).ap()
        self.AR1 = A("sb_AR1", [128, 65536], U8)
        self.AR3 = A("sb_AR3", [128, 65536], U8)
        self.ST = A("sb_ST", [128, 34816], U8)
        self.ROWS = A("sb_rows", [1, 4096], F32)
        self.drow = A("sb_drow", [1, 2048], BF16)
        self.ksum_sb = A("sb_ksum_sb", [1, 2048], BF16)
        self.kv16 = A("sb_kv16", [128, 2048], BF16)
        self.tmpf = A("sb_tmpf", [128, 512], F32)
        self.mb = A("sb_mb", [128, 512], F32)
        self.rb = A("sb_rb", [128, 512], F32)
        self.sqt = [A("sb_sqt0", [128, 512], BF16), A("sb_sqt1", [128, 512], BF16)]
        self.mstrip = A("sb_mstrip", [128, 896], BF16)
        self.ident = A("sb_ident", [128, 128], BF16)
        self.cst = A("sb_cst", [128, 8], F32)
        self.ones_k1 = A("sb_ones_k1", [1, 128], BF16)
        self.n128_k1 = A("sb_n128_k1", [1, 128], BF16)
        self.ones_m1 = A("sb_ones_m1", [128, 1], BF16)
        self.ones_k1f = A("sb_ones_k1f", [1, 128], F32)
        self.ones_m1f = A("sb_ones_m1f", [128, 1], F32)
        self.posf = A("sb_posf", [128, 16], F32)
        self.decay_col = A("sb_decay_col", [128, 32], F32)
        self.poso = A("sb_poso", [1, 2], F32)
        self.offb = A("sb_offb", [1, 2], F32)
        self.bq_col = A("sb_bq_col", [128, 16], F32)
        self.bo_col = A("sb_bo_col", [128, 16], F32)
        self.sg_col = A("sb_sg_col", [128, 44], F32)
        self.su_col = A("sb_su_col", [128, 44], F32)
        self.sd_col = A("sb_sd_col", [128, 16], F32)
        self.bk_row = A("sb_bk_row", [1, 1024], BF16)
        self.bv_row = A("sb_bv_row", [1, 1024], BF16)
        self.b1q = A("sb_b1q", [128, 8], F32)
        self.b1k = A("sb_b1k", [128, 8], F32)
        self.b1v = A("sb_b1v", [128, 8], F32)
        self.bo1_col = A("sb_bo1_col", [128, 16], F32)
        self.gam = A("sb_gam", [1, 4], F32)
        self.gcol = A("sb_gcol", [128, 4], F32)
        self.acc = A("sb_acc", [128, 2], F32)
        self.gacc = A("sb_gacc", [128, 16], F32)
        self.r11 = A("sb_r11", [1, 8], F32)
        self.negC = A("sb_negC", [128, 1], F32)
        AR3f = self.AR3.bitcast(F32)
        AR3b_ = self.AR3.bitcast(BF16)
        self.tA = AR3f[:, 13824:14336]
        self.tB = AR3f[:, 14336:14848]
        self.tC = AR3f[:, 14848:15360]
        self.tD = AR3b_[:, 30720:31232]
        self.tE = AR3b_[:, 31232:31744]


def bf(ap):
    return ap.bitcast(BF16)


def f32(ap):
    return ap.bitcast(F32)



def l1w_tasks(S, B, ext, PS):
    AR3b = bf(B.AR3)
    wst = B.AR3.bitcast(F32)[:, 11264:11776]
    qt = B.AR3.bitcast(F32)[:, 11776:12288]
    swm = AR3b[:, 24576:25088]
    m01 = AR3b[:, 25088:25600]
    wht = AR3b[:, 25600:26112]
    W = lambda *a: list(a)
    st = {"ld": 0, "abs": 0, "acc": 0, "sq": 0, "vm": 0, "sg": 0, "vw": 0, "dq_st": 0,
          "gi": 0}
    tasks = []

    pend = []

    def mk_gamma(wT, m, ch, j, F):
        def emit():
            slot = st["gi"] % 16
            st["gi"] += 1
            dw = S.dma("q", wst, ext[wT][ch * 128:(ch + 1) * 128, j * 512:(j + 1) * 512],
                       waits=W(("ss", st["abs"])))
            sa = S.on("s", lambda e: e.activation(qt, wst, AF.Abs),
                      waits=W(("dq", dw), ("vv", st["acc"])), inc="ss")
            st["abs"] = sa
            st["acc"] = S.on("v", lambda e, slot=slot: e.reduce_sum(
                B.gacc[:, slot:slot + 1], qt, axis=mybir.AxisListType.X),
                waits=W(("ss", sa)), inc="vv")
            pend.append((slot, m))
            if len(pend) > 8:
                s2, m2 = pend.pop(0)
                st["acc"] = S.on("v", lambda e, s2=s2, m2=m2: e.tensor_add(
                    B.acc[:, m2:m2 + 1], B.acc[:, m2:m2 + 1], B.gacc[:, s2:s2 + 1]),
                    inc="vv")
        return emit

    def mk_gflush():
        def emit():
            while pend:
                s2, m2 = pend.pop(0)
                S.on("v", lambda e: e.tensor_copy(B.gcol[:, 3:4], B.gacc[:, 0:1]),
                     inc="vv")
                S.on("v", lambda e: e.tensor_copy(B.gcol[:, 3:4], B.gacc[:, 1:2]),
                     inc="vv")
                st["acc"] = S.on("v", lambda e, s2=s2, m2=m2: e.tensor_add(
                    B.acc[:, m2:m2 + 1], B.acc[:, m2:m2 + 1], B.gacc[:, s2:s2 + 1]),
                    inc="vv")
        return emit

    def mk_gfin(m):
        def emit():
            tg1 = S.on("t", lambda e, m=m: e.matmul(PS[0:1, 7 * 512:7 * 512 + 1],
                                                    B.acc[:, m:m + 1], B.ones_m1f,
                                                    start=True, stop=True),
                       waits=W(("vv", S.c("vv")), ("ss", S.c("ss"))), inc="tt")
            sg1 = S.on("s", lambda e, m=m: e.activation(B.r11[0:1, m:m + 1],
                                                        PS[0:1, 7 * 512:7 * 512 + 1],
                                                        AF.Copy, scale=1.0 / N_W_FC),
                       waits=W(("tt", tg1)), inc="ss")
            sg2 = S.on("s", lambda e, m=m: e.activation(B.r11[0:1, 2 + m:3 + m],
                                                        B.r11[0:1, m:m + 1], AF.Copy,
                                                        bias=EPS_B), inc="ss")
            vg1 = S.on("v", lambda e, m=m: e.reciprocal(B.r11[0:1, 4 + m:5 + m],
                                                        B.r11[0:1, 2 + m:3 + m]),
                       waits=W(("ss", sg2)), inc="vv")
            tb = S.on("t", lambda e, m=m: e.matmul(PS[:, 7 * 512:7 * 512 + 1], B.ones_k1f,
                                                   B.r11[0:1, 4 + m:5 + m],
                                                   start=True, stop=True),
                      waits=W(("vv", vg1)), inc="tt")
            vb = S.on("v", lambda e, m=m: e.tensor_copy(B.gcol[:, 2 * m:2 * m + 1],
                                                        PS[:, 7 * 512:7 * 512 + 1]),
                      waits=W(("tt", tb)), inc="vv")
            tb2 = S.on("t", lambda e, m=m: e.matmul(PS[:, 7 * 512:7 * 512 + 1], B.ones_k1f,
                                                    B.r11[0:1, m:m + 1],
                                                    start=True, stop=True),
                       waits=W(("vv", vb)), inc="tt")
            st["acc"] = S.on("v", lambda e, m=m: e.tensor_copy(
                B.gcol[:, 2 * m + 1:2 * m + 2], PS[:, 7 * 512:7 * 512 + 1]),
                waits=W(("tt", tb2)), inc="vv")
        return emit

    def mk_tern(wT, outT, m, ch, j):
        def emit():
            dw = S.dma("q", wst, ext[wT][ch * 128:(ch + 1) * 128, j * 512:(j + 1) * 512],
                       waits=W(("ss", st["sg"]), ("dq", st["dq_st"])))
            sq_ = S.on("s", lambda e, m=m: e.activation(
                qt, wst, AF.Square, scale=B.gcol[:, 2 * m:2 * m + 1]),
                waits=W(("dq", dw), ("vv", st["acc"]), ("vv", st["vm"])), inc="ss")
            st["sq"] = sq_
            vm = S.on("v", lambda e: e.tensor_scalar(
                out=m01, in0=qt, scalar1=0.25, scalar2=None, op0=ALU.is_ge),
                waits=W(("ss", sq_), ("vv", st["vw"])), inc="vv")
            st["vm"] = vm
            sg_ = S.on("s", lambda e: e.activation(swm, wst, AF.Sign),
                       waits=W(("vv", st["vw"])), inc="ss")
            st["sg"] = sg_
            vw = S.on("v", lambda e, m=m: e.scalar_tensor_tensor(
                out=wht, in0=swm, scalar=B.gcol[:, 2 * m + 1:2 * m + 2],
                in1=m01, op0=ALU.mult, op1=ALU.mult),
                waits=W(("ss", sg_), ("vv", vm), ("dq", st["dq_st"])), inc="vv")
            st["vw"] = vw
            st["dq_st"] = S.dma("q", ext[outT][ch * 128:(ch + 1) * 128,
                                               j * 512:(j + 1) * 512], wht,
                                waits=W(("vv", vw)))
        return emit

    init_done = {"v": False}

    def mk_init():
        def emit():
            S.on("v", lambda e: e.memset(B.acc, 0.0), inc="vv")
        return emit

    tasks.append(mk_init())
    for m, (wT, nch, nj) in enumerate((("wfcT", 16, 16), ("wpjT", 64, 4))):
        for ch in range(nch):
            for j in range(nj):
                tasks.append(mk_gamma(wT, m, ch, j, 512))
    tasks.append(mk_gflush())
    tasks.append(mk_gfin(0))
    tasks.append(mk_gfin(1))
    for m, (wT, outT, nch, nj) in enumerate((("wfcT", "whatfc", 16, 16),
                                             ("wpjT", "whatpj", 64, 4))):
        for ch in range(nch):
            for j in range(nj):
                tasks.append(mk_tern(wT, outT, m, ch, j))
    return tasks


def build_common(S, B, nc):
    CONSTS = [EPS_LN, EPS_Z, EPS_B, -128.0, -0.25, 0.5, 1.0, 0.0]
    for j, cv in enumerate(CONSTS):
        S.on("g", lambda e, j=j, cv=cv: e.memset(B.cst[:, j:j + 1], cv))
        nc.const_aps.register(F32, cv, B.cst[:, j:j + 1])
    S.on("g", lambda e: e.memset(B.ones_k1, 1.0))
    S.on("g", lambda e: e.memset(B.n128_k1, -128.0))
    S.on("g", lambda e: e.memset(B.ones_m1, 1.0))
    S.on("g", lambda e: e.memset(B.ones_k1f, 1.0))
    S.on("g", lambda e: e.memset(B.ones_m1f, 1.0))
    S.on("g", lambda e: e.memset(B.ident, 0.0))
    S.on("g", lambda e: e.affine_select(out=B.ident, in_=B.ident, compare_op=ALU.not_equal,
                                        fill=1.0, base=0, pattern=[[-1, 128]],
                                        channel_multiplier=1))
    S.on("g", lambda e: e.memset(B.mstrip, 1.0))
    S.on("g", lambda e: e.affine_select(out=B.mstrip, in_=B.mstrip, compare_op=ALU.is_ge,
                                        fill=0.0, base=-384, pattern=[[1, 896]],
                                        channel_multiplier=-1))


def build_l0(S, B, ext, PS, l1w=None):
    nc = S.nc
    P = lambda b, n=1: PS[:, b * 512:(b + n) * 512]
    PR = lambda b, w=512: PS[0:1, b * 512:b * 512 + w]
    W = lambda *a: list(a)
    l1w = list(l1w) if l1w else []

    def drain(n):
        for _ in range(min(n, len(l1w))):
            l1w.pop(0)()

    xn0 = FM(bf(B.AR1), NC, NOWN)
    xnO = FM(bf(B.AR3), NC, NOWN)
    guA = FM(bf(B.AR3), NH0, 512)
    qA = FM(bf(B.AR3), NC, NOWN)
    atA = FM(bf(B.AR1), NC, NOWN)

    STb = bf(B.ST)
    xslab = FM(STb[:, 0:NC * 512], NC, 512)
    ktile16 = FM(STb[:, 0:NC * 512], NC, 512)
    wg_c = [FM(STb[:, i * 2048:(i + 1) * 2048], NC, 128) for i in (0, 1)]
    wu_c = [FM(STb[:, (2 + i) * 2048:(3 + i) * 2048], NC, 128) for i in (0, 1)]
    wd_c = FM(STb[:, 0:NH0 * 128], NH0, 128)
    wo_c = [FM(STb[:, i * 2048:(i + 1) * 2048], NC, 128) for i in (0, 1)]
    wbuf = FM(STb[:, 8192:8192 + NC * 512], NC, 512)
    wgd = FM(STb[:, 8192:8192 + 2048], NC, 128)
    wud = FM(STb[:, 8192 + 2048:8192 + 4096], NC, 128)
    wdd = FM(STb[:, 8192:8192 + NH0 * 128], NH0, 128)
    vtile = STb[:, 16384:16896]
    gtmp = STb[:, 16896:17408]
    x1colb = STb[:, 8192:16384]

    ROWS = B.ROWS
    r_mean, r_msq, r_t1, r_rstd = (ROWS[0:1, i * 512:(i + 1) * 512] for i in range(4))
    r_u = ROWS[0:1, 0:2048]
    r_zb = ROWS[0:1, 2048:4096]

    gP0 = S.on("g", lambda e: e.iota(B.posf, pattern=[[128, 16]], base=0,
                                     channel_multiplier=1,
                                     allow_small_or_imprecise_dtypes=True), inc="gg")
    gP1 = S.on("g", lambda e: e.iota(B.drow, pattern=[[1, 2048]], base=0,
                                     channel_multiplier=0,
                                     allow_small_or_imprecise_dtypes=True), inc="gg")
    d0 = S.dma("g", B.poso, ext["pos2"])
    S.dma("g", B.bq_col, ext["bq_col"])
    S.dma("g", B.bo_col, ext["bo_col"])
    S.dma("g", B.sg_col, ext["sg_col"])
    S.dma("g", B.su_col, ext["su_col"])
    S.dma("g", B.sd_col, ext["sd_col"])
    S.dma("g", B.bk_row, ext["bk_row"])
    dsm = S.dma("g", B.bv_row, ext["bv_row"])
    s0 = S.on("s", lambda e: e.activation(B.offb, B.poso, AF.Copy, scale=-1.0 / NFULL),
              waits=W(("dg", d0)), inc="ss")
    tof = S.on("t", lambda e: e.matmul(PS[:, 0:2], B.ones_k1f, B.offb,
                                       start=True, stop=True),
               waits=W(("ss", s0)), inc="tt")
    vof = S.on("v", lambda e: e.tensor_copy(B.gcol[:, 0:2], PS[:, 0:2]),
               waits=W(("tt", tof)), inc="vv")
    S.on("s", lambda e: e.activation(B.decay_col[:, 0:16], B.posf, AF.Exp,
                                     scale=-1.0 / NFULL, bias=B.gcol[:, 0:1]),
         waits=W(("gg", gP0), ("vv", vof)), inc="ss")
    sdc = S.on("s", lambda e: e.activation(B.decay_col[:, 16:32], B.posf, AF.Exp,
                                           scale=-1.0 / NFULL, bias=B.gcol[:, 1:2]),
               inc="ss")
    sdr = S.on("s", lambda e: e.activation(B.drow, B.drow, AF.Exp,
                                           scale=-1.0 / NFULL, bias=B.offb[0:1, 0:1]),
               waits=W(("gg", gP1), ("ss", s0)), inc="ss")

    for half in (0, 1):
        cb = half * 2048
        xdst = xn0 if half == 0 else xnO
        for s4 in range(4):
            dx = S.dma("g", xslab.all3(),
                       slab_ap(ext["x0T"], cb + s4 * 512, cb + s4 * 512 + 512),
                       waits=W(("tt", S.c("tt")), ("vv", S.c("vv")), ("ss", S.c("ss"))))
            tcs = 0
            tsq = 0
            for c in range(16):
                tcs = S.on("t", lambda e, c=c: e.matmul(PR(0), B.ones_m1, xslab.ch(c),
                                                        start=(c == 0), stop=(c == 15)),
                           waits=W(("dg", dx), ("ss", S.c("ss")), ("vv", S.c("vv"))),
                           inc="tt")
            for c in range(16):
                sqv = S.on("s", lambda e, c=c: e.activation(B.sqt[c % 2], xslab.ch(c),
                                                            AF.Square),
                           waits=W(("dg", dx), ("tt", tsq)), inc="ss")
                tsq = S.on("t", lambda e, c=c: e.matmul(PR(1), B.ones_m1, B.sqt[c % 2],
                                                        start=(c == 0), stop=(c == 15)),
                           waits=W(("ss", sqv)), inc="tt")
            sm1 = S.on("s", lambda e: e.activation(r_mean, PR(0), AF.Copy, scale=1.0 / D),
                       waits=W(("tt", tcs)), inc="ss")
            sm2 = S.on("s", lambda e: e.activation(r_msq, PR(1), AF.Copy, scale=1.0 / D),
                       waits=W(("tt", tsq)), inc="ss")
            v1 = S.on("v", lambda e: e.tensor_mul(r_t1, r_mean, r_mean),
                      waits=W(("ss", sm2)), inc="vv")
            v2 = S.on("v", lambda e: e.tensor_sub(r_t1, r_msq, r_t1), inc="vv")
            sv1 = S.on("s", lambda e: e.activation(r_t1, r_t1, AF.Ln, bias=EPS_LN),
                       waits=W(("vv", v2)), inc="ss")
            sv2 = S.on("s", lambda e: e.activation(r_rstd, r_t1, AF.Exp, scale=-0.5),
                       inc="ss")
            tb1 = S.on("t", lambda e: e.matmul(P(2), B.ones_k1f, r_mean,
                                               start=True, stop=True),
                       waits=W(("ss", sm1), ("vv", S.c("vv"))), inc="tt")
            tb2 = S.on("t", lambda e: e.matmul(P(3), B.ones_k1f, r_rstd,
                                               start=True, stop=True),
                       waits=W(("ss", sv2)), inc="tt")
            vb1 = S.on("v", lambda e: e.tensor_copy(B.mb, P(2)), waits=W(("tt", tb1)),
                       inc="vv")
            vb2 = S.on("v", lambda e: e.tensor_copy(B.rb, P(3)), waits=W(("tt", tb2)),
                       inc="vv")
            for c in range(16):
                dst = xdst.ch(c, s4 * 512, s4 * 512 + 512)
                S.on("v", lambda e, c=c: e.tensor_sub(B.tmpf[:, 0:512], xslab.ch(c), B.mb),
                     waits=W(("vv", vb2)), inc="vv")
                S.on("v", lambda e, dst=dst: e.tensor_mul(dst, B.tmpf[:, 0:512], B.rb),
                     inc="vv")
    vxn_done = S.c("vv")

    kv_first = {4: True, 5: True}
    ks_first = {6: True, 7: True}
    for half in (0, 1):
        xsrc = xn0 if half == 0 else xnO
        for sc in range(2):
            dwk = S.dma("g", wbuf.all3(), slab_ap(ext["wkT_oh"], sc * 512, sc * 512 + 512),
                        waits=W(("tt", S.c("tt"))))
            ek = [0] * 16
            for i in range(16):
                bk_ = i % 2
                tk = 0
                for c in range(16):
                    tk = S.on("t", lambda e, c=c, i=i, bk_=bk_, xsrc=xsrc:
                              e.matmul(P(bk_), xsrc.ch(c, i * 128, i * 128 + 128),
                                       wbuf.ch(c), start=(c == 0), stop=False),
                              waits=W(("dg", dwk), ("vv", vxn_done),
                                      ("ss", ek[i - 2] if i >= 2 else 0)), inc="tt")
                tk = S.on("t", lambda e, sc=sc, bk_=bk_:
                          e.matmul(P(bk_), B.ones_k1, B.bk_row[0:1, sc * 512:sc * 512 + 512],
                                   start=False, stop=True),
                          waits=W(("dg", dsm)), inc="tt")
                dci = half * 16 + i
                ek[i] = S.on("s", lambda e, i=i, dci=dci, bk_=bk_:
                             e.activation(ktile16.ch(i), P(bk_), AF.Relu,
                                          scale=B.decay_col[:, dci:dci + 1]),
                             waits=W(("tt", tk)), inc="ss")
            dwv = S.dma("g", wbuf.all3(), slab_ap(ext["wvT_oh"], sc * 512, sc * 512 + 512),
                        waits=W(("tt", S.c("tt"))))
            evp = 0
            for i in range(16):
                bv_ = 2 + i % 2
                tv = 0
                for c in range(16):
                    tv = S.on("t", lambda e, c=c, i=i, bv_=bv_, xsrc=xsrc:
                              e.matmul(P(bv_), xsrc.ch(c, i * 128, i * 128 + 128),
                                       wbuf.ch(c), start=(c == 0), stop=False),
                              waits=W(("dg", dwv), ("vv", S.c("vv")),
                                      ("ss", evp if i >= 1 else 0)), inc="tt")
                tv = S.on("t", lambda e, sc=sc, bv_=bv_:
                          e.matmul(P(bv_), B.ones_k1, B.bv_row[0:1, sc * 512:sc * 512 + 512],
                                   start=False, stop=True), inc="tt")
                ev = S.on("s", lambda e, bv_=bv_: e.activation(vtile, P(bv_), AF.Copy),
                          waits=W(("tt", tv)), inc="ss")
                evp = ev
                for hh in range(4):
                    o = hh * 128
                    ft = kv_first[4 + sc]
                    kv_first[4 + sc] = False
                    S.on("t", lambda e, o=o, sc=sc, i=i, ft=ft:
                         e.matmul(PS[:, (4 + sc) * 512 + o:(4 + sc) * 512 + o + 128],
                                  ktile16.ap[:, i * 512 + o:i * 512 + o + 128],
                                  vtile[:, o:o + 128],
                                  start=ft, stop=False, skip_group_check=True),
                         waits=W(("ss", ev)), inc="tt")
                ft = ks_first[6 + sc]
                ks_first[6 + sc] = False
                S.on("t", lambda e, i=i, sc=sc, ft=ft:
                     e.matmul(PS[0:1, (6 + sc) * 512:(6 + sc) * 512 + 512],
                              B.ones_m1, ktile16.ch(i),
                              start=ft, stop=False, skip_group_check=True),
                     inc="tt")
    tkv_done = S.c("tt")

    vk1 = S.on("v", lambda e: e.tensor_copy(B.tmpf[:, 0:512], P(4)),
               waits=W(("tt", tkv_done)), inc="vv")
    dk1 = S.dma("q", ext["kvc"][0:512].rearrange("(h d) e -> d h e", d=128),
                B.tmpf[:, 0:512].rearrange("p (h e) -> p h e", e=128),
                waits=W(("vv", vk1)))
    vk2 = S.on("v", lambda e: e.tensor_copy(B.tmpf[:, 0:512], P(5)),
               waits=W(("dq", dk1)), inc="vv")
    dk2 = S.dma("q", ext["kvc"][512:1024].rearrange("(h d) e -> d h e", d=128),
                B.tmpf[:, 0:512].rearrange("p (h e) -> p h e", e=128),
                waits=W(("vv", vk2)))
    vk3 = S.on("v", lambda e: e.tensor_copy(r_u[0:1, 0:1024], PS[0:1, 6 * 512:6 * 512 + 1024]),
               inc="vv")
    dk3 = S.dma("q", ext["kvc"][1024:1032].rearrange("h e -> (h e)")[None, :],
                r_u[0:1, 0:1024], waits=W(("vv", vk3)))
    c1 = S.on("g", lambda e: e.collective_compute(
        "AllGather", ALU.bypass, replica_groups=GROUPS,
        ins=[ext["kvc"][:]], outs=[ext["kvg"][:]]),
        waits=W(("dq", max(dk1, dk2, dk3))), inc="c1")

    csxn = B.ksum_sb
    csgu = B.bk_row
    tcx = 0
    vcx = S.c("vv")
    for q4 in range(4):
        for c in range(16):
            tcx = S.on("t", lambda e, c=c, q4=q4: e.matmul(
                PR(0), B.ones_m1, xn0.ch(c, q4 * 512, q4 * 512 + 512),
                start=(c == 0), stop=(c == 15)),
                waits=W(("vv", max(vxn_done, vcx)), ("ss", S.c("ss"))), inc="tt")
        vcx = S.on("v", lambda e, q4=q4: e.tensor_copy(csxn[0:1, q4 * 512:q4 * 512 + 512],
                                                       PR(0)),
                   waits=W(("tt", tcx)), inc="vv")
    csxn_v = S.c("vv")
    gu_sb = STb[:, 16384:16896]
    gtmp = STb[:, 16896:17408]
    evg = {b: S.c("ss") for b in range(4)}
    evu = {b: S.c("vv") for b in range(8)}
    dgu_p = 0
    wrd = {0: 0, 1: 0}
    for ht in range(NH0):
        drain(8)
        bsel = ht % 2
        dgc = S.dma("g", wg_c[bsel].all3(), colload_ap(ext["wqgT"], ht),
                    waits=W(("tt", wrd[bsel])))
        duc = S.dma("g", wu_c[bsel].all3(), colload_ap(ext["wquT"], ht))
        tg = {}
        for q4 in range(4):
            t_ = 0
            for c in range(16):
                t_ = S.on("t", lambda e, c=c, q4=q4, bsel=bsel: e.matmul(
                    P(q4), wg_c[bsel].ch(c), xn0.ch(c, q4 * 512, q4 * 512 + 512),
                    start=(c == 0), stop=False),
                    waits=W(("dg", dgc), ("ss", evg[q4]), ("vv", csxn_v)), inc="tt")
            tg[q4] = S.on("t", lambda e, q4=q4: e.matmul(
                P(q4), B.n128_k1, csxn[0:1, q4 * 512:q4 * 512 + 512],
                start=False, stop=True), inc="tt")
        tu = {}
        for q4 in range(4):
            t_ = 0
            for c in range(16):
                t_ = S.on("t", lambda e, c=c, q4=q4, bsel=bsel: e.matmul(
                    P(4 + q4), wu_c[bsel].ch(c), xn0.ch(c, q4 * 512, q4 * 512 + 512),
                    start=(c == 0), stop=False),
                    waits=W(("dg", duc), ("vv", evu[4 + q4])), inc="tt")
            tu[q4] = S.on("t", lambda e, q4=q4: e.matmul(
                P(4 + q4), B.n128_k1, csxn[0:1, q4 * 512:q4 * 512 + 512],
                start=False, stop=True), inc="tt")
        wrd[bsel] = tu[3]
        vg = 0
        for q4 in range(4):
            sg = S.on("s", lambda e, ht=ht, q4=q4: e.activation(
                gtmp, P(q4), AF.Silu, scale=B.sg_col[:, ht:ht + 1]),
                waits=W(("tt", tg[q4]), ("vv", vg if q4 else S.c("vv"))), inc="ss")
            evg[q4] = sg
            vu = S.on("v", lambda e, ht=ht, q4=q4: e.tensor_scalar(
                out=B.tmpf[:, 0:512], in0=P(4 + q4), scalar1=B.su_col[:, ht:ht + 1],
                scalar2=None, op0=ALU.mult), waits=W(("tt", tu[q4])), inc="vv")
            evu[4 + q4] = vu
            vg = S.on("v", lambda e: e.tensor_mul(gu_sb, gtmp, B.tmpf[:, 0:512]),
                      waits=W(("ss", sg), ("dq", dgu_p)), inc="vv")
            dgu_p = S.dma("q", ext["gus"][ht * 128:(ht + 1) * 128, q4 * 512:q4 * 512 + 512],
                          gu_sb, waits=W(("vv", vg)))
    wd_c2 = [FM(STb[:, 0:NH0 * 128], NH0, 128),
             FM(STb[:, 8192:8192 + NH0 * 128], NH0, 128)]
    evd = {b: S.c("ss") for b in (2, 3)}
    drd = {0: S.c("tt"), 1: S.c("tt")}
    for ns in range(4):
        n0 = ns * 512
        dgs = S.dma("g", guA.all3(), slab_ap(ext["gus"], n0, n0 + 512),
                    waits=W(("tt", S.c("tt")), ("dq", S.c("dq"))))
        tcg = 0
        for hc in range(NH0):
            tcg = S.on("t", lambda e, hc=hc: e.matmul(
                PR(6), B.ones_m1, guA.ch(hc), start=(hc == 0), stop=(hc == NH0 - 1)),
                waits=W(("dg", dgs), ("vv", S.c("vv")), ("ss", S.c("ss"))), inc="tt")
        vcg = S.on("v", lambda e: e.tensor_copy(csgu[0:1, 0:512], PR(6)),
                   waits=W(("tt", tcg)), inc="vv")
        for dt in range(16):
            drain(4)
            bsel = dt % 2
            ddc = S.dma("g", wd_c2[bsel].all3(), colload_ap(ext["wqdT"], dt),
                        waits=W(("tt", drd.get(bsel, 0))))
            pb = 2 + dt % 2
            td = 0
            for hc in range(NH0):
                td = S.on("t", lambda e, hc=hc, pb=pb, bsel=bsel: e.matmul(
                    P(pb), wd_c2[bsel].ch(hc), guA.ch(hc),
                    start=(hc == 0), stop=False),
                    waits=W(("dg", ddc), ("ss", evd[pb]), ("vv", vcg)), inc="tt")
            td = S.on("t", lambda e, pb=pb: e.matmul(
                P(pb), B.n128_k1, csgu[0:1, 0:512], start=False, stop=True), inc="tt")
            drd[bsel] = td
            sf = S.on("s", lambda e, dt=dt, pb=pb: e.activation(
                B.tD, P(pb), AF.Copy, scale=B.sd_col[:, dt:dt + 1]),
                waits=W(("tt", td)), inc="ss")
            evd[pb] = sf
            S.dma("q", ext["f0s"][dt * 128:(dt + 1) * 128, n0:n0 + 512], B.tD,
                  waits=W(("ss", sf)))

    qev = {0: S.c("ss"), 1: S.c("ss")}
    qrd = {0: S.c("tt"), 1: S.c("tt")}
    vvP4 = S.c("vv")
    dqP4 = S.c("dq")
    for t16 in range(16):
        bsel = t16 % 2
        dqc = S.dma("g", wo_c[bsel].all3(), colload_ap(ext["wqT"], t16),
                    waits=W(("tt", qrd[bsel])))
        for s4 in range(4):
            pb = s4 % 2
            tq = 0
            for c in range(16):
                tq = S.on("t", lambda e, c=c, s4=s4, bsel=bsel, pb=pb:
                          e.matmul(P(pb), wo_c[bsel].ch(c),
                                   xn0.ch(c, s4 * 512, s4 * 512 + 512),
                                   start=(c == 0), stop=(c == 15)),
                          waits=W(("dg", dqc), ("ss", qev[pb])), inc="tt")
            qev[pb] = S.on("s", lambda e, t16=t16, s4=s4, pb=pb: e.activation(
                qA.ch(t16, s4 * 512, s4 * 512 + 512), P(pb), AF.Relu,
                bias=B.bq_col[:, t16:t16 + 1]),
                waits=W(("tt", tq), ("vv", vvP4), ("dq", dqP4)), inc="ss")
            qrd[bsel] = tq
    q_done = S.c("ss")

    S.on("g", lambda e: e.nop(), waits=W(("c1", 1)))
    dk16 = 0
    for H in range(16):
        r, hp = H // 8, H % 8
        dk16 = S.dma("g", B.kv16[:, H * 128:(H + 1) * 128],
                     ext["kvg"][r * 1032 + hp * 128: r * 1032 + (hp + 1) * 128, :],
                     waits=W(("c1", 1)))
    dks = 0
    for r in range(2):
        dks = S.dma("g", B.ksum_sb[0:1, r * 1024:(r + 1) * 1024],
                    ext["kvg"][r * 1032 + 1024: r * 1032 + 1032, :].rearrange("h e -> (h e)")[None, :],
                    waits=W(("c1", 1), ("tt", S.c("tt"))))
    kc_t = 0
    for H in range(16):
        kc_t = S.on("t", lambda e, H=H: e.matmul(
            PS[:, 6 * 512 + H:6 * 512 + H + 1],
            B.ksum_sb[0:1, H * 128:(H + 1) * 128], B.ones_k1[0:1, 0:1],
            start=(H == 0), stop=(H == 15), skip_group_check=True),
            waits=W(("dg", dks), ("ss", q_done)), inc="tt")
    vkc = S.on("v", lambda e: e.tensor_copy(B.sqt[0][:, 0:16], PS[:, 6 * 512:6 * 512 + 16]),
               waits=W(("tt", kc_t)), inc="vv")
    kcols = B.sqt[0][:, 0:16]

    rd = {b: S.c("vv") for b in range(8)}
    for H in range(16):
        for s4 in range(4):
            ub = 6 + (s4 % 2)
            tu4 = S.on("t", lambda e, H=H, s4=s4, ub=ub: e.matmul(
                PR(ub, 512), kcols[:, H:H + 1],
                qA.ch(H, s4 * 512, s4 * 512 + 512),
                start=True, stop=True), waits=W(("vv", max(vkc, rd[ub]))), inc="tt")
            rd[ub] = S.on("v", lambda e, s4=s4, ub=ub: e.tensor_copy(
                r_u[0:1, s4 * 512:s4 * 512 + 512], PR(ub, 512)),
                waits=W(("tt", tu4)), inc="vv")
        v5 = S.on("v", lambda e: e.tensor_mul(r_zb, r_u, B.drow),
                  waits=W(("ss", sdr), ("vv", S.c("vv"))), inc="vv")
        s5 = S.on("s", lambda e: e.activation(r_zb, r_zb, AF.Copy, bias=EPS_Z),
                  waits=W(("vv", v5)), inc="ss")
        v6a = S.on("v", lambda e: e.reciprocal(r_zb, r_zb),
                   waits=W(("ss", s5)), inc="vv")
        v6 = S.on("v", lambda e: e.tensor_mul(r_u, r_zb, B.drow), inc="vv")
        for s4 in range(4):
            pb = s4 % 2
            to = S.on("t", lambda e, H=H, s4=s4, pb=pb: e.matmul(
                P(pb), B.kv16[:, H * 128:(H + 1) * 128],
                qA.ch(H, s4 * 512, s4 * 512 + 512), start=True, stop=True),
                waits=W(("vv", max(v6, rd[pb])), ("dg", dk16)), inc="tt")
            tz = S.on("t", lambda e, s4=s4, pb=pb: e.matmul(
                P(2 + pb), B.ones_k1f, r_u[0:1, s4 * 512:s4 * 512 + 512],
                start=True, stop=True), waits=W(("vv", rd[2 + pb])), inc="tt")
            vz = S.on("v", lambda e, pb=pb: e.tensor_copy(B.mb, P(2 + pb)),
                      waits=W(("tt", tz)), inc="vv")
            rd[2 + pb] = vz
            rd[pb] = S.on("v", lambda e, H=H, s4=s4, pb=pb: e.tensor_mul(
                atA.ch(H, s4 * 512, s4 * 512 + 512), P(pb), B.mb),
                waits=W(("tt", to), ("vv", vz)), inc="vv")
    at_done = S.c("vv")

    for _e in ("q", "s", "v"):
        S.on(_e, lambda e: e.nop(), waits=W(("tt", S.c("tt"))))
    swv = {0: 0, 1: 0}
    ord_ = {0: at_done, 1: at_done}
    vb_p = 0
    dx1_p = 0
    dxn_p = 0
    tsq_p = S.c("tt")
    for s4 in range(4):
        x1v = [0] * 16
        for t16 in range(16):
            drain(8)
            bsel = t16 % 2
            dwo = S.dma("g", wo_c[bsel].all3(), colload_ap(ext["woT"], t16),
                        waits=W(("tt", ord_[bsel])))
            pb = t16 % 2
            tw = 0
            for c in range(16):
                tw = S.on("t", lambda e, c=c, s4=s4, bsel=bsel, pb=pb:
                          e.matmul(P(pb), wo_c[bsel].ch(c),
                                   atA.ch(c, s4 * 512, s4 * 512 + 512),
                                   start=(c == 0), stop=(c == 15)),
                          waits=W(("dg", dwo), ("vv", max(at_done, swv[pb]))),
                          inc="tt")
            ord_[bsel] = tw
            sw = S.on("v", lambda e, t16=t16, pb=pb: e.tensor_scalar(
                out=B.tA, in0=P(pb), scalar1=B.bo_col[:, t16:t16 + 1],
                scalar2=None, op0=ALU.add),
                waits=W(("tt", tw), ("vv", vb_p), ("dq", dx1_p)), inc="vv")
            swv[pb] = sw
            df0 = S.dma("g", B.tD, ext["f0s"][t16 * 128:(t16 + 1) * 128,
                                              s4 * 512:s4 * 512 + 512],
                        waits=W(("vv", S.c("vv")), ("dq", S.c("dq"))))
            dx0 = S.dma("q", B.tB, ext["x0T"][t16 * 128:(t16 + 1) * 128,
                                              s4 * 512:s4 * 512 + 512],
                        waits=W(("vv", max(vb_p, at_done))))
            va = S.on("v", lambda e: e.tensor_add(B.tC, B.tA, B.tD),
                      waits=W(("dg", df0)), inc="vv")
            vb = S.on("v", lambda e: e.tensor_add(B.tA, B.tC, B.tB),
                      waits=W(("dq", dx0)), inc="vv")
            vb_p = vb
            dx1 = S.dma("q", ext["x1T"][t16 * 128:(t16 + 1) * 128, s4 * 512:s4 * 512 + 512],
                        B.tA, waits=W(("vv", vb)))
            dx1_p = dx1
            x1v[t16] = S.on("s", lambda e, t16=t16: e.activation(
                x1colb[:, t16 * 512:(t16 + 1) * 512], B.tA, AF.Copy),
                waits=W(("vv", vb), ("tt", tsq_p)), inc="ss")
        tcs = tsq = 0
        for t16 in range(16):
            tcs = S.on("t", lambda e, t16=t16: e.matmul(
                PR(2), B.ones_m1, x1colb[:, t16 * 512:(t16 + 1) * 512],
                start=(t16 == 0), stop=(t16 == 15)),
                waits=W(("ss", x1v[t16])), inc="tt")
        for t16 in range(16):
            sq2 = S.on("s", lambda e, t16=t16: e.activation(
                B.sqt[t16 % 2], x1colb[:, t16 * 512:(t16 + 1) * 512], AF.Square),
                waits=W(("tt", tsq)), inc="ss")
            tsq = S.on("t", lambda e, t16=t16: e.matmul(
                PR(3), B.ones_m1, B.sqt[t16 % 2], start=(t16 == 0), stop=(t16 == 15)),
                waits=W(("ss", sq2)), inc="tt")
        tsq_p = tsq
        sm1 = S.on("s", lambda e: e.activation(r_mean, PR(2), AF.Copy, scale=1.0 / D),
                   waits=W(("tt", tcs)), inc="ss")
        sm2 = S.on("s", lambda e: e.activation(r_msq, PR(3), AF.Copy, scale=1.0 / D),
                   waits=W(("tt", tsq)), inc="ss")
        v1 = S.on("v", lambda e: e.tensor_mul(r_t1, r_mean, r_mean),
                  waits=W(("ss", sm2)), inc="vv")
        v2 = S.on("v", lambda e: e.tensor_sub(r_t1, r_msq, r_t1), inc="vv")
        S.on("s", lambda e: e.activation(r_t1, r_t1, AF.Ln, bias=EPS_LN),
             waits=W(("vv", v2)), inc="ss")
        sv2 = S.on("s", lambda e: e.activation(r_rstd, r_t1, AF.Exp, scale=-0.5), inc="ss")
        tb1 = S.on("t", lambda e: e.matmul(P(4), B.ones_k1f, r_mean, start=True, stop=True),
                   waits=W(("ss", sv2)), inc="tt")
        tb2 = S.on("t", lambda e: e.matmul(P(5), B.ones_k1f, r_rstd, start=True, stop=True),
                   inc="tt")
        vb1 = S.on("v", lambda e: e.tensor_copy(B.mb, P(4)), waits=W(("tt", tb1)), inc="vv")
        vb2 = S.on("v", lambda e: e.tensor_copy(B.rb, P(5)), waits=W(("tt", tb2)), inc="vv")
        for t16 in range(16):
            S.on("v", lambda e, t16=t16: e.tensor_sub(
                B.tmpf[:, 0:512], x1colb[:, t16 * 512:(t16 + 1) * 512], B.mb),
                waits=W(("vv", vb2)), inc="vv")
            vx = S.on("v", lambda e: e.tensor_mul(B.tE, B.tmpf[:, 0:512], B.rb),
                      waits=W(("dq", dxn_p)), inc="vv")
            dxn_p = S.dma("q", ext["xn1T"][t16 * 128:(t16 + 1) * 128,
                                           s4 * 512:s4 * 512 + 512],
                          B.tE, waits=W(("vv", vx)))
    S.on("q", lambda e: e.nop(), waits=W(("dq", S.c("dq")), ("dg", S.c("dg"))))



def build_l1(S, B, ext, PS):
    nc = S.nc
    P = lambda b, n=1: PS[:, b * 512:(b + n) * 512]
    PR = lambda b, w=512: PS[0:1, b * 512:b * 512 + w]
    W = lambda *a: list(a)

    AR1b, AR1f = bf(B.AR1), f32(B.AR1)
    AR3b = bf(B.AR3)
    STb = bf(B.ST)

    wstage = AR1f[:, 0:8192]
    qtile = AR1f[:, 8192:16384]
    swt = AR3b[:, 0:8192]
    m01t = AR3b[:, 8192:16384]
    whatt = AR3b[:, 16384:24576]
    xg = FM(AR1b, NC, 2048)
    Qh = AR3b[:, 0:4096]
    Kh = AR3b[:, 4096:8192]
    Vth = AR3b[:, 8192:12288]
    Vtok = AR3b[:, 12288:16384]
    q2t = AR3b[:, 16384:20480]
    Psl = [AR3b[:, 20480 + i * 512:20480 + (i + 1) * 512] for i in range(4)]
    wqc = FM(STb[:, 0:2048], NC, 128)
    wkc = FM(STb[:, 2048:4096], NC, 128)
    wvc = FM(STb[:, 4096:6144], NC, 128)
    PB7 = PS[:, 7 * 512:8 * 512].bitcast(BF16)
    harena = FM(AR1b, NH1, 512)
    xnf = FM(AR3b[:, 0:8192], NC, 512)
    attE = FM(AR3b[:, 8192:16384], NC, 512)
    wpjh = [FM(AR3b[:, 16384 + j * 4096:16384 + (j + 1) * 4096], 32, 128) for j in (0, 1)]
    xslab4 = FM(STb[:, 0:8192], NC, 512)
    wfc_col = [FM(STb[:, 8192 + i * 2048:8192 + (i + 1) * 2048], NC, 128) for i in (0, 1)]
    wo_c4 = [FM(STb[:, 12288 + i * 2048:12288 + (i + 1) * 2048], NC, 128) for i in (0, 1)]
    idxE = B.mstrip.bitcast(I32)[:, 0:64]

    ROWS = B.ROWS
    r_mean, r_msq, r_t1, r_rstd = (ROWS[0:1, i * 512:(i + 1) * 512] for i in range(4))
    r_sum = ROWS[0:1, 2048:2560]

    b1qc = B.decay_col[:, 0:8]

    S.fence()
    dc0 = S.dma("q", ext["xn1c"][:], ext["xn1T"][:], waits=W(("dq", S.c("dq"))))
    c2 = S.on("g", lambda e: e.collective_compute(
        "AllGather", ALU.bypass, replica_groups=GROUPS,
        ins=[ext["xn1c"][:]], outs=[ext["xn1g"][:]]),
        waits=W(("dq", dc0)), inc="c2")

    db1 = S.dma("g", B.b1q, ext["b1q"])
    S.dma("g", B.b1k, ext["b1k"])
    S.dma("g", B.b1v, ext["b1v"])
    dbo = S.dma("g", B.bo1_col, ext["bo1_col"])
    sb1 = S.on("s", lambda e: e.activation(b1qc, B.b1q, AF.Copy, scale=CQK),
               waits=W(("dg", db1)), inc="ss")

    if ext.get("WPREP_DONE"):
        return build_l1_rest(S, B, ext, PS)
    S.on("v", lambda e: e.memset(B.acc, 0.0), inc="vv")
    gam_cols = {}
    for m, (wT, nch, F) in enumerate((("wfcT", 16, 8192), ("wpjT", 64, 2048))):
        vlast = 0
        for ch in range(nch):
            half = ch % 2
            dw = S.dma("q", wstage[:, 0:F],
                       ext[wT][ch * 128:(ch + 1) * 128, :],
                       waits=W(("ss", S.c("ss")), ("vv", S.c("vv"))))
            sa = S.on("s", lambda e, F=F: e.activation(swt[:, 0:F], wstage[:, 0:F], AF.Abs,
                                                       accum_out=B.gcol[:, 3:4]),
                      waits=W(("dq", dw)), inc="ss")
            vlast = S.on("v", lambda e, m=m: e.tensor_add(B.acc[:, m:m + 1], B.acc[:, m:m + 1],
                                                          B.gcol[:, 3:4]),
                         waits=W(("ss", sa)), inc="vv")
        tg1 = S.on("t", lambda e, m=m: e.matmul(PS[0:1, 6 * 512:6 * 512 + 1],
                                                B.acc[:, m:m + 1], B.ones_m1f,
                                                start=True, stop=True),
                   waits=W(("vv", vlast)), inc="tt")
        sg1 = S.on("s", lambda e, m=m: e.activation(B.r11[0:1, m:m + 1],
                                                    PS[0:1, 6 * 512:6 * 512 + 1],
                                                    AF.Copy, scale=1.0 / N_W_FC),
                   waits=W(("tt", tg1)), inc="ss")
        sg2 = S.on("s", lambda e, m=m: e.activation(B.r11[0:1, 2 + m:3 + m],
                                                    B.r11[0:1, m:m + 1], AF.Copy,
                                                    bias=EPS_B),
                   inc="ss")
        vg1 = S.on("v", lambda e, m=m: e.reciprocal(B.r11[0:1, 4 + m:5 + m],
                                                    B.r11[0:1, 2 + m:3 + m]),
                   waits=W(("ss", sg2)), inc="vv")
        tb = S.on("t", lambda e, m=m: e.matmul(PS[:, 6 * 512:6 * 512 + 1], B.ones_k1f,
                                               B.r11[0:1, 4 + m:5 + m], start=True, stop=True),
                  waits=W(("vv", vg1)), inc="tt")
        vb = S.on("v", lambda e, m=m: e.tensor_copy(B.gcol[:, 2 * m:2 * m + 1],
                                                    PS[:, 6 * 512:6 * 512 + 1]),
                  waits=W(("tt", tb)), inc="vv")
        tb2 = S.on("t", lambda e, m=m: e.matmul(PS[:, 6 * 512:6 * 512 + 1], B.ones_k1f,
                                                B.r11[0:1, m:m + 1], start=True, stop=True),
                   waits=W(("vv", vb)), inc="tt")
        vb2 = S.on("v", lambda e, m=m: e.tensor_copy(B.gcol[:, 2 * m + 1:2 * m + 2],
                                                     PS[:, 6 * 512:6 * 512 + 1]),
                   waits=W(("tt", tb2)), inc="vv")
        gam_cols[m] = vb2

    wst_p = 0
    for m, (wT, outT, nch, F) in enumerate((("wfcT", "whatfc", 16, 8192),
                                            ("wpjT", "whatpj", 64, 2048))):
        for ch in range(nch):
            dw = S.dma("q", wstage[:, 0:F], ext[wT][ch * 128:(ch + 1) * 128, :],
                       waits=W(("ss", S.c("ss")), ("vv", S.c("vv"))))
            sq_ = S.on("s", lambda e, m=m, F=F: e.activation(
                qtile[:, 0:F], wstage[:, 0:F], AF.Square, scale=B.gcol[:, 2 * m:2 * m + 1]),
                waits=W(("dq", dw), ("vv", gam_cols[1]), ("vv2" if False else "vv", S.c("vv"))),
                inc="ss")
            vm = S.on("v", lambda e, F=F: e.tensor_scalar(
                out=m01t[:, 0:F], in0=qtile[:, 0:F], scalar1=0.25, scalar2=None,
                op0=ALU.is_ge), waits=W(("ss", sq_)), inc="vv")
            ss_ = S.on("s", lambda e, F=F: e.activation(swt[:, 0:F], wstage[:, 0:F], AF.Sign),
                       inc="ss")
            vw = S.on("v", lambda e, m=m, F=F: e.scalar_tensor_tensor(
                out=whatt[:, 0:F], in0=swt[:, 0:F], scalar=B.gcol[:, 2 * m + 1:2 * m + 2],
                in1=m01t[:, 0:F], op0=ALU.mult, op1=ALU.mult),
                waits=W(("ss", ss_), ("vv", vm), ("dq", wst_p)), inc="vv")
            wst_p = S.dma("q", ext[outT][ch * 128:(ch + 1) * 128, :], whatt[:, 0:F],
                          waits=W(("vv", vw)))

    return build_l1_rest(S, B, ext, PS)


def build_l1_rest(S, B, ext, PS):
    nc = S.nc
    P = lambda b, n=1: PS[:, b * 512:(b + n) * 512]
    PR = lambda b, w=512: PS[0:1, b * 512:b * 512 + w]
    W = lambda *a: list(a)
    AR1b = bf(B.AR1)
    AR3b = bf(B.AR3)
    STb = bf(B.ST)
    xg = FM(AR1b, NC, 2048)
    Qh = AR3b[:, 0:4096]
    Kh = AR3b[:, 4096:8192]
    Vth = AR3b[:, 8192:12288]
    Vtok = AR3b[:, 12288:16384]
    q2t = AR3b[:, 16384:20480]
    Psl = [AR3b[:, 20480 + i * 512:20480 + (i + 1) * 512] for i in range(4)]
    wqc = FM(STb[:, 0:2048], NC, 128)
    wkc = FM(STb[:, 2048:4096], NC, 128)
    wvc = FM(STb[:, 4096:6144], NC, 128)
    PB7 = PS[:, 7 * 512:8 * 512].bitcast(BF16)
    harena = FM(AR1b, NH1, 512)
    xnf = FM(AR3b[:, 0:8192], NC, 512)
    attE = FM(AR3b[:, 8192:16384], NC, 512)
    wpjh = [FM(AR3b[:, 16384 + j * 4096:16384 + (j + 1) * 4096], 32, 128) for j in (0, 1)]
    xslab4 = FM(STb[:, 0:8192], NC, 512)
    wfc_col = [FM(STb[:, 8192 + i * 2048:8192 + (i + 1) * 2048], NC, 128) for i in (0, 1)]
    wo_c4 = [FM(STb[:, 12288 + i * 2048:12288 + (i + 1) * 2048], NC, 128) for i in (0, 1)]
    idxE = B.mstrip.bitcast(I32)[:, 0:64]
    ROWS = B.ROWS
    r_mean, r_msq, r_t1, r_rstd = (ROWS[0:1, i * 512:(i + 1) * 512] for i in range(4))
    r_sum = ROWS[0:1, 2048:2560]
    b1qc = B.decay_col[:, 0:8]

    sb1 = S.c("ss")
    dbo = S.c("dg")
    S.on("g", lambda e: e.nop(), waits=W(("c2", 1)))
    S.fence()
    head_tt = S.c("tt")
    head_ss = S.c("ss")
    head_vv = S.c("vv")
    for hp in range(8):
        dq_ = S.dma("g", wqc.all3(), colload_ap(ext["wq1T"], hp), waits=W(("tt", head_tt)))
        dk_ = S.dma("g", wkc.all3(), colload_ap(ext["wk1T"], hp))
        dv_ = S.dma("g", wvc.all3(), colload_ap(ext["wv1T"], hp))
        rdq = {0: head_vv, 1: head_vv, 2: head_vv}
        evq = S.c("vv")
        for nh in range(2):
            dxg = 0
            for c in range(16):
                dxg = S.dma("g", xg.ch(c),
                            ext["xn1g"][nh * 2048 + c * 128: nh * 2048 + (c + 1) * 128, :],
                            waits=W(("tt", S.c("tt"))))
            for s4 in range(4):
                col = nh * 2048 + s4 * 512
                tq = tk = tv = 0
                for c in range(16):
                    tq = S.on("t", lambda e, c=c, s4=s4: e.matmul(
                        P(0), wqc.ch(c), xg.ch(c, s4 * 512, s4 * 512 + 512),
                        start=(c == 0), stop=(c == 15)),
                        waits=W(("dg", max(dq_, dxg)), ("vv", rdq[0]),
                                ("ss", sb1)), inc="tt")
                rdq[0] = S.on("v", lambda e, hp=hp, col=col: e.tensor_scalar(
                    out=Qh[:, col:col + 512], in0=P(0), scalar1=CQK, scalar2=b1qc[:, hp:hp + 1],
                    op0=ALU.mult, op1=ALU.add), waits=W(("tt", tq)), inc="vv")
                for c in range(16):
                    tk = S.on("t", lambda e, c=c, s4=s4: e.matmul(
                        P(1), wkc.ch(c), xg.ch(c, s4 * 512, s4 * 512 + 512),
                        start=(c == 0), stop=(c == 15)),
                        waits=W(("dg", max(dk_, dxg)), ("vv", rdq[1])), inc="tt")
                rdq[1] = S.on("v", lambda e, hp=hp, col=col: e.tensor_scalar(
                    out=Kh[:, col:col + 512], in0=P(1), scalar1=B.b1k[:, hp:hp + 1],
                    scalar2=None, op0=ALU.add), waits=W(("tt", tk)), inc="vv")
                for c in range(16):
                    tv = S.on("t", lambda e, c=c, s4=s4: e.matmul(
                        P(2), wvc.ch(c), xg.ch(c, s4 * 512, s4 * 512 + 512),
                        start=(c == 0), stop=(c == 15)),
                        waits=W(("dg", max(dv_, dxg)), ("vv", rdq[2])), inc="tt")
                rdq[2] = S.on("v", lambda e, hp=hp, col=col: e.tensor_scalar(
                    out=Vth[:, col:col + 512], in0=P(2), scalar1=B.b1v[:, hp:hp + 1],
                    scalar2=None, op0=ALU.add), waits=W(("tt", tv)), inc="vv")
        proj_vv = S.c("vv")
        vtr = S.c("vv")
        for kt in range(32):
            sl = kt % 2
            ttr = S.on("t", lambda e, kt=kt, sl=sl: e.transpose(
                PB7[:, sl * 128:(sl + 1) * 128], Vth[:, kt * 128:(kt + 1) * 128], B.ident),
                waits=W(("vv", max(proj_vv, vtr))), inc="tt")
            vtr = S.on("v", lambda e, kt=kt, sl=sl: e.tensor_copy(
                Vtok[:, kt * 128:(kt + 1) * 128], PB7[:, sl * 128:(sl + 1) * 128]),
                waits=W(("tt", ttr)), inc="vv")
        sqq = S.on("s", lambda e: e.activation(q2t, Qh, AF.Square), waits=W(("vv", proj_vv)),
                   inc="ss")
        vi1 = S.on("v", lambda e: e.memset(B.r11[0:1, 0:2], 0.0), inc="vv")
        mxv = vi1
        for part, src_ in ((0, "q"), (1, "k")):
            if part == 1:
                sqq = S.on("s", lambda e: e.activation(q2t, Kh, AF.Square),
                           waits=W(("tt", S.c("tt"))), inc="ss")
            for s8 in range(8):
                tsum = S.on("t", lambda e, s8=s8: e.matmul(
                    PR(6), B.ones_m1, q2t[:, s8 * 512:(s8 + 1) * 512],
                    start=True, stop=True), waits=W(("ss", sqq), ("vv", mxv)), inc="tt")
                vr = S.on("v", lambda e: e.reduce_max(B.r11[0:1, 6:7], PR(6),
                                                      axis=mybir.AxisListType.X),
                          waits=W(("tt", tsum)), inc="vv")
                mxv = S.on("v", lambda e, part=part: e.tensor_max(
                    B.r11[0:1, part:part + 1], B.r11[0:1, part:part + 1], B.r11[0:1, 6:7]),
                    inc="vv")
        s_c1 = S.on("s", lambda e: e.activation(B.r11[0:1, 2:3], B.r11[0:1, 0:1], AF.Sqrt),
                    waits=W(("vv", mxv)), inc="ss")
        s_c2 = S.on("s", lambda e: e.activation(B.r11[0:1, 3:4], B.r11[0:1, 1:2], AF.Sqrt),
                    inc="ss")
        v_c = S.on("v", lambda e: e.tensor_mul(B.r11[0:1, 4:5], B.r11[0:1, 2:3],
                                               B.r11[0:1, 3:4]),
                   waits=W(("ss", s_c2)), inc="vv")
        t_c = S.on("t", lambda e: e.matmul(PS[:, 6 * 512:6 * 512 + 1], B.ones_k1f,
                                           B.r11[0:1, 4:5], start=True, stop=True),
                   waits=W(("vv", v_c)), inc="tt")
        s_c3 = S.on("s", lambda e: e.activation(B.negC, PS[:, 6 * 512:6 * 512 + 1],
                                                AF.Copy, scale=-1.0),
                    waits=W(("tt", t_c)), inc="ss")
        pread = {i: ("vv", S.c("vv")) for i in range(4)}
        pslot_rd = {i: S.c("tt") for i in range(4)}
        av_rd = S.c("vv")
        zb_rd = S.c("vv")
        tE_rd = S.c("vv")
        tD_dma = 0
        for s8 in range(8):
            ktmax = 4 * (s8 + 1)
            tsum = tav = 0
            for kt in range(ktmax):
                sl = kt % 4
                tS = S.on("t", lambda e, kt=kt, s8=s8, sl=sl: e.matmul(
                    P(sl), Kh[:, kt * 128:(kt + 1) * 128],
                    Qh[:, s8 * 512:(s8 + 1) * 512], start=True, stop=True),
                    waits=W(pread[sl], ("ss", s_c3)), inc="tt")
                boundary = kt >= 4 * s8
                if not boundary:
                    sE = S.on("s", lambda e, sl=sl: e.activation(
                        Psl[sl], P(sl), AF.Exp, bias=B.negC),
                        waits=W(("tt", max(tS, pslot_rd[sl]))), inc="ss")
                    pr = ("ss", sE)
                    pread[sl] = ("ss", sE)
                else:
                    off = kt * 128 - s8 * 512
                    sE = S.on("s", lambda e, sl=sl: e.activation(
                        B.tE, P(sl), AF.Exp, bias=B.negC),
                        waits=W(("tt", tS), ("vv", tE_rd)), inc="ss")
                    pread[sl] = ("ss", sE)
                    vE = S.on("v", lambda e, sl=sl, off=off: e.tensor_mul(
                        Psl[sl], B.tE, B.mstrip[:, 384 - off:384 - off + 512]),
                        waits=W(("ss", sE), ("tt", pslot_rd[sl])), inc="vv")
                    tE_rd = vE
                    pr = ("vv", vE)
                tsum = S.on("t", lambda e, sl=sl, kt=kt, ktmax=ktmax: e.matmul(
                    PR(4), B.ones_m1, Psl[sl], start=(kt == 0), stop=(kt == ktmax - 1)),
                    waits=W(pr, ("vv", zb_rd)), inc="tt")
                tav = S.on("t", lambda e, sl=sl, kt=kt, ktmax=ktmax: e.matmul(
                    P(5), Vtok[:, kt * 128:(kt + 1) * 128], Psl[sl],
                    start=(kt == 0), stop=(kt == ktmax - 1)),
                    waits=W(("vv", av_rd)), inc="tt")
                pslot_rd[sl] = tav
            vz1 = S.on("v", lambda e: e.tensor_copy(r_sum, PR(4)), waits=W(("tt", tsum)),
                       inc="vv")
            zb_rd = vz1
            vz2 = S.on("v", lambda e: e.reciprocal(r_sum, r_sum), inc="vv")
            tzb = S.on("t", lambda e: e.matmul(P(6), B.ones_k1f, r_sum, start=True, stop=True),
                       waits=W(("vv", vz2)), inc="tt")
            vz3 = S.on("v", lambda e: e.tensor_copy(B.mb, P(6)), waits=W(("tt", tzb)),
                       inc="vv")
            av_rd = S.on("v", lambda e: e.tensor_mul(B.tD, P(5), B.mb),
                         waits=W(("tt", tav), ("vv", vz3), ("dq", tD_dma)), inc="vv")
            tD_dma = S.dma("q", ext["attc"][hp * 128:(hp + 1) * 128,
                                            s8 * 512:(s8 + 1) * 512],
                           B.tD, waits=W(("vv", av_rd)))
        head_tt = S.c("tt")
        head_vv = S.c("vv")
    c3 = S.on("g", lambda e: e.collective_compute(
        "AllGather", ALU.bypass, replica_groups=GROUPS,
        ins=[ext["attc"][:]], outs=[ext["attg"][:]]),
        waits=W(("dq", S.c("dq"))), inc="c3")

    S.fence()
    dix = S.dma("g", idxE, ext["idxE"], waits=W(("ss", S.c("ss")), ("vv", S.c("vv"))))
    swv = {4: S.c("ss"), 5: S.c("ss")}
    frd = {0: S.c("tt"), 1: S.c("tt")}
    prd = {0: S.c("tt"), 1: S.c("tt")}
    ord4 = {0: S.c("tt"), 1: S.c("tt")}
    jrd = {6: S.c("vv"), 7: S.c("vv")}
    vb_p = 0
    dx2_p = 0
    tsq_p = S.c("tt")
    hrd = S.c("tt")
    for sl in range(4):
        dxs = S.dma("q", xslab4.all3(), slab_ap(ext["xn1T"], sl * 512, sl * 512 + 512),
                    waits=W(("tt", S.c("tt")), ("ss", S.c("ss")), ("vv", S.c("vv"))))
        tcs = tsq = 0
        for c in range(16):
            tcs = S.on("t", lambda e, c=c: e.matmul(PR(0), B.ones_m1, xslab4.ch(c),
                                                    start=(c == 0), stop=(c == 15)),
                       waits=W(("dq", dxs)), inc="tt")
        for c in range(16):
            sqv = S.on("s", lambda e, c=c: e.activation(B.sqt[c % 2], xslab4.ch(c), AF.Square),
                       waits=W(("dq", dxs), ("tt", tsq)), inc="ss")
            tsq = S.on("t", lambda e, c=c: e.matmul(PR(1), B.ones_m1, B.sqt[c % 2],
                                                    start=(c == 0), stop=(c == 15)),
                       waits=W(("ss", sqv)), inc="tt")
        sm1 = S.on("s", lambda e: e.activation(r_mean, PR(0), AF.Copy, scale=1.0 / D),
                   waits=W(("tt", tcs)), inc="ss")
        sm2 = S.on("s", lambda e: e.activation(r_msq, PR(1), AF.Copy, scale=1.0 / D),
                   waits=W(("tt", tsq)), inc="ss")
        v1 = S.on("v", lambda e: e.tensor_mul(r_t1, r_mean, r_mean), waits=W(("ss", sm2)),
                  inc="vv")
        v2 = S.on("v", lambda e: e.tensor_sub(r_t1, r_msq, r_t1), inc="vv")
        S.on("s", lambda e: e.activation(r_t1, r_t1, AF.Ln, bias=EPS_B),
             waits=W(("vv", v2)), inc="ss")
        sv2 = S.on("s", lambda e: e.activation(r_rstd, r_t1, AF.Exp, scale=-0.5), inc="ss")
        tb1 = S.on("t", lambda e: e.matmul(P(2), B.ones_k1f, r_mean, start=True, stop=True),
                   waits=W(("ss", sm1), ("vv", S.c("vv"))), inc="tt")
        tb2 = S.on("t", lambda e: e.matmul(P(3), B.ones_k1f, r_rstd, start=True, stop=True),
                   waits=W(("ss", sv2)), inc="tt")
        vb1 = S.on("v", lambda e: e.tensor_copy(B.mb, P(2)), waits=W(("tt", tb1)), inc="vv")
        vb2 = S.on("v", lambda e: e.tensor_copy(B.rb, P(3)), waits=W(("tt", tb2)), inc="vv")
        vxn = 0
        for c in range(16):
            S.on("v", lambda e, c=c: e.tensor_sub(B.tmpf[:, 0:512], xslab4.ch(c), B.mb),
                 waits=W(("vv", vb2)), inc="vv")
            vxn = S.on("v", lambda e, c=c: e.tensor_mul(xnf.ch(c), B.tmpf[:, 0:512], B.rb),
                       inc="vv")
        for ot in range(NH1):
            bsel = ot % 2
            pb = 4 + ot % 2
            dwf = S.dma("g", wfc_col[bsel].all3(), colload_ap(ext["whatfc"], ot),
                        waits=W(("tt", frd[bsel])))
            th = 0
            for c in range(16):
                th = S.on("t", lambda e, c=c, bsel=bsel, pb=pb: e.matmul(
                    P(pb), wfc_col[bsel].ch(c), xnf.ch(c),
                    start=(c == 0), stop=(c == 15)),
                    waits=W(("dg", dwf), ("vv", vxn), ("ss", swv[pb])), inc="tt")
            frd[bsel] = th
            swv[pb] = S.on("s", lambda e, ot=ot, pb=pb: e.activation(
                harena.ch(ot), P(pb), AF.Gelu), waits=W(("tt", th)), inc="ss")
        tcs = tsq = 0
        for ot in range(NH1):
            tcs = S.on("t", lambda e, ot=ot: e.matmul(PR(0), B.ones_m1, harena.ch(ot),
                                                      start=(ot == 0), stop=(ot == NH1 - 1)),
                       waits=W(("ss", swv[4] if ot == 0 else 0), ("ss", swv[5])), inc="tt")
        for ot in range(NH1):
            sqv = S.on("s", lambda e, ot=ot: e.activation(B.sqt[ot % 2], harena.ch(ot),
                                                          AF.Square),
                       waits=W(("tt", tsq)), inc="ss")
            tsq = S.on("t", lambda e, ot=ot: e.matmul(PR(1), B.ones_m1, B.sqt[ot % 2],
                                                      start=(ot == 0), stop=(ot == NH1 - 1)),
                       waits=W(("ss", sqv)), inc="tt")
        sm1 = S.on("s", lambda e: e.activation(r_mean, PR(0), AF.Copy, scale=1.0 / HID1),
                   waits=W(("tt", tcs)), inc="ss")
        sm2 = S.on("s", lambda e: e.activation(r_msq, PR(1), AF.Copy, scale=1.0 / HID1),
                   waits=W(("tt", tsq)), inc="ss")
        v1 = S.on("v", lambda e: e.tensor_mul(r_t1, r_mean, r_mean), waits=W(("ss", sm2)),
                  inc="vv")
        v2 = S.on("v", lambda e: e.tensor_sub(r_t1, r_msq, r_t1), inc="vv")
        S.on("s", lambda e: e.activation(r_t1, r_t1, AF.Ln, bias=EPS_B),
             waits=W(("vv", v2)), inc="ss")
        sv2 = S.on("s", lambda e: e.activation(r_rstd, r_t1, AF.Exp, scale=-0.5), inc="ss")
        tb1 = S.on("t", lambda e: e.matmul(P(2), B.ones_k1f, r_mean, start=True, stop=True),
                   waits=W(("ss", sm1), ("vv", S.c("vv"))), inc="tt")
        tb2 = S.on("t", lambda e: e.matmul(P(3), B.ones_k1f, r_rstd, start=True, stop=True),
                   waits=W(("ss", sv2)), inc="tt")
        vb1 = S.on("v", lambda e: e.tensor_copy(B.mb, P(2)), waits=W(("tt", tb1)), inc="vv")
        vb2 = S.on("v", lambda e: e.tensor_copy(B.rb, P(3)), waits=W(("tt", tb2)), inc="vv")
        hnorm = 0
        for ot in range(NH1):
            S.on("v", lambda e, ot=ot: e.tensor_sub(B.tmpf[:, 0:512], harena.ch(ot), B.mb),
                 waits=W(("vv", vb2)), inc="vv")
            hnorm = S.on("v", lambda e, ot=ot: e.tensor_mul(harena.ch(ot), B.tmpf[:, 0:512],
                                                            B.rb), inc="vv")
        S.on("g", lambda e: e.nop(), waits=W(("c3", 1)))
        dga = 0
        attg_v = ext["attg"].rearrange("e (b n) -> (e b) n", n=512)
        for c in range(16):
            dga = S.on("g", lambda e, c=c, sl=sl: e.indirect_dma_start(
                out=attE.ch(c), out_offset=None, in_=attg_v,
                in_offset=bass.IndirectOffsetOnAxis(idxE[:, c * 4 + sl:c * 4 + sl + 1], 0)),
                waits=W(("dg", dix), ("tt", hrd)), inc="dg", incn=16)
        for dt in range(16):
            pb = 6 + dt % 2
            hsel = dt % 2
            dwpA = S.dma("g", wpjh[0].all3(), colload_half(ext["whatpj"], dt, 0),
                         waits=W(("tt", prd[0])))
            dwo = S.dma("g", wo_c4[hsel].all3(), colload_ap(ext["wo1T"], dt),
                        waits=W(("tt", ord4[hsel])))
            dwpB = S.dma("g", wpjh[1].all3(), colload_half(ext["whatpj"], dt, 1),
                         waits=W(("tt", prd[1])))
            tj = 0
            for c in range(16):
                tj = S.on("t", lambda e, c=c, pb=pb, hsel=hsel: e.matmul(
                    P(pb), wo_c4[hsel].ch(c), attE.ch(c), start=(c == 0), stop=False),
                    waits=W(("dg", max(dwo, dga)), ("vv", jrd[pb])), inc="tt")
            ord4[hsel] = tj
            for hc in range(NH1):
                tj = S.on("t", lambda e, hc=hc, pb=pb: e.matmul(
                    P(pb), wpjh[hc // 32].ch(hc % 32), harena.ch(hc),
                    start=False, stop=(hc == NH1 - 1)),
                    waits=W(("dg", dwpA if hc < 32 else dwpB), ("vv", hnorm)), inc="tt")
                if hc == 31:
                    prd[0] = tj
            prd[1] = tj
            hrd = tj
            vo = S.on("v", lambda e, dt=dt, pb=pb: e.tensor_scalar(
                out=B.tA, in0=P(pb), scalar1=B.bo1_col[:, dt:dt + 1], scalar2=None,
                op0=ALU.add), waits=W(("tt", tj), ("dg", dbo), ("vv", vb_p),
                                      ("dq", dx2_p)), inc="vv")
            jrd[pb] = vo
            dx1_ = S.dma("q", B.tB, ext["x1T"][dt * 128:(dt + 1) * 128,
                                               sl * 512:sl * 512 + 512],
                         waits=W(("vv", vb_p)))
            vb_p = S.on("v", lambda e: e.tensor_add(B.tC, B.tA, B.tB),
                        waits=W(("dq", dx1_)), inc="vv")
            dx2_p = S.dma("q", ext["x2T"][dt * 128:(dt + 1) * 128, sl * 512:sl * 512 + 512],
                          B.tC.bitcast(F32) if False else B.tC, waits=W(("vv", vb_p)))
    if ext.get("DEBUG"):
        S.dma("g", ext["dbg_gam"][:, 0:4], B.gcol,
              waits=W(("vv", S.c("vv")), ("ss", S.c("ss"))))
        S.dma("g", ext["dbg_gam"][0:1, 4:8], B.r11[0:1, 0:4])
        S.dma("g", ext["dbg_wf"], ext["whatfc"][0:128, :])
        S.dma("g", ext["dbg_attc"], ext["attc"][:, :])
        S.dma("g", ext["dbg_q"], Qh)
        S.dma("g", ext["dbg_h"], harena.ap[:, 0:NH1 * 512])
        S.dma("g", ext["dbg_xnf"], xnf.ap[:, 0:NC * 512])
        S.dma("g", ext["dbg_atE"], attE.ap[:, 0:NC * 512])
    S.on("q", lambda e: e.nop(), waits=W(("dq", S.c("dq")), ("dg", S.c("dg"))))


def _l1_inmaps(inputs, l0_results):
    C = np.ascontiguousarray
    wqT = C(np.asarray(inputs["L1_attn_wq"]).T)
    wkT = C(np.asarray(inputs["L1_attn_wk"]).T)
    wvT = C(np.asarray(inputs["L1_attn_wv"]).T)
    woT = C(np.asarray(inputs["L1_attn_wo"]).T)
    bq = np.asarray(inputs["L1_attn_bq"])
    bk = np.asarray(inputs["L1_attn_bk"])
    bv = np.asarray(inputs["L1_attn_bv"])
    bo = np.asarray(inputs["L1_attn_bo"])
    wfcT = C(np.asarray(inputs["L1_ffn_w_fc"]).T)
    wpjT = C(np.asarray(inputs["L1_ffn_w_proj"]).T)
    col = lambda v, n: C(v.reshape(n, 128).T.astype(np.float32))
    maps = []
    for core in range(8):
        h = core % 2
        hs = h * 1024
        p = np.arange(128)[:, None]
        cs = np.arange(64)[None, :]
        idxE = ((cs // 4) * 128 + p) * 8 + h * 4 + (cs % 4)
        maps.append({
            **({"xn1T": np.asarray(l0_results[core]["xn1T"]),
                "x1T": np.asarray(l0_results[core]["x1T"])} if l0_results is not None else {}),
            "wq1T": C(wqT[:, hs:hs + 1024]), "b1q": col(bq[hs:hs + 1024], 8),
            "wk1T": C(wkT[:, hs:hs + 1024]), "b1k": col(bk[hs:hs + 1024], 8),
            "wv1T": C(wvT[:, hs:hs + 1024]), "b1v": col(bv[hs:hs + 1024], 8),
            "wo1T": woT, "bo1_col": col(bo, 16),
            "wfcT": wfcT, "wpjT": wpjT,
            "idxE": idxE.astype(np.int32),
        })
    return maps


def _build_l1_nc():
    nc = bass.Bass("TRN2", target_bir_lowering=False, debug=False)
    ext = {}
    EI = lambda n, s, d=F32: nc.dram_tensor(n, s, d, kind="ExternalInput").ap()
    ext["xn1T"] = EI("xn1T", [D, NOWN], BF16)
    ext["x1T"] = EI("x1T", [D, NOWN])
    ext["wq1T"] = EI("wq1T", [D, 1024])
    ext["b1q"] = EI("b1q", [128, 8])
    ext["wk1T"] = EI("wk1T", [D, 1024])
    ext["b1k"] = EI("b1k", [128, 8])
    ext["wv1T"] = EI("wv1T", [D, 1024])
    ext["b1v"] = EI("b1v", [128, 8])
    ext["wo1T"] = EI("wo1T", [D, D])
    ext["bo1_col"] = EI("bo1_col", [128, 16])
    ext["wfcT"] = EI("wfcT", [D, HID1])
    ext["wpjT"] = EI("wpjT", [HID1, D])
    ext["idxE"] = EI("idxE", [128, 64], I32)
    ext["x2T"] = nc.dram_tensor("x2T", [D, NOWN], F32, kind="ExternalOutput").ap()
    ext["DEBUG"] = DEBUG_L1
    if DEBUG_L1:
        ext["dbg_gam"] = nc.dram_tensor("dbg_gam", [128, 8], F32, kind="ExternalOutput").ap()
        ext["dbg_wf"] = nc.dram_tensor("dbg_wf", [128, HID1], F32, kind="ExternalOutput").ap()
        ext["dbg_attc"] = nc.dram_tensor("dbg_attc", [1024, NFULL], F32, kind="ExternalOutput").ap()
        ext["dbg_q"] = nc.dram_tensor("dbg_q", [128, NFULL], F32, kind="ExternalOutput").ap()
        ext["dbg_h"] = nc.dram_tensor("dbg_h", [128, NH1 * 512], F32, kind="ExternalOutput").ap()
        ext["dbg_xnf"] = nc.dram_tensor("dbg_xnf", [128, NC * 512], F32, kind="ExternalOutput").ap()
        ext["dbg_atE"] = nc.dram_tensor("dbg_atE", [128, NC * 512], F32, kind="ExternalOutput").ap()
    ext["xn1c"] = nc.dram_tensor("xn1c", [D, NOWN], BF16, kind="Internal").ap()
    ext["xn1g"] = nc.dram_tensor("xn1g", [2 * D, NOWN], BF16, kind="Internal").ap()
    ext["attc"] = nc.dram_tensor("attc", [1024, NFULL], BF16, kind="Internal").ap()
    ext["attg"] = nc.dram_tensor("attg", [2048, NFULL], BF16, kind="Internal").ap()
    ext["whatfc"] = nc.dram_tensor("whatfc", [D, HID1], BF16, kind="Internal").ap()
    ext["whatpj"] = nc.dram_tensor("whatpj", [HID1, D], BF16, kind="Internal").ap()

    B = Buf(nc)
    with nc.psum_tensor("PS", [128, 4096], F32) as PS:
        S = Sched(nc)
        build_common(S, B, nc)
        build_l1(S, B, ext, PS[:])
        S.finalize()
    return nc

def _l0_inmaps(inputs):
    x = np.asarray(inputs["L0_attn_x"])
    C = np.ascontiguousarray
    wqT = C(np.asarray(inputs["L0_attn_Wq"]).T)
    wkT = C(np.asarray(inputs["L0_attn_Wk"]).T)
    wvT = C(np.asarray(inputs["L0_attn_Wv"]).T)
    woT = C(np.asarray(inputs["L0_attn_Wo"]).T)
    bq = np.asarray(inputs["L0_attn_bq"])
    bk = np.asarray(inputs["L0_attn_bk"])
    bv = np.asarray(inputs["L0_attn_bv"])
    bo = np.asarray(inputs["L0_attn_bo"])
    wqgT = C(np.asarray(inputs["L0_ffn_wq_gate"]).T)
    wquT = C(np.asarray(inputs["L0_ffn_wq_up"]).T)
    wqdT = C(np.asarray(inputs["L0_ffn_wq_down"]).T)
    sg = np.asarray(inputs["L0_ffn_s_gate"])
    su = np.asarray(inputs["L0_ffn_s_up"])
    sd = np.asarray(inputs["L0_ffn_s_down"])
    col = lambda v, n: C(v.reshape(n, 128).T.astype(np.float32))
    maps = []
    for core in range(8):
        b, h = core // 2, core % 2
        xT = x[b].T
        own = xT[:, h * 2048:(h + 1) * 2048]
        oth = xT[:, (1 - h) * 2048:(2 - h) * 2048]
        hs = h * 1024
        maps.append({
            "x0T": C(np.concatenate([own, oth], axis=1)),
            "pos2": np.array([[h * 2048.0, (1 - h) * 2048.0]], np.float32),
            "wqT": wqT, "bq_col": col(bq, 16),
            "wkT_oh": C(wkT[:, hs:hs + 1024]), "bk_row": C(bk[None, hs:hs + 1024]),
            "wvT_oh": C(wvT[:, hs:hs + 1024]), "bv_row": C(bv[None, hs:hs + 1024]),
            "woT": woT, "bo_col": col(bo, 16),
            "wqgT": wqgT, "sg_col": col(sg, 44),
            "wquT": wquT, "su_col": col(su, 44),
            "wqdT": wqdT, "sd_col": col(sd, 16),
        })
    return maps


def _build_l0_nc():
    nc = bass.Bass("TRN2", target_bir_lowering=False, debug=False)
    ext = {}
    EI = lambda n, s, d=F32: nc.dram_tensor(n, s, d, kind="ExternalInput").ap()
    ext["x0T"] = EI("x0T", [D, NFULL])
    ext["pos2"] = EI("pos2", [1, 2])
    ext["wqT"] = EI("wqT", [D, D])
    ext["bq_col"] = EI("bq_col", [128, 16])
    ext["wkT_oh"] = EI("wkT_oh", [D, 1024])
    ext["bk_row"] = EI("bk_row", [1, 1024])
    ext["wvT_oh"] = EI("wvT_oh", [D, 1024])
    ext["bv_row"] = EI("bv_row", [1, 1024])
    ext["woT"] = EI("woT", [D, D])
    ext["bo_col"] = EI("bo_col", [128, 16])
    ext["wqgT"] = EI("wqgT", [D, HID0], I32)
    ext["sg_col"] = EI("sg_col", [128, 44])
    ext["wquT"] = EI("wquT", [D, HID0], I32)
    ext["su_col"] = EI("su_col", [128, 44])
    ext["wqdT"] = EI("wqdT", [HID0, D], I32)
    ext["sd_col"] = EI("sd_col", [128, 16])
    ext["x1T"] = nc.dram_tensor("x1T", [D, NOWN], F32, kind="ExternalOutput").ap()
    ext["xn1T"] = nc.dram_tensor("xn1T", [D, NOWN], BF16, kind="ExternalOutput").ap()
    ext["dbg_kv"] = nc.dram_tensor("dbg_kv", [128, 2048], F32, kind="ExternalOutput").ap()
    ext["dbg_ks"] = nc.dram_tensor("dbg_ks", [1, 2048], F32, kind="ExternalOutput").ap()
    ext["dbg_at"] = nc.dram_tensor("dbg_at", [D, NOWN], F32, kind="ExternalOutput").ap()
    ext["dbg_f0"] = nc.dram_tensor("dbg_f0", [D, NOWN], F32, kind="ExternalOutput").ap()
    ext["dbg_q"] = nc.dram_tensor("dbg_q", [D, NOWN], F32, kind="ExternalOutput").ap()
    ext["kvc"] = nc.dram_tensor("kvc", [1032, 128], F32, kind="Internal").ap()
    ext["kvg"] = nc.dram_tensor("kvg", [2064, 128], F32, kind="Internal").ap()
    ext["f0s"] = nc.dram_tensor("f0s", [D, NOWN], BF16, kind="Internal").ap()
    ext["gus"] = nc.dram_tensor("gus", [HID0, NOWN], BF16, kind="Internal").ap()

    ext["DEBUG"] = DEBUG_L0
    B = Buf(nc)
    with nc.psum_tensor("PS", [128, 4096], F32) as PS:
        S = Sched(nc)
        build_common(S, B, nc)
        build_l0(S, B, ext, PS[:])
        S.finalize()
    return nc


def _build_fused_nc():
    nc = bass.Bass("TRN2", target_bir_lowering=False, debug=False)
    ext = {}
    EI = lambda n, s, d=F32: nc.dram_tensor(n, s, d, kind="ExternalInput").ap()
    IN = lambda n, s, d=F32: nc.dram_tensor(n, s, d, kind="Internal").ap()
    ext["x0T"] = EI("x0T", [D, NFULL])
    ext["pos2"] = EI("pos2", [1, 2])
    ext["wqT"] = EI("wqT", [D, D])
    ext["bq_col"] = EI("bq_col", [128, 16])
    ext["wkT_oh"] = EI("wkT_oh", [D, 1024])
    ext["bk_row"] = EI("bk_row", [1, 1024])
    ext["wvT_oh"] = EI("wvT_oh", [D, 1024])
    ext["bv_row"] = EI("bv_row", [1, 1024])
    ext["woT"] = EI("woT", [D, D])
    ext["bo_col"] = EI("bo_col", [128, 16])
    ext["wqgT"] = EI("wqgT", [D, HID0], I32)
    ext["sg_col"] = EI("sg_col", [128, 44])
    ext["wquT"] = EI("wquT", [D, HID0], I32)
    ext["su_col"] = EI("su_col", [128, 44])
    ext["wqdT"] = EI("wqdT", [HID0, D], I32)
    ext["sd_col"] = EI("sd_col", [128, 16])
    ext["wq1T"] = EI("wq1T", [D, 1024])
    ext["b1q"] = EI("b1q", [128, 8])
    ext["wk1T"] = EI("wk1T", [D, 1024])
    ext["b1k"] = EI("b1k", [128, 8])
    ext["wv1T"] = EI("wv1T", [D, 1024])
    ext["b1v"] = EI("b1v", [128, 8])
    ext["wo1T"] = EI("wo1T", [D, D])
    ext["bo1_col"] = EI("bo1_col", [128, 16])
    ext["wfcT"] = EI("wfcT", [D, HID1])
    ext["wpjT"] = EI("wpjT", [HID1, D])
    ext["idxE"] = EI("idxE", [128, 64], I32)
    ext["x2T"] = nc.dram_tensor("x2T", [D, NOWN], F32, kind="ExternalOutput").ap()
    ext["x1T"] = IN("x1s", [D, NOWN])
    ext["xn1T"] = IN("xn1s", [D, NOWN], BF16)
    ext["kvc"] = IN("kvc", [1032, 128])
    ext["kvg"] = IN("kvg", [2064, 128])
    ext["f0s"] = IN("f0s", [D, NOWN], BF16)
    ext["gus"] = IN("gus", [HID0, NOWN], BF16)
    ext["xn1c"] = IN("xn1c", [D, NOWN], BF16)
    ext["xn1g"] = IN("xn1g", [2 * D, NOWN], BF16)
    ext["attc"] = IN("attc", [1024, NFULL], BF16)
    ext["attg"] = IN("attg", [2048, NFULL], BF16)
    ext["whatfc"] = IN("whatfc", [D, HID1], BF16)
    ext["whatpj"] = IN("whatpj", [HID1, D], BF16)
    ext["DEBUG"] = False
    if DEBUG_FUSED:
        ext["dbg_gam"] = nc.dram_tensor("dbg_gam", [128, 8], F32, kind="ExternalOutput").ap()
        ext["dbg_wf"] = nc.dram_tensor("dbg_wf", [128, HID1], F32, kind="ExternalOutput").ap()
        ext["dbg_wp"] = nc.dram_tensor("dbg_wp", [128, D], F32, kind="ExternalOutput").ap()

    B = Buf(nc)
    with nc.psum_tensor("PS", [128, 4096], F32) as PS:
        S = Sched(nc)
        build_common(S, B, nc)
        tasks = l1w_tasks(S, B, ext, PS[:])
        build_l0(S, B, ext, PS[:], l1w=tasks)
        ext["WPREP_DONE"] = True
        build_l1(S, B, ext, PS[:])
        if DEBUG_FUSED:
            S.dma("g", ext["dbg_gam"][:, 0:4], B.gcol,
                  waits=[("vv", S.c("vv")), ("ss", S.c("ss")), ("dq", S.c("dq"))])
            S.dma("g", ext["dbg_gam"][:, 4:6], B.acc)
            S.dma("g", ext["dbg_gam"][:, 6:8], B.gacc[:, 0:2])
            S.dma("g", ext["dbg_wp"][:, 0:16], B.gacc)
            S.dma("g", ext["dbg_wf"], ext["whatfc"][0:128, :])
            S.dma("g", ext["dbg_wp"], ext["whatpj"][256:384, :])
            S.on("q", lambda e: e.nop(), waits=[("dg", S.c("dg"))])
        S.finalize()
    return nc


FUSED = True
_NC_CACHE = {}


def kernel(**inputs):
    maps0 = _l0_inmaps(inputs)
    if FUSED:
        maps1 = _l1_inmaps(inputs, None)
        maps = [dict(m0, **m1) for m0, m1 in zip(maps0, maps1)]
        if "fused" not in _NC_CACHE:
            _NC_CACHE["fused"] = _build_fused_nc()
        res1 = run_bass_kernel_spmd(_NC_CACHE["fused"], maps, core_ids=list(range(8)))
    else:
        nc0 = _build_l0_nc()
        res0 = run_bass_kernel_spmd(nc0, maps0, core_ids=list(range(8)))
        maps1 = _l1_inmaps(inputs, res0.results)
        nc1 = _build_l1_nc()
        res1 = run_bass_kernel_spmd(nc1, maps1, core_ids=list(range(8)))
    outs = []
    for b in range(NB):
        xa = np.asarray(res1.results[2 * b]["x2T"], np.float32)
        xb_ = np.asarray(res1.results[2 * b + 1]["x2T"], np.float32)
        outs.append(np.concatenate([xa, xb_], axis=1).T)
    return np.stack(outs).astype(np.float32)


if __name__ == "__main__":
    nc = _build_l0_nc()
    n = sum(len(v) for v in Sched(bass.Bass("TRN2", target_bir_lowering=False)).ops.values())
    print("L0 build OK")


# version 15
# speedup vs baseline: 1.0179x; 1.0179x over previous
import sys
sys.path.insert(0, "/opt/trn_rl_repo")
import numpy as np
import ml_dtypes
import concourse.bass as bass
import concourse.mybir as mybir
from concourse.bass_utils import run_bass_kernel_spmd

F32 = mybir.dt.float32
BF16 = mybir.dt.bfloat16
I32 = mybir.dt.int32
U8 = mybir.dt.uint8
AF = mybir.ActivationFunctionType
ALU = mybir.AluOpType

D = 2048
NC = 16
NOWN = 2048
NFULL = 4096
NB = 4
HID0 = 5632
NH0 = 44
HID1 = 8192
NH1 = 64
EPS_LN = 1e-5
EPS_Z = 1e-6
EPS_B = 1e-6
CQK = float(1.0 / np.sqrt(128.0))
GROUPS = [[0, 1], [2, 3], [4, 5], [6, 7]]
DEBUG_L0 = False
DEBUG_L1 = False
N_W_FC = float(HID1 * D)


class Sched:
    ENGS = ("t", "v", "s", "g", "q")

    def __init__(self, nc):
        self.nc = nc
        self.ops = {e: [] for e in self.ENGS}
        self.cnt = {}
        self.sems = {}
        self.lastw = {e: {} for e in self.ENGS}
        self._ctx = []

    def sem(self, name):
        if name not in self.sems:
            g = self.nc.semaphore(name)
            self.sems[name] = g.__enter__()
            self._ctx.append(g)
            self.cnt[name] = 0
        return self.sems[name]

    def on(self, e, fn, waits=(), inc=None, incn=1):
        wl = []
        for sname, v in waits:
            if v <= 0:
                continue
            self.sem(sname)
            if self.lastw[e].get(sname, -1) >= v:
                continue
            self.lastw[e][sname] = v
            wl.append((self.sems[sname], v))
        ih = self.sem(inc) if inc else None

        def run(eng, fn=fn, wl=tuple(wl), ih=ih, incn=incn):
            for s, v in wl:
                eng.wait_ge(s, v)
            r = fn(eng)
            if ih is not None:
                r.then_inc(ih, incn)
        self.ops[e].append(run)
        if inc:
            self.cnt[inc] += incn
            return self.cnt[inc]
        return None

    def dma(self, e, out, in_, waits=()):
        sem = "dg" if e == "g" else "dq"
        self.sem(sem)
        return self.on(e, lambda eng, o=out, i=in_: eng.dma_start(out=o, in_=i),
                       waits, inc=sem, incn=16)

    def c(self, name):
        return self.cnt.get(name, 0)

    def fence(self, engines="tvsgq"):
        vals = [(s, self.cnt[s]) for s in list(self.cnt)]
        for e in engines:
            self.on(e, lambda eng: eng.nop(), waits=vals)

    def finalize(self):
        nc = self.nc
        with nc.Block() as blk:
            @blk.tensor
            def _(t):
                for f in self.ops["t"]:
                    f(t)

            @blk.vector
            def _(v):
                for f in self.ops["v"]:
                    f(v)

            @blk.scalar
            def _(s):
                for f in self.ops["s"]:
                    f(s)

            @blk.gpsimd
            def _(g):
                for f in self.ops["g"]:
                    f(g)

            @blk.sync
            def _(q):
                for f in self.ops["q"]:
                    f(q)


def colload_half(wT, t, half):
    K = wT.shape[0]
    return wT[half * K // 2:(half + 1) * K // 2,
              t * 128:(t + 1) * 128].rearrange("(c p) o -> p c o", p=128)


def colload_ap(wT, t, w=128):
    return wT[:, t * w:(t + 1) * w].rearrange("(c p) o -> p c o", p=128)


def slab_ap(wT, c0, c1):
    return wT[:, c0:c1].rearrange("(c p) o -> p c o", p=128)


class FM:

    def __init__(self, ap, nchunk, w):
        self.ap, self.nchunk, self.w = ap, nchunk, w

    def ch(self, c, a=0, b=None):
        b = self.w if b is None else b
        return self.ap[:, c * self.w + a: c * self.w + b]

    def all3(self):
        return self.ap[:, 0:self.nchunk * self.w].rearrange("p (c n) -> p c n", n=self.w)


class Buf:

    def __init__(self, nc):
        A = lambda name, shape, dt: nc.alloc_sbuf_tensor(name, shape, dt).ap()
        self.AR1 = A("sb_AR1", [128, 65536], U8)
        self.AR3 = A("sb_AR3", [128, 65536], U8)
        self.ST = A("sb_ST", [128, 34816], U8)
        self.ROWS = A("sb_rows", [1, 4096], F32)
        self.drow = A("sb_drow", [1, 2048], BF16)
        self.ksum_sb = A("sb_ksum_sb", [1, 2048], BF16)
        self.kv16 = A("sb_kv16", [128, 2048], BF16)
        self.tmpf = A("sb_tmpf", [128, 512], F32)
        self.mb = A("sb_mb", [128, 512], F32)
        self.rb = A("sb_rb", [128, 512], F32)
        self.sqt = [A("sb_sqt0", [128, 512], BF16), A("sb_sqt1", [128, 512], BF16)]
        self.mstrip = A("sb_mstrip", [128, 896], BF16)
        self.ident = A("sb_ident", [128, 128], BF16)
        self.cst = A("sb_cst", [128, 8], F32)
        self.ones_k1 = A("sb_ones_k1", [1, 128], BF16)
        self.n128_k1 = A("sb_n128_k1", [1, 128], BF16)
        self.ones_m1 = A("sb_ones_m1", [128, 1], BF16)
        self.ones_k1f = A("sb_ones_k1f", [1, 128], F32)
        self.ones_m1f = A("sb_ones_m1f", [128, 1], F32)
        self.posf = A("sb_posf", [128, 16], F32)
        self.decay_col = A("sb_decay_col", [128, 32], F32)
        self.poso = A("sb_poso", [1, 2], F32)
        self.offb = A("sb_offb", [1, 2], F32)
        self.bq_col = A("sb_bq_col", [128, 16], F32)
        self.bo_col = A("sb_bo_col", [128, 16], F32)
        self.sg_col = A("sb_sg_col", [128, 44], F32)
        self.su_col = A("sb_su_col", [128, 44], F32)
        self.sd_col = A("sb_sd_col", [128, 16], F32)
        self.bk_row = A("sb_bk_row", [1, 1024], BF16)
        self.bv_row = A("sb_bv_row", [1, 1024], BF16)
        self.b1q = A("sb_b1q", [128, 8], F32)
        self.b1k = A("sb_b1k", [128, 8], F32)
        self.b1v = A("sb_b1v", [128, 8], F32)
        self.bo1_col = A("sb_bo1_col", [128, 16], F32)
        self.gam = A("sb_gam", [1, 4], F32)
        self.gcol = A("sb_gcol", [128, 4], F32)
        self.acc = A("sb_acc", [128, 2], F32)
        self.r11 = A("sb_r11", [1, 8], F32)
        self.negC = A("sb_negC", [128, 1], F32)
        AR3f = self.AR3.bitcast(F32)
        AR3b_ = self.AR3.bitcast(BF16)
        self.tA = AR3f[:, 13824:14336]
        self.tB = AR3f[:, 14336:14848]
        self.tC = AR3f[:, 14848:15360]
        self.tD = AR3b_[:, 30720:31232]
        self.tE = AR3b_[:, 31232:31744]


def bf(ap):
    return ap.bitcast(BF16)


def f32(ap):
    return ap.bitcast(F32)


def build_common(S, B, nc):
    CONSTS = [EPS_LN, EPS_Z, EPS_B, -128.0, -0.25, 0.5, 1.0, 0.0]
    for j, cv in enumerate(CONSTS):
        S.on("g", lambda e, j=j, cv=cv: e.memset(B.cst[:, j:j + 1], cv))
        nc.const_aps.register(F32, cv, B.cst[:, j:j + 1])
    S.on("g", lambda e: e.memset(B.ones_k1, 1.0))
    S.on("g", lambda e: e.memset(B.n128_k1, -128.0))
    S.on("g", lambda e: e.memset(B.ones_m1, 1.0))
    S.on("g", lambda e: e.memset(B.ones_k1f, 1.0))
    S.on("g", lambda e: e.memset(B.ones_m1f, 1.0))
    S.on("g", lambda e: e.memset(B.ident, 0.0))
    S.on("g", lambda e: e.affine_select(out=B.ident, in_=B.ident, compare_op=ALU.not_equal,
                                        fill=1.0, base=0, pattern=[[-1, 128]],
                                        channel_multiplier=1))
    S.on("g", lambda e: e.memset(B.mstrip, 1.0))
    S.on("g", lambda e: e.affine_select(out=B.mstrip, in_=B.mstrip, compare_op=ALU.is_ge,
                                        fill=0.0, base=-384, pattern=[[1, 896]],
                                        channel_multiplier=-1))


def build_l0(S, B, ext, PS):
    nc = S.nc
    P = lambda b, n=1: PS[:, b * 512:(b + n) * 512]
    PR = lambda b, w=512: PS[0:1, b * 512:b * 512 + w]
    W = lambda *a: list(a)

    xn0 = FM(bf(B.AR1), NC, NOWN)
    xnO = FM(bf(B.AR3), NC, NOWN)
    guA = FM(bf(B.AR3), NH0, 512)
    qA = FM(bf(B.AR3), NC, NOWN)
    atA = FM(bf(B.AR1), NC, NOWN)

    STb = bf(B.ST)
    xslab = FM(STb[:, 0:NC * 512], NC, 512)
    ktile16 = FM(STb[:, 0:NC * 512], NC, 512)
    wg_c = [FM(STb[:, i * 2048:(i + 1) * 2048], NC, 128) for i in (0, 1)]
    wu_c = [FM(STb[:, (2 + i) * 2048:(3 + i) * 2048], NC, 128) for i in (0, 1)]
    wd_c = FM(STb[:, 0:NH0 * 128], NH0, 128)
    wo_c = [FM(STb[:, i * 2048:(i + 1) * 2048], NC, 128) for i in (0, 1)]
    wbuf = FM(STb[:, 8192:8192 + NC * 512], NC, 512)
    wgd = FM(STb[:, 8192:8192 + 2048], NC, 128)
    wud = FM(STb[:, 8192 + 2048:8192 + 4096], NC, 128)
    wdd = FM(STb[:, 8192:8192 + NH0 * 128], NH0, 128)
    vtile = STb[:, 16384:16896]
    gtmp = STb[:, 16896:17408]
    x1colb = STb[:, 8192:16384]

    ROWS = B.ROWS
    r_mean, r_msq, r_t1, r_rstd = (ROWS[0:1, i * 512:(i + 1) * 512] for i in range(4))
    r_u = ROWS[0:1, 0:2048]
    r_zb = ROWS[0:1, 2048:4096]

    gP0 = S.on("g", lambda e: e.iota(B.posf, pattern=[[128, 16]], base=0,
                                     channel_multiplier=1,
                                     allow_small_or_imprecise_dtypes=True), inc="gg")
    gP1 = S.on("g", lambda e: e.iota(B.drow, pattern=[[1, 2048]], base=0,
                                     channel_multiplier=0,
                                     allow_small_or_imprecise_dtypes=True), inc="gg")
    d0 = S.dma("g", B.poso, ext["pos2"])
    S.dma("g", B.bq_col, ext["bq_col"])
    S.dma("g", B.bo_col, ext["bo_col"])
    S.dma("g", B.sg_col, ext["sg_col"])
    S.dma("g", B.su_col, ext["su_col"])
    S.dma("g", B.sd_col, ext["sd_col"])
    S.dma("g", B.bk_row, ext["bk_row"])
    dsm = S.dma("g", B.bv_row, ext["bv_row"])
    s0 = S.on("s", lambda e: e.activation(B.offb, B.poso, AF.Copy, scale=-1.0 / NFULL),
              waits=W(("dg", d0)), inc="ss")
    tof = S.on("t", lambda e: e.matmul(PS[:, 0:2], B.ones_k1f, B.offb,
                                       start=True, stop=True),
               waits=W(("ss", s0)), inc="tt")
    vof = S.on("v", lambda e: e.tensor_copy(B.gcol[:, 0:2], PS[:, 0:2]),
               waits=W(("tt", tof)), inc="vv")
    S.on("s", lambda e: e.activation(B.decay_col[:, 0:16], B.posf, AF.Exp,
                                     scale=-1.0 / NFULL, bias=B.gcol[:, 0:1]),
         waits=W(("gg", gP0), ("vv", vof)), inc="ss")
    sdc = S.on("s", lambda e: e.activation(B.decay_col[:, 16:32], B.posf, AF.Exp,
                                           scale=-1.0 / NFULL, bias=B.gcol[:, 1:2]),
               inc="ss")
    sdr = S.on("s", lambda e: e.activation(B.drow, B.drow, AF.Exp,
                                           scale=-1.0 / NFULL, bias=B.offb[0:1, 0:1]),
               waits=W(("gg", gP1), ("ss", s0)), inc="ss")

    for half in (0, 1):
        cb = half * 2048
        xdst = xn0 if half == 0 else xnO
        for s4 in range(4):
            dx = S.dma("g", xslab.all3(),
                       slab_ap(ext["x0T"], cb + s4 * 512, cb + s4 * 512 + 512),
                       waits=W(("tt", S.c("tt")), ("vv", S.c("vv")), ("ss", S.c("ss"))))
            tcs = 0
            tsq = 0
            for c in range(16):
                tcs = S.on("t", lambda e, c=c: e.matmul(PR(0), B.ones_m1, xslab.ch(c),
                                                        start=(c == 0), stop=(c == 15)),
                           waits=W(("dg", dx), ("ss", S.c("ss")), ("vv", S.c("vv"))),
                           inc="tt")
            for c in range(16):
                sqv = S.on("s", lambda e, c=c: e.activation(B.sqt[c % 2], xslab.ch(c),
                                                            AF.Square),
                           waits=W(("dg", dx), ("tt", tsq)), inc="ss")
                tsq = S.on("t", lambda e, c=c: e.matmul(PR(1), B.ones_m1, B.sqt[c % 2],
                                                        start=(c == 0), stop=(c == 15)),
                           waits=W(("ss", sqv)), inc="tt")
            sm1 = S.on("s", lambda e: e.activation(r_mean, PR(0), AF.Copy, scale=1.0 / D),
                       waits=W(("tt", tcs)), inc="ss")
            sm2 = S.on("s", lambda e: e.activation(r_msq, PR(1), AF.Copy, scale=1.0 / D),
                       waits=W(("tt", tsq)), inc="ss")
            v1 = S.on("v", lambda e: e.tensor_mul(r_t1, r_mean, r_mean),
                      waits=W(("ss", sm2)), inc="vv")
            v2 = S.on("v", lambda e: e.tensor_sub(r_t1, r_msq, r_t1), inc="vv")
            sv1 = S.on("s", lambda e: e.activation(r_t1, r_t1, AF.Ln, bias=EPS_LN),
                       waits=W(("vv", v2)), inc="ss")
            sv2 = S.on("s", lambda e: e.activation(r_rstd, r_t1, AF.Exp, scale=-0.5),
                       inc="ss")
            tb1 = S.on("t", lambda e: e.matmul(P(2), B.ones_k1f, r_mean,
                                               start=True, stop=True),
                       waits=W(("ss", sm1), ("vv", S.c("vv"))), inc="tt")
            tb2 = S.on("t", lambda e: e.matmul(P(3), B.ones_k1f, r_rstd,
                                               start=True, stop=True),
                       waits=W(("ss", sv2)), inc="tt")
            vb1 = S.on("v", lambda e: e.tensor_copy(B.mb, P(2)), waits=W(("tt", tb1)),
                       inc="vv")
            vb2 = S.on("v", lambda e: e.tensor_copy(B.rb, P(3)), waits=W(("tt", tb2)),
                       inc="vv")
            for c in range(16):
                dst = xdst.ch(c, s4 * 512, s4 * 512 + 512)
                S.on("v", lambda e, c=c: e.tensor_sub(B.tmpf[:, 0:512], xslab.ch(c), B.mb),
                     waits=W(("vv", vb2)), inc="vv")
                S.on("v", lambda e, dst=dst: e.tensor_mul(dst, B.tmpf[:, 0:512], B.rb),
                     inc="vv")
    vxn_done = S.c("vv")

    kv_first = {4: True, 5: True}
    ks_first = {6: True, 7: True}
    for half in (0, 1):
        xsrc = xn0 if half == 0 else xnO
        for sc in range(2):
            dwk = S.dma("g", wbuf.all3(), slab_ap(ext["wkT_oh"], sc * 512, sc * 512 + 512),
                        waits=W(("tt", S.c("tt"))))
            ek = [0] * 16
            for i in range(16):
                bk_ = i % 2
                tk = 0
                for c in range(16):
                    tk = S.on("t", lambda e, c=c, i=i, bk_=bk_, xsrc=xsrc:
                              e.matmul(P(bk_), xsrc.ch(c, i * 128, i * 128 + 128),
                                       wbuf.ch(c), start=(c == 0), stop=False),
                              waits=W(("dg", dwk), ("vv", vxn_done),
                                      ("ss", ek[i - 2] if i >= 2 else 0)), inc="tt")
                tk = S.on("t", lambda e, sc=sc, bk_=bk_:
                          e.matmul(P(bk_), B.ones_k1, B.bk_row[0:1, sc * 512:sc * 512 + 512],
                                   start=False, stop=True),
                          waits=W(("dg", dsm)), inc="tt")
                dci = half * 16 + i
                ek[i] = S.on("s", lambda e, i=i, dci=dci, bk_=bk_:
                             e.activation(ktile16.ch(i), P(bk_), AF.Relu,
                                          scale=B.decay_col[:, dci:dci + 1]),
                             waits=W(("tt", tk)), inc="ss")
            dwv = S.dma("g", wbuf.all3(), slab_ap(ext["wvT_oh"], sc * 512, sc * 512 + 512),
                        waits=W(("tt", S.c("tt"))))
            evp = 0
            for i in range(16):
                bv_ = 2 + i % 2
                tv = 0
                for c in range(16):
                    tv = S.on("t", lambda e, c=c, i=i, bv_=bv_, xsrc=xsrc:
                              e.matmul(P(bv_), xsrc.ch(c, i * 128, i * 128 + 128),
                                       wbuf.ch(c), start=(c == 0), stop=False),
                              waits=W(("dg", dwv), ("vv", S.c("vv")),
                                      ("ss", evp if i >= 1 else 0)), inc="tt")
                tv = S.on("t", lambda e, sc=sc, bv_=bv_:
                          e.matmul(P(bv_), B.ones_k1, B.bv_row[0:1, sc * 512:sc * 512 + 512],
                                   start=False, stop=True), inc="tt")
                ev = S.on("s", lambda e, bv_=bv_: e.activation(vtile, P(bv_), AF.Copy),
                          waits=W(("tt", tv)), inc="ss")
                evp = ev
                for hh in range(4):
                    o = hh * 128
                    ft = kv_first[4 + sc]
                    kv_first[4 + sc] = False
                    S.on("t", lambda e, o=o, sc=sc, i=i, ft=ft:
                         e.matmul(PS[:, (4 + sc) * 512 + o:(4 + sc) * 512 + o + 128],
                                  ktile16.ap[:, i * 512 + o:i * 512 + o + 128],
                                  vtile[:, o:o + 128],
                                  start=ft, stop=False, skip_group_check=True),
                         waits=W(("ss", ev)), inc="tt")
                ft = ks_first[6 + sc]
                ks_first[6 + sc] = False
                S.on("t", lambda e, i=i, sc=sc, ft=ft:
                     e.matmul(PS[0:1, (6 + sc) * 512:(6 + sc) * 512 + 512],
                              B.ones_m1, ktile16.ch(i),
                              start=ft, stop=False, skip_group_check=True),
                     inc="tt")
    tkv_done = S.c("tt")

    vk1 = S.on("v", lambda e: e.tensor_copy(B.tmpf[:, 0:512], P(4)),
               waits=W(("tt", tkv_done)), inc="vv")
    dk1 = S.dma("q", ext["kvc"][0:512].rearrange("(h d) e -> d h e", d=128),
                B.tmpf[:, 0:512].rearrange("p (h e) -> p h e", e=128),
                waits=W(("vv", vk1)))
    vk2 = S.on("v", lambda e: e.tensor_copy(B.tmpf[:, 0:512], P(5)),
               waits=W(("dq", dk1)), inc="vv")
    dk2 = S.dma("q", ext["kvc"][512:1024].rearrange("(h d) e -> d h e", d=128),
                B.tmpf[:, 0:512].rearrange("p (h e) -> p h e", e=128),
                waits=W(("vv", vk2)))
    vk3 = S.on("v", lambda e: e.tensor_copy(r_u[0:1, 0:1024], PS[0:1, 6 * 512:6 * 512 + 1024]),
               inc="vv")
    dk3 = S.dma("q", ext["kvc"][1024:1032].rearrange("h e -> (h e)")[None, :],
                r_u[0:1, 0:1024], waits=W(("vv", vk3)))
    c1 = S.on("g", lambda e: e.collective_compute(
        "AllGather", ALU.bypass, replica_groups=GROUPS,
        ins=[ext["kvc"][:]], outs=[ext["kvg"][:]]),
        waits=W(("dq", max(dk1, dk2, dk3))), inc="c1")

    csxn = B.ksum_sb
    csgu = B.bk_row
    tcx = 0
    vcx = S.c("vv")
    for q4 in range(4):
        for c in range(16):
            tcx = S.on("t", lambda e, c=c, q4=q4: e.matmul(
                PR(0), B.ones_m1, xn0.ch(c, q4 * 512, q4 * 512 + 512),
                start=(c == 0), stop=(c == 15)),
                waits=W(("vv", max(vxn_done, vcx)), ("ss", S.c("ss"))), inc="tt")
        vcx = S.on("v", lambda e, q4=q4: e.tensor_copy(csxn[0:1, q4 * 512:q4 * 512 + 512],
                                                       PR(0)),
                   waits=W(("tt", tcx)), inc="vv")
    csxn_v = S.c("vv")
    gu_sb = STb[:, 16384:16896]
    gtmp = STb[:, 16896:17408]
    evg = {b: S.c("ss") for b in range(4)}
    evu = {b: S.c("vv") for b in range(8)}
    dgu_p = 0
    wrd = {0: 0, 1: 0}
    for ht in range(NH0):
        bsel = ht % 2
        dgc = S.dma("g", wg_c[bsel].all3(), colload_ap(ext["wqgT"], ht),
                    waits=W(("tt", wrd[bsel])))
        duc = S.dma("g", wu_c[bsel].all3(), colload_ap(ext["wquT"], ht))
        tg = {}
        for q4 in range(4):
            t_ = 0
            for c in range(16):
                t_ = S.on("t", lambda e, c=c, q4=q4, bsel=bsel: e.matmul(
                    P(q4), wg_c[bsel].ch(c), xn0.ch(c, q4 * 512, q4 * 512 + 512),
                    start=(c == 0), stop=False),
                    waits=W(("dg", dgc), ("ss", evg[q4]), ("vv", csxn_v)), inc="tt")
            tg[q4] = S.on("t", lambda e, q4=q4: e.matmul(
                P(q4), B.n128_k1, csxn[0:1, q4 * 512:q4 * 512 + 512],
                start=False, stop=True), inc="tt")
        tu = {}
        for q4 in range(4):
            t_ = 0
            for c in range(16):
                t_ = S.on("t", lambda e, c=c, q4=q4, bsel=bsel: e.matmul(
                    P(4 + q4), wu_c[bsel].ch(c), xn0.ch(c, q4 * 512, q4 * 512 + 512),
                    start=(c == 0), stop=False),
                    waits=W(("dg", duc), ("vv", evu[4 + q4])), inc="tt")
            tu[q4] = S.on("t", lambda e, q4=q4: e.matmul(
                P(4 + q4), B.n128_k1, csxn[0:1, q4 * 512:q4 * 512 + 512],
                start=False, stop=True), inc="tt")
        wrd[bsel] = tu[3]
        vg = 0
        for q4 in range(4):
            sg = S.on("s", lambda e, ht=ht, q4=q4: e.activation(
                gtmp, P(q4), AF.Silu, scale=B.sg_col[:, ht:ht + 1]),
                waits=W(("tt", tg[q4]), ("vv", vg if q4 else S.c("vv"))), inc="ss")
            evg[q4] = sg
            vu = S.on("v", lambda e, ht=ht, q4=q4: e.tensor_scalar(
                out=B.tmpf[:, 0:512], in0=P(4 + q4), scalar1=B.su_col[:, ht:ht + 1],
                scalar2=None, op0=ALU.mult), waits=W(("tt", tu[q4])), inc="vv")
            evu[4 + q4] = vu
            vg = S.on("v", lambda e: e.tensor_mul(gu_sb, gtmp, B.tmpf[:, 0:512]),
                      waits=W(("ss", sg), ("dq", dgu_p)), inc="vv")
            dgu_p = S.dma("q", ext["gus"][ht * 128:(ht + 1) * 128, q4 * 512:q4 * 512 + 512],
                          gu_sb, waits=W(("vv", vg)))
    wd_c2 = [FM(STb[:, 0:NH0 * 128], NH0, 128),
             FM(STb[:, 8192:8192 + NH0 * 128], NH0, 128)]
    evd = {b: S.c("ss") for b in (2, 3)}
    drd = {0: S.c("tt"), 1: S.c("tt")}
    for ns in range(4):
        n0 = ns * 512
        dgs = S.dma("g", guA.all3(), slab_ap(ext["gus"], n0, n0 + 512),
                    waits=W(("tt", S.c("tt")), ("dq", S.c("dq"))))
        tcg = 0
        for hc in range(NH0):
            tcg = S.on("t", lambda e, hc=hc: e.matmul(
                PR(6), B.ones_m1, guA.ch(hc), start=(hc == 0), stop=(hc == NH0 - 1)),
                waits=W(("dg", dgs), ("vv", S.c("vv")), ("ss", S.c("ss"))), inc="tt")
        vcg = S.on("v", lambda e: e.tensor_copy(csgu[0:1, 0:512], PR(6)),
                   waits=W(("tt", tcg)), inc="vv")
        for dt in range(16):
            bsel = dt % 2
            ddc = S.dma("g", wd_c2[bsel].all3(), colload_ap(ext["wqdT"], dt),
                        waits=W(("tt", drd.get(bsel, 0))))
            pb = 2 + dt % 2
            td = 0
            for hc in range(NH0):
                td = S.on("t", lambda e, hc=hc, pb=pb, bsel=bsel: e.matmul(
                    P(pb), wd_c2[bsel].ch(hc), guA.ch(hc),
                    start=(hc == 0), stop=False),
                    waits=W(("dg", ddc), ("ss", evd[pb]), ("vv", vcg)), inc="tt")
            td = S.on("t", lambda e, pb=pb: e.matmul(
                P(pb), B.n128_k1, csgu[0:1, 0:512], start=False, stop=True), inc="tt")
            drd[bsel] = td
            sf = S.on("s", lambda e, dt=dt, pb=pb: e.activation(
                B.tD, P(pb), AF.Copy, scale=B.sd_col[:, dt:dt + 1]),
                waits=W(("tt", td)), inc="ss")
            evd[pb] = sf
            S.dma("q", ext["f0s"][dt * 128:(dt + 1) * 128, n0:n0 + 512], B.tD,
                  waits=W(("ss", sf)))

    qev = {0: S.c("ss"), 1: S.c("ss")}
    qrd = {0: S.c("tt"), 1: S.c("tt")}
    for t16 in range(16):
        bsel = t16 % 2
        dqc = S.dma("g", wo_c[bsel].all3(), colload_ap(ext["wqT"], t16),
                    waits=W(("tt", qrd[bsel])))
        for s4 in range(4):
            pb = s4 % 2
            tq = 0
            for c in range(16):
                tq = S.on("t", lambda e, c=c, s4=s4, bsel=bsel, pb=pb:
                          e.matmul(P(pb), wo_c[bsel].ch(c),
                                   xn0.ch(c, s4 * 512, s4 * 512 + 512),
                                   start=(c == 0), stop=(c == 15)),
                          waits=W(("dg", dqc), ("ss", qev[pb])), inc="tt")
            qev[pb] = S.on("s", lambda e, t16=t16, s4=s4, pb=pb: e.activation(
                qA.ch(t16, s4 * 512, s4 * 512 + 512), P(pb), AF.Relu,
                bias=B.bq_col[:, t16:t16 + 1]),
                waits=W(("tt", tq)), inc="ss")
            qrd[bsel] = tq
    q_done = S.c("ss")

    S.on("g", lambda e: e.nop(), waits=W(("c1", 1)))
    dk16 = 0
    for H in range(16):
        r, hp = H // 8, H % 8
        dk16 = S.dma("g", B.kv16[:, H * 128:(H + 1) * 128],
                     ext["kvg"][r * 1032 + hp * 128: r * 1032 + (hp + 1) * 128, :],
                     waits=W(("c1", 1)))
    dks = 0
    for r in range(2):
        dks = S.dma("g", B.ksum_sb[0:1, r * 1024:(r + 1) * 1024],
                    ext["kvg"][r * 1032 + 1024: r * 1032 + 1032, :].rearrange("h e -> (h e)")[None, :],
                    waits=W(("c1", 1), ("tt", S.c("tt"))))
    kc_t = 0
    for H in range(16):
        kc_t = S.on("t", lambda e, H=H: e.matmul(
            PS[:, 6 * 512 + H:6 * 512 + H + 1],
            B.ksum_sb[0:1, H * 128:(H + 1) * 128], B.ones_k1[0:1, 0:1],
            start=(H == 0), stop=(H == 15), skip_group_check=True),
            waits=W(("dg", dks), ("ss", q_done)), inc="tt")
    vkc = S.on("v", lambda e: e.tensor_copy(B.sqt[0][:, 0:16], PS[:, 6 * 512:6 * 512 + 16]),
               waits=W(("tt", kc_t)), inc="vv")
    kcols = B.sqt[0][:, 0:16]

    rd = {b: S.c("vv") for b in range(8)}
    for H in range(16):
        for s4 in range(4):
            ub = 6 + (s4 % 2)
            tu4 = S.on("t", lambda e, H=H, s4=s4, ub=ub: e.matmul(
                PR(ub, 512), kcols[:, H:H + 1],
                qA.ch(H, s4 * 512, s4 * 512 + 512),
                start=True, stop=True), waits=W(("vv", max(vkc, rd[ub]))), inc="tt")
            rd[ub] = S.on("v", lambda e, s4=s4, ub=ub: e.tensor_copy(
                r_u[0:1, s4 * 512:s4 * 512 + 512], PR(ub, 512)),
                waits=W(("tt", tu4)), inc="vv")
        v5 = S.on("v", lambda e: e.tensor_mul(r_zb, r_u, B.drow),
                  waits=W(("ss", sdr), ("vv", S.c("vv"))), inc="vv")
        s5 = S.on("s", lambda e: e.activation(r_zb, r_zb, AF.Copy, bias=EPS_Z),
                  waits=W(("vv", v5)), inc="ss")
        v6a = S.on("v", lambda e: e.reciprocal(r_zb, r_zb),
                   waits=W(("ss", s5)), inc="vv")
        v6 = S.on("v", lambda e: e.tensor_mul(r_u, r_zb, B.drow), inc="vv")
        for s4 in range(4):
            pb = s4 % 2
            to = S.on("t", lambda e, H=H, s4=s4, pb=pb: e.matmul(
                P(pb), B.kv16[:, H * 128:(H + 1) * 128],
                qA.ch(H, s4 * 512, s4 * 512 + 512), start=True, stop=True),
                waits=W(("vv", max(v6, rd[pb])), ("dg", dk16)), inc="tt")
            tz = S.on("t", lambda e, s4=s4, pb=pb: e.matmul(
                P(2 + pb), B.ones_k1f, r_u[0:1, s4 * 512:s4 * 512 + 512],
                start=True, stop=True), waits=W(("vv", rd[2 + pb])), inc="tt")
            vz = S.on("v", lambda e, pb=pb: e.tensor_copy(B.mb, P(2 + pb)),
                      waits=W(("tt", tz)), inc="vv")
            rd[2 + pb] = vz
            rd[pb] = S.on("v", lambda e, H=H, s4=s4, pb=pb: e.tensor_mul(
                atA.ch(H, s4 * 512, s4 * 512 + 512), P(pb), B.mb),
                waits=W(("tt", to), ("vv", vz)), inc="vv")
    at_done = S.c("vv")

    swv = {0: 0, 1: 0}
    ord_ = {0: at_done, 1: at_done}
    vb_p = 0
    dx1_p = 0
    dxn_p = 0
    tsq_p = S.c("tt")
    for s4 in range(4):
        x1v = [0] * 16
        for t16 in range(16):
            bsel = t16 % 2
            dwo = S.dma("g", wo_c[bsel].all3(), colload_ap(ext["woT"], t16),
                        waits=W(("tt", ord_[bsel])))
            pb = t16 % 2
            tw = 0
            for c in range(16):
                tw = S.on("t", lambda e, c=c, s4=s4, bsel=bsel, pb=pb:
                          e.matmul(P(pb), wo_c[bsel].ch(c),
                                   atA.ch(c, s4 * 512, s4 * 512 + 512),
                                   start=(c == 0), stop=(c == 15)),
                          waits=W(("dg", dwo), ("vv", max(at_done, swv[pb]))),
                          inc="tt")
            ord_[bsel] = tw
            sw = S.on("v", lambda e, t16=t16, pb=pb: e.tensor_scalar(
                out=B.tA, in0=P(pb), scalar1=B.bo_col[:, t16:t16 + 1],
                scalar2=None, op0=ALU.add),
                waits=W(("tt", tw), ("vv", vb_p), ("dq", dx1_p)), inc="vv")
            swv[pb] = sw
            df0 = S.dma("g", B.tD, ext["f0s"][t16 * 128:(t16 + 1) * 128,
                                              s4 * 512:s4 * 512 + 512],
                        waits=W(("vv", S.c("vv")), ("dq", S.c("dq"))))
            dx0 = S.dma("q", B.tB, ext["x0T"][t16 * 128:(t16 + 1) * 128,
                                              s4 * 512:s4 * 512 + 512],
                        waits=W(("vv", max(vb_p, at_done))))
            va = S.on("v", lambda e: e.tensor_add(B.tC, B.tA, B.tD),
                      waits=W(("dg", df0)), inc="vv")
            vb = S.on("v", lambda e: e.tensor_add(B.tA, B.tC, B.tB),
                      waits=W(("dq", dx0)), inc="vv")
            vb_p = vb
            dx1 = S.dma("q", ext["x1T"][t16 * 128:(t16 + 1) * 128, s4 * 512:s4 * 512 + 512],
                        B.tA, waits=W(("vv", vb)))
            dx1_p = dx1
            x1v[t16] = S.on("s", lambda e, t16=t16: e.activation(
                x1colb[:, t16 * 512:(t16 + 1) * 512], B.tA, AF.Copy),
                waits=W(("vv", vb), ("tt", tsq_p)), inc="ss")
        tcs = tsq = 0
        for t16 in range(16):
            tcs = S.on("t", lambda e, t16=t16: e.matmul(
                PR(2), B.ones_m1, x1colb[:, t16 * 512:(t16 + 1) * 512],
                start=(t16 == 0), stop=(t16 == 15)),
                waits=W(("ss", x1v[t16])), inc="tt")
        for t16 in range(16):
            sq2 = S.on("s", lambda e, t16=t16: e.activation(
                B.sqt[t16 % 2], x1colb[:, t16 * 512:(t16 + 1) * 512], AF.Square),
                waits=W(("tt", tsq)), inc="ss")
            tsq = S.on("t", lambda e, t16=t16: e.matmul(
                PR(3), B.ones_m1, B.sqt[t16 % 2], start=(t16 == 0), stop=(t16 == 15)),
                waits=W(("ss", sq2)), inc="tt")
        tsq_p = tsq
        sm1 = S.on("s", lambda e: e.activation(r_mean, PR(2), AF.Copy, scale=1.0 / D),
                   waits=W(("tt", tcs)), inc="ss")
        sm2 = S.on("s", lambda e: e.activation(r_msq, PR(3), AF.Copy, scale=1.0 / D),
                   waits=W(("tt", tsq)), inc="ss")
        v1 = S.on("v", lambda e: e.tensor_mul(r_t1, r_mean, r_mean),
                  waits=W(("ss", sm2)), inc="vv")
        v2 = S.on("v", lambda e: e.tensor_sub(r_t1, r_msq, r_t1), inc="vv")
        S.on("s", lambda e: e.activation(r_t1, r_t1, AF.Ln, bias=EPS_LN),
             waits=W(("vv", v2)), inc="ss")
        sv2 = S.on("s", lambda e: e.activation(r_rstd, r_t1, AF.Exp, scale=-0.5), inc="ss")
        tb1 = S.on("t", lambda e: e.matmul(P(4), B.ones_k1f, r_mean, start=True, stop=True),
                   waits=W(("ss", sv2)), inc="tt")
        tb2 = S.on("t", lambda e: e.matmul(P(5), B.ones_k1f, r_rstd, start=True, stop=True),
                   inc="tt")
        vb1 = S.on("v", lambda e: e.tensor_copy(B.mb, P(4)), waits=W(("tt", tb1)), inc="vv")
        vb2 = S.on("v", lambda e: e.tensor_copy(B.rb, P(5)), waits=W(("tt", tb2)), inc="vv")
        for t16 in range(16):
            S.on("v", lambda e, t16=t16: e.tensor_sub(
                B.tmpf[:, 0:512], x1colb[:, t16 * 512:(t16 + 1) * 512], B.mb),
                waits=W(("vv", vb2)), inc="vv")
            vx = S.on("v", lambda e: e.tensor_mul(B.tE, B.tmpf[:, 0:512], B.rb),
                      waits=W(("dq", dxn_p)), inc="vv")
            dxn_p = S.dma("q", ext["xn1T"][t16 * 128:(t16 + 1) * 128,
                                           s4 * 512:s4 * 512 + 512],
                          B.tE, waits=W(("vv", vx)))
    S.on("q", lambda e: e.nop(), waits=W(("dq", S.c("dq")), ("dg", S.c("dg"))))



def build_l1(S, B, ext, PS):
    nc = S.nc
    P = lambda b, n=1: PS[:, b * 512:(b + n) * 512]
    PR = lambda b, w=512: PS[0:1, b * 512:b * 512 + w]
    W = lambda *a: list(a)

    AR1b, AR1f = bf(B.AR1), f32(B.AR1)
    AR3b = bf(B.AR3)
    STb = bf(B.ST)

    wstage = AR1f[:, 0:8192]
    qtile = AR1f[:, 8192:16384]
    swt = AR3b[:, 0:8192]
    m01t = AR3b[:, 8192:16384]
    whatt = AR3b[:, 16384:24576]
    xg = FM(AR1b, NC, 2048)
    Qh = AR3b[:, 0:4096]
    Kh = AR3b[:, 4096:8192]
    Vth = AR3b[:, 8192:12288]
    Vtok = AR3b[:, 12288:16384]
    q2t = AR3b[:, 16384:20480]
    Psl = [AR3b[:, 20480 + i * 512:20480 + (i + 1) * 512] for i in range(4)]
    wqc = FM(STb[:, 0:2048], NC, 128)
    wkc = FM(STb[:, 2048:4096], NC, 128)
    wvc = FM(STb[:, 4096:6144], NC, 128)
    PB7 = PS[:, 7 * 512:8 * 512].bitcast(BF16)
    harena = FM(AR1b, NH1, 512)
    xnf = FM(AR3b[:, 0:8192], NC, 512)
    attE = FM(AR3b[:, 8192:16384], NC, 512)
    wpjh = [FM(AR3b[:, 16384 + j * 4096:16384 + (j + 1) * 4096], 32, 128) for j in (0, 1)]
    xslab4 = FM(STb[:, 0:8192], NC, 512)
    wfc_col = [FM(STb[:, 8192 + i * 2048:8192 + (i + 1) * 2048], NC, 128) for i in (0, 1)]
    wo_c4 = [FM(STb[:, 12288 + i * 2048:12288 + (i + 1) * 2048], NC, 128) for i in (0, 1)]
    idxE = B.mstrip.bitcast(I32)[:, 0:64]

    ROWS = B.ROWS
    r_mean, r_msq, r_t1, r_rstd = (ROWS[0:1, i * 512:(i + 1) * 512] for i in range(4))
    r_sum = ROWS[0:1, 2048:2560]

    b1qc = B.decay_col[:, 0:8]

    S.fence()
    dc0 = S.dma("q", ext["xn1c"][:], ext["xn1T"][:], waits=W(("dq", S.c("dq"))))
    c2 = S.on("g", lambda e: e.collective_compute(
        "AllGather", ALU.bypass, replica_groups=GROUPS,
        ins=[ext["xn1c"][:]], outs=[ext["xn1g"][:]]),
        waits=W(("dq", dc0)), inc="c2")

    db1 = S.dma("g", B.b1q, ext["b1q"])
    S.dma("g", B.b1k, ext["b1k"])
    S.dma("g", B.b1v, ext["b1v"])
    dbo = S.dma("g", B.bo1_col, ext["bo1_col"])
    sb1 = S.on("s", lambda e: e.activation(b1qc, B.b1q, AF.Copy, scale=CQK),
               waits=W(("dg", db1)), inc="ss")

    S.on("v", lambda e: e.memset(B.acc, 0.0), inc="vv")
    gam_cols = {}
    for m, (wT, nch, F) in enumerate((("wfcT", 16, 8192), ("wpjT", 64, 2048))):
        vlast = 0
        for ch in range(nch):
            half = ch % 2
            dw = S.dma("q", wstage[:, 0:F],
                       ext[wT][ch * 128:(ch + 1) * 128, :],
                       waits=W(("ss", S.c("ss")), ("vv", S.c("vv"))))
            sa = S.on("s", lambda e, F=F: e.activation(swt[:, 0:F], wstage[:, 0:F], AF.Abs,
                                                       accum_out=B.gcol[:, 3:4]),
                      waits=W(("dq", dw)), inc="ss")
            vlast = S.on("v", lambda e, m=m: e.tensor_add(B.acc[:, m:m + 1], B.acc[:, m:m + 1],
                                                          B.gcol[:, 3:4]),
                         waits=W(("ss", sa)), inc="vv")
        tg1 = S.on("t", lambda e, m=m: e.matmul(PS[0:1, 6 * 512:6 * 512 + 1],
                                                B.acc[:, m:m + 1], B.ones_m1f,
                                                start=True, stop=True),
                   waits=W(("vv", vlast)), inc="tt")
        sg1 = S.on("s", lambda e, m=m: e.activation(B.r11[0:1, m:m + 1],
                                                    PS[0:1, 6 * 512:6 * 512 + 1],
                                                    AF.Copy, scale=1.0 / N_W_FC),
                   waits=W(("tt", tg1)), inc="ss")
        sg2 = S.on("s", lambda e, m=m: e.activation(B.r11[0:1, 2 + m:3 + m],
                                                    B.r11[0:1, m:m + 1], AF.Copy,
                                                    bias=EPS_B),
                   inc="ss")
        vg1 = S.on("v", lambda e, m=m: e.reciprocal(B.r11[0:1, 4 + m:5 + m],
                                                    B.r11[0:1, 2 + m:3 + m]),
                   waits=W(("ss", sg2)), inc="vv")
        tb = S.on("t", lambda e, m=m: e.matmul(PS[:, 6 * 512:6 * 512 + 1], B.ones_k1f,
                                               B.r11[0:1, 4 + m:5 + m], start=True, stop=True),
                  waits=W(("vv", vg1)), inc="tt")
        vb = S.on("v", lambda e, m=m: e.tensor_copy(B.gcol[:, 2 * m:2 * m + 1],
                                                    PS[:, 6 * 512:6 * 512 + 1]),
                  waits=W(("tt", tb)), inc="vv")
        tb2 = S.on("t", lambda e, m=m: e.matmul(PS[:, 6 * 512:6 * 512 + 1], B.ones_k1f,
                                                B.r11[0:1, m:m + 1], start=True, stop=True),
                   waits=W(("vv", vb)), inc="tt")
        vb2 = S.on("v", lambda e, m=m: e.tensor_copy(B.gcol[:, 2 * m + 1:2 * m + 2],
                                                     PS[:, 6 * 512:6 * 512 + 1]),
                   waits=W(("tt", tb2)), inc="vv")
        gam_cols[m] = vb2

    wst_p = 0
    for m, (wT, outT, nch, F) in enumerate((("wfcT", "whatfc", 16, 8192),
                                            ("wpjT", "whatpj", 64, 2048))):
        for ch in range(nch):
            dw = S.dma("q", wstage[:, 0:F], ext[wT][ch * 128:(ch + 1) * 128, :],
                       waits=W(("ss", S.c("ss")), ("vv", S.c("vv"))))
            sq_ = S.on("s", lambda e, m=m, F=F: e.activation(
                qtile[:, 0:F], wstage[:, 0:F], AF.Square, scale=B.gcol[:, 2 * m:2 * m + 1]),
                waits=W(("dq", dw), ("vv", gam_cols[1]), ("vv2" if False else "vv", S.c("vv"))),
                inc="ss")
            vm = S.on("v", lambda e, F=F: e.tensor_scalar(
                out=m01t[:, 0:F], in0=qtile[:, 0:F], scalar1=0.25, scalar2=None,
                op0=ALU.is_ge), waits=W(("ss", sq_)), inc="vv")
            ss_ = S.on("s", lambda e, F=F: e.activation(swt[:, 0:F], wstage[:, 0:F], AF.Sign),
                       inc="ss")
            vw = S.on("v", lambda e, m=m, F=F: e.scalar_tensor_tensor(
                out=whatt[:, 0:F], in0=swt[:, 0:F], scalar=B.gcol[:, 2 * m + 1:2 * m + 2],
                in1=m01t[:, 0:F], op0=ALU.mult, op1=ALU.mult),
                waits=W(("ss", ss_), ("vv", vm), ("dq", wst_p)), inc="vv")
            wst_p = S.dma("q", ext[outT][ch * 128:(ch + 1) * 128, :], whatt[:, 0:F],
                          waits=W(("vv", vw)))

    S.on("g", lambda e: e.nop(), waits=W(("c2", 1)))
    S.fence()
    head_tt = S.c("tt")
    head_ss = S.c("ss")
    head_vv = S.c("vv")
    for hp in range(8):
        dq_ = S.dma("g", wqc.all3(), colload_ap(ext["wq1T"], hp), waits=W(("tt", head_tt)))
        dk_ = S.dma("g", wkc.all3(), colload_ap(ext["wk1T"], hp))
        dv_ = S.dma("g", wvc.all3(), colload_ap(ext["wv1T"], hp))
        rdq = {0: head_vv, 1: head_vv, 2: head_vv}
        evq = S.c("vv")
        for nh in range(2):
            dxg = 0
            for c in range(16):
                dxg = S.dma("g", xg.ch(c),
                            ext["xn1g"][nh * 2048 + c * 128: nh * 2048 + (c + 1) * 128, :],
                            waits=W(("tt", S.c("tt"))))
            for s4 in range(4):
                col = nh * 2048 + s4 * 512
                tq = tk = tv = 0
                for c in range(16):
                    tq = S.on("t", lambda e, c=c, s4=s4: e.matmul(
                        P(0), wqc.ch(c), xg.ch(c, s4 * 512, s4 * 512 + 512),
                        start=(c == 0), stop=(c == 15)),
                        waits=W(("dg", max(dq_, dxg)), ("vv", rdq[0]),
                                ("ss", sb1)), inc="tt")
                rdq[0] = S.on("v", lambda e, hp=hp, col=col: e.tensor_scalar(
                    out=Qh[:, col:col + 512], in0=P(0), scalar1=CQK, scalar2=b1qc[:, hp:hp + 1],
                    op0=ALU.mult, op1=ALU.add), waits=W(("tt", tq)), inc="vv")
                for c in range(16):
                    tk = S.on("t", lambda e, c=c, s4=s4: e.matmul(
                        P(1), wkc.ch(c), xg.ch(c, s4 * 512, s4 * 512 + 512),
                        start=(c == 0), stop=(c == 15)),
                        waits=W(("dg", max(dk_, dxg)), ("vv", rdq[1])), inc="tt")
                rdq[1] = S.on("v", lambda e, hp=hp, col=col: e.tensor_scalar(
                    out=Kh[:, col:col + 512], in0=P(1), scalar1=B.b1k[:, hp:hp + 1],
                    scalar2=None, op0=ALU.add), waits=W(("tt", tk)), inc="vv")
                for c in range(16):
                    tv = S.on("t", lambda e, c=c, s4=s4: e.matmul(
                        P(2), wvc.ch(c), xg.ch(c, s4 * 512, s4 * 512 + 512),
                        start=(c == 0), stop=(c == 15)),
                        waits=W(("dg", max(dv_, dxg)), ("vv", rdq[2])), inc="tt")
                rdq[2] = S.on("v", lambda e, hp=hp, col=col: e.tensor_scalar(
                    out=Vth[:, col:col + 512], in0=P(2), scalar1=B.b1v[:, hp:hp + 1],
                    scalar2=None, op0=ALU.add), waits=W(("tt", tv)), inc="vv")
        proj_vv = S.c("vv")
        vtr = S.c("vv")
        for kt in range(32):
            sl = kt % 2
            ttr = S.on("t", lambda e, kt=kt, sl=sl: e.transpose(
                PB7[:, sl * 128:(sl + 1) * 128], Vth[:, kt * 128:(kt + 1) * 128], B.ident),
                waits=W(("vv", max(proj_vv, vtr))), inc="tt")
            vtr = S.on("v", lambda e, kt=kt, sl=sl: e.tensor_copy(
                Vtok[:, kt * 128:(kt + 1) * 128], PB7[:, sl * 128:(sl + 1) * 128]),
                waits=W(("tt", ttr)), inc="vv")
        sqq = S.on("s", lambda e: e.activation(q2t, Qh, AF.Square), waits=W(("vv", proj_vv)),
                   inc="ss")
        vi1 = S.on("v", lambda e: e.memset(B.r11[0:1, 0:2], 0.0), inc="vv")
        mxv = vi1
        for part, src_ in ((0, "q"), (1, "k")):
            if part == 1:
                sqq = S.on("s", lambda e: e.activation(q2t, Kh, AF.Square),
                           waits=W(("tt", S.c("tt"))), inc="ss")
            for s8 in range(8):
                tsum = S.on("t", lambda e, s8=s8: e.matmul(
                    PR(6), B.ones_m1, q2t[:, s8 * 512:(s8 + 1) * 512],
                    start=True, stop=True), waits=W(("ss", sqq), ("vv", mxv)), inc="tt")
                vr = S.on("v", lambda e: e.reduce_max(B.r11[0:1, 6:7], PR(6),
                                                      axis=mybir.AxisListType.X),
                          waits=W(("tt", tsum)), inc="vv")
                mxv = S.on("v", lambda e, part=part: e.tensor_max(
                    B.r11[0:1, part:part + 1], B.r11[0:1, part:part + 1], B.r11[0:1, 6:7]),
                    inc="vv")
        s_c1 = S.on("s", lambda e: e.activation(B.r11[0:1, 2:3], B.r11[0:1, 0:1], AF.Sqrt),
                    waits=W(("vv", mxv)), inc="ss")
        s_c2 = S.on("s", lambda e: e.activation(B.r11[0:1, 3:4], B.r11[0:1, 1:2], AF.Sqrt),
                    inc="ss")
        v_c = S.on("v", lambda e: e.tensor_mul(B.r11[0:1, 4:5], B.r11[0:1, 2:3],
                                               B.r11[0:1, 3:4]),
                   waits=W(("ss", s_c2)), inc="vv")
        t_c = S.on("t", lambda e: e.matmul(PS[:, 6 * 512:6 * 512 + 1], B.ones_k1f,
                                           B.r11[0:1, 4:5], start=True, stop=True),
                   waits=W(("vv", v_c)), inc="tt")
        s_c3 = S.on("s", lambda e: e.activation(B.negC, PS[:, 6 * 512:6 * 512 + 1],
                                                AF.Copy, scale=-1.0),
                    waits=W(("tt", t_c)), inc="ss")
        pread = {i: ("vv", S.c("vv")) for i in range(4)}
        pslot_rd = {i: S.c("tt") for i in range(4)}
        av_rd = S.c("vv")
        zb_rd = S.c("vv")
        tE_rd = S.c("vv")
        tD_dma = S.c("dq")
        for s8 in range(8):
            ktmax = 4 * (s8 + 1)
            tsum = tav = 0
            for kt in range(ktmax):
                sl = kt % 4
                tS = S.on("t", lambda e, kt=kt, s8=s8, sl=sl: e.matmul(
                    P(sl), Kh[:, kt * 128:(kt + 1) * 128],
                    Qh[:, s8 * 512:(s8 + 1) * 512], start=True, stop=True),
                    waits=W(pread[sl], ("ss", s_c3)), inc="tt")
                boundary = kt >= 4 * s8
                if not boundary:
                    sE = S.on("s", lambda e, sl=sl: e.activation(
                        Psl[sl], P(sl), AF.Exp, bias=B.negC),
                        waits=W(("tt", max(tS, pslot_rd[sl]))), inc="ss")
                    pr = ("ss", sE)
                    pread[sl] = ("ss", sE)
                else:
                    off = kt * 128 - s8 * 512
                    sE = S.on("s", lambda e, sl=sl: e.activation(
                        B.tE, P(sl), AF.Exp, bias=B.negC),
                        waits=W(("tt", tS), ("vv", tE_rd)), inc="ss")
                    pread[sl] = ("ss", sE)
                    vE = S.on("v", lambda e, sl=sl, off=off: e.tensor_mul(
                        Psl[sl], B.tE, B.mstrip[:, 384 - off:384 - off + 512]),
                        waits=W(("ss", sE), ("tt", pslot_rd[sl])), inc="vv")
                    tE_rd = vE
                    pr = ("vv", vE)
                tsum = S.on("t", lambda e, sl=sl, kt=kt, ktmax=ktmax: e.matmul(
                    PR(4), B.ones_m1, Psl[sl], start=(kt == 0), stop=(kt == ktmax - 1)),
                    waits=W(pr, ("vv", zb_rd)), inc="tt")
                tav = S.on("t", lambda e, sl=sl, kt=kt, ktmax=ktmax: e.matmul(
                    P(5), Vtok[:, kt * 128:(kt + 1) * 128], Psl[sl],
                    start=(kt == 0), stop=(kt == ktmax - 1)),
                    waits=W(("vv", av_rd)), inc="tt")
                pslot_rd[sl] = tav
            vz1 = S.on("v", lambda e: e.tensor_copy(r_sum, PR(4)), waits=W(("tt", tsum)),
                       inc="vv")
            zb_rd = vz1
            vz2 = S.on("v", lambda e: e.reciprocal(r_sum, r_sum), inc="vv")
            tzb = S.on("t", lambda e: e.matmul(P(6), B.ones_k1f, r_sum, start=True, stop=True),
                       waits=W(("vv", vz2)), inc="tt")
            vz3 = S.on("v", lambda e: e.tensor_copy(B.mb, P(6)), waits=W(("tt", tzb)),
                       inc="vv")
            av_rd = S.on("v", lambda e: e.tensor_mul(B.tD, P(5), B.mb),
                         waits=W(("tt", tav), ("vv", vz3), ("dq", tD_dma)), inc="vv")
            tD_dma = S.dma("q", ext["attc"][hp * 128:(hp + 1) * 128,
                                            s8 * 512:(s8 + 1) * 512],
                           B.tD, waits=W(("vv", av_rd)))
        head_tt = S.c("tt")
        head_vv = S.c("vv")
    c3 = S.on("g", lambda e: e.collective_compute(
        "AllGather", ALU.bypass, replica_groups=GROUPS,
        ins=[ext["attc"][:]], outs=[ext["attg"][:]]),
        waits=W(("dq", S.c("dq"))), inc="c3")

    S.fence()
    dix = S.dma("g", idxE, ext["idxE"], waits=W(("ss", S.c("ss")), ("vv", S.c("vv"))))
    swv = {4: S.c("ss"), 5: S.c("ss")}
    frd = {0: S.c("tt"), 1: S.c("tt")}
    prd = {0: S.c("tt"), 1: S.c("tt")}
    ord4 = {0: S.c("tt"), 1: S.c("tt")}
    jrd = {6: S.c("vv"), 7: S.c("vv")}
    vb_p = 0
    dx2_p = 0
    tsq_p = S.c("tt")
    hrd = S.c("tt")
    for sl in range(4):
        dxs = S.dma("q", xslab4.all3(), slab_ap(ext["xn1T"], sl * 512, sl * 512 + 512),
                    waits=W(("tt", S.c("tt")), ("ss", S.c("ss")), ("vv", S.c("vv"))))
        tcs = tsq = 0
        for c in range(16):
            tcs = S.on("t", lambda e, c=c: e.matmul(PR(0), B.ones_m1, xslab4.ch(c),
                                                    start=(c == 0), stop=(c == 15)),
                       waits=W(("dq", dxs)), inc="tt")
        for c in range(16):
            sqv = S.on("s", lambda e, c=c: e.activation(B.sqt[c % 2], xslab4.ch(c), AF.Square),
                       waits=W(("dq", dxs), ("tt", tsq)), inc="ss")
            tsq = S.on("t", lambda e, c=c: e.matmul(PR(1), B.ones_m1, B.sqt[c % 2],
                                                    start=(c == 0), stop=(c == 15)),
                       waits=W(("ss", sqv)), inc="tt")
        sm1 = S.on("s", lambda e: e.activation(r_mean, PR(0), AF.Copy, scale=1.0 / D),
                   waits=W(("tt", tcs)), inc="ss")
        sm2 = S.on("s", lambda e: e.activation(r_msq, PR(1), AF.Copy, scale=1.0 / D),
                   waits=W(("tt", tsq)), inc="ss")
        v1 = S.on("v", lambda e: e.tensor_mul(r_t1, r_mean, r_mean), waits=W(("ss", sm2)),
                  inc="vv")
        v2 = S.on("v", lambda e: e.tensor_sub(r_t1, r_msq, r_t1), inc="vv")
        S.on("s", lambda e: e.activation(r_t1, r_t1, AF.Ln, bias=EPS_B),
             waits=W(("vv", v2)), inc="ss")
        sv2 = S.on("s", lambda e: e.activation(r_rstd, r_t1, AF.Exp, scale=-0.5), inc="ss")
        tb1 = S.on("t", lambda e: e.matmul(P(2), B.ones_k1f, r_mean, start=True, stop=True),
                   waits=W(("ss", sm1), ("vv", S.c("vv"))), inc="tt")
        tb2 = S.on("t", lambda e: e.matmul(P(3), B.ones_k1f, r_rstd, start=True, stop=True),
                   waits=W(("ss", sv2)), inc="tt")
        vb1 = S.on("v", lambda e: e.tensor_copy(B.mb, P(2)), waits=W(("tt", tb1)), inc="vv")
        vb2 = S.on("v", lambda e: e.tensor_copy(B.rb, P(3)), waits=W(("tt", tb2)), inc="vv")
        vxn = 0
        for c in range(16):
            S.on("v", lambda e, c=c: e.tensor_sub(B.tmpf[:, 0:512], xslab4.ch(c), B.mb),
                 waits=W(("vv", vb2)), inc="vv")
            vxn = S.on("v", lambda e, c=c: e.tensor_mul(xnf.ch(c), B.tmpf[:, 0:512], B.rb),
                       inc="vv")
        for ot in range(NH1):
            bsel = ot % 2
            pb = 4 + ot % 2
            dwf = S.dma("g", wfc_col[bsel].all3(), colload_ap(ext["whatfc"], ot),
                        waits=W(("tt", frd[bsel])))
            th = 0
            for c in range(16):
                th = S.on("t", lambda e, c=c, bsel=bsel, pb=pb: e.matmul(
                    P(pb), wfc_col[bsel].ch(c), xnf.ch(c),
                    start=(c == 0), stop=(c == 15)),
                    waits=W(("dg", dwf), ("vv", vxn), ("ss", swv[pb])), inc="tt")
            frd[bsel] = th
            swv[pb] = S.on("s", lambda e, ot=ot, pb=pb: e.activation(
                harena.ch(ot), P(pb), AF.Gelu), waits=W(("tt", th)), inc="ss")
        tcs = tsq = 0
        for ot in range(NH1):
            tcs = S.on("t", lambda e, ot=ot: e.matmul(PR(0), B.ones_m1, harena.ch(ot),
                                                      start=(ot == 0), stop=(ot == NH1 - 1)),
                       waits=W(("ss", swv[4] if ot == 0 else 0), ("ss", swv[5])), inc="tt")
        for ot in range(NH1):
            sqv = S.on("s", lambda e, ot=ot: e.activation(B.sqt[ot % 2], harena.ch(ot),
                                                          AF.Square),
                       waits=W(("tt", tsq)), inc="ss")
            tsq = S.on("t", lambda e, ot=ot: e.matmul(PR(1), B.ones_m1, B.sqt[ot % 2],
                                                      start=(ot == 0), stop=(ot == NH1 - 1)),
                       waits=W(("ss", sqv)), inc="tt")
        sm1 = S.on("s", lambda e: e.activation(r_mean, PR(0), AF.Copy, scale=1.0 / HID1),
                   waits=W(("tt", tcs)), inc="ss")
        sm2 = S.on("s", lambda e: e.activation(r_msq, PR(1), AF.Copy, scale=1.0 / HID1),
                   waits=W(("tt", tsq)), inc="ss")
        v1 = S.on("v", lambda e: e.tensor_mul(r_t1, r_mean, r_mean), waits=W(("ss", sm2)),
                  inc="vv")
        v2 = S.on("v", lambda e: e.tensor_sub(r_t1, r_msq, r_t1), inc="vv")
        S.on("s", lambda e: e.activation(r_t1, r_t1, AF.Ln, bias=EPS_B),
             waits=W(("vv", v2)), inc="ss")
        sv2 = S.on("s", lambda e: e.activation(r_rstd, r_t1, AF.Exp, scale=-0.5), inc="ss")
        tb1 = S.on("t", lambda e: e.matmul(P(2), B.ones_k1f, r_mean, start=True, stop=True),
                   waits=W(("ss", sm1), ("vv", S.c("vv"))), inc="tt")
        tb2 = S.on("t", lambda e: e.matmul(P(3), B.ones_k1f, r_rstd, start=True, stop=True),
                   waits=W(("ss", sv2)), inc="tt")
        vb1 = S.on("v", lambda e: e.tensor_copy(B.mb, P(2)), waits=W(("tt", tb1)), inc="vv")
        vb2 = S.on("v", lambda e: e.tensor_copy(B.rb, P(3)), waits=W(("tt", tb2)), inc="vv")
        hnorm = 0
        for ot in range(NH1):
            S.on("v", lambda e, ot=ot: e.tensor_sub(B.tmpf[:, 0:512], harena.ch(ot), B.mb),
                 waits=W(("vv", vb2)), inc="vv")
            hnorm = S.on("v", lambda e, ot=ot: e.tensor_mul(harena.ch(ot), B.tmpf[:, 0:512],
                                                            B.rb), inc="vv")
        S.on("g", lambda e: e.nop(), waits=W(("c3", 1)))
        dga = 0
        attg_v = ext["attg"].rearrange("e (b n) -> (e b) n", n=512)
        for c in range(16):
            dga = S.on("g", lambda e, c=c, sl=sl: e.indirect_dma_start(
                out=attE.ch(c), out_offset=None, in_=attg_v,
                in_offset=bass.IndirectOffsetOnAxis(idxE[:, c * 4 + sl:c * 4 + sl + 1], 0)),
                waits=W(("dg", dix), ("tt", hrd)), inc="dg", incn=16)
        for dt in range(16):
            pb = 6 + dt % 2
            hsel = dt % 2
            dwpA = S.dma("g", wpjh[0].all3(), colload_half(ext["whatpj"], dt, 0),
                         waits=W(("tt", prd[0])))
            dwo = S.dma("g", wo_c4[hsel].all3(), colload_ap(ext["wo1T"], dt),
                        waits=W(("tt", ord4[hsel])))
            dwpB = S.dma("g", wpjh[1].all3(), colload_half(ext["whatpj"], dt, 1),
                         waits=W(("tt", prd[1])))
            tj = 0
            for c in range(16):
                tj = S.on("t", lambda e, c=c, pb=pb, hsel=hsel: e.matmul(
                    P(pb), wo_c4[hsel].ch(c), attE.ch(c), start=(c == 0), stop=False),
                    waits=W(("dg", max(dwo, dga)), ("vv", jrd[pb])), inc="tt")
            ord4[hsel] = tj
            for hc in range(NH1):
                tj = S.on("t", lambda e, hc=hc, pb=pb: e.matmul(
                    P(pb), wpjh[hc // 32].ch(hc % 32), harena.ch(hc),
                    start=False, stop=(hc == NH1 - 1)),
                    waits=W(("dg", dwpA if hc < 32 else dwpB), ("vv", hnorm)), inc="tt")
                if hc == 31:
                    prd[0] = tj
            prd[1] = tj
            hrd = tj
            vo = S.on("v", lambda e, dt=dt, pb=pb: e.tensor_scalar(
                out=B.tA, in0=P(pb), scalar1=B.bo1_col[:, dt:dt + 1], scalar2=None,
                op0=ALU.add), waits=W(("tt", tj), ("dg", dbo), ("vv", vb_p),
                                      ("dq", dx2_p)), inc="vv")
            jrd[pb] = vo
            dx1_ = S.dma("q", B.tB, ext["x1T"][dt * 128:(dt + 1) * 128,
                                               sl * 512:sl * 512 + 512],
                         waits=W(("vv", vb_p)))
            vb_p = S.on("v", lambda e: e.tensor_add(B.tC, B.tA, B.tB),
                        waits=W(("dq", dx1_)), inc="vv")
            dx2_p = S.dma("q", ext["x2T"][dt * 128:(dt + 1) * 128, sl * 512:sl * 512 + 512],
                          B.tC.bitcast(F32) if False else B.tC, waits=W(("vv", vb_p)))
    if ext.get("DEBUG"):
        S.dma("g", ext["dbg_gam"][:, 0:4], B.gcol,
              waits=W(("vv", S.c("vv")), ("ss", S.c("ss"))))
        S.dma("g", ext["dbg_gam"][0:1, 4:8], B.r11[0:1, 0:4])
        S.dma("g", ext["dbg_wf"], ext["whatfc"][0:128, :])
        S.dma("g", ext["dbg_attc"], ext["attc"][:, :])
        S.dma("g", ext["dbg_q"], Qh)
        S.dma("g", ext["dbg_h"], harena.ap[:, 0:NH1 * 512])
        S.dma("g", ext["dbg_xnf"], xnf.ap[:, 0:NC * 512])
        S.dma("g", ext["dbg_atE"], attE.ap[:, 0:NC * 512])
    S.on("q", lambda e: e.nop(), waits=W(("dq", S.c("dq")), ("dg", S.c("dg"))))


def _l1_inmaps(inputs, l0_results):
    C = np.ascontiguousarray
    wqT = C(np.asarray(inputs["L1_attn_wq"]).T)
    wkT = C(np.asarray(inputs["L1_attn_wk"]).T)
    wvT = C(np.asarray(inputs["L1_attn_wv"]).T)
    woT = C(np.asarray(inputs["L1_attn_wo"]).T)
    bq = np.asarray(inputs["L1_attn_bq"])
    bk = np.asarray(inputs["L1_attn_bk"])
    bv = np.asarray(inputs["L1_attn_bv"])
    bo = np.asarray(inputs["L1_attn_bo"])
    wfcT = C(np.asarray(inputs["L1_ffn_w_fc"]).T)
    wpjT = C(np.asarray(inputs["L1_ffn_w_proj"]).T)
    col = lambda v, n: C(v.reshape(n, 128).T.astype(np.float32))
    maps = []
    for core in range(8):
        h = core % 2
        hs = h * 1024
        p = np.arange(128)[:, None]
        cs = np.arange(64)[None, :]
        idxE = ((cs // 4) * 128 + p) * 8 + h * 4 + (cs % 4)
        maps.append({
            **({"xn1T": np.asarray(l0_results[core]["xn1T"]),
                "x1T": np.asarray(l0_results[core]["x1T"])} if l0_results is not None else {}),
            "wq1T": C(wqT[:, hs:hs + 1024]), "b1q": col(bq[hs:hs + 1024], 8),
            "wk1T": C(wkT[:, hs:hs + 1024]), "b1k": col(bk[hs:hs + 1024], 8),
            "wv1T": C(wvT[:, hs:hs + 1024]), "b1v": col(bv[hs:hs + 1024], 8),
            "wo1T": woT, "bo1_col": col(bo, 16),
            "wfcT": wfcT, "wpjT": wpjT,
            "idxE": idxE.astype(np.int32),
        })
    return maps


def _build_l1_nc():
    nc = bass.Bass("TRN2", target_bir_lowering=False, debug=False)
    ext = {}
    EI = lambda n, s, d=F32: nc.dram_tensor(n, s, d, kind="ExternalInput").ap()
    ext["xn1T"] = EI("xn1T", [D, NOWN], BF16)
    ext["x1T"] = EI("x1T", [D, NOWN])
    ext["wq1T"] = EI("wq1T", [D, 1024])
    ext["b1q"] = EI("b1q", [128, 8])
    ext["wk1T"] = EI("wk1T", [D, 1024])
    ext["b1k"] = EI("b1k", [128, 8])
    ext["wv1T"] = EI("wv1T", [D, 1024])
    ext["b1v"] = EI("b1v", [128, 8])
    ext["wo1T"] = EI("wo1T", [D, D])
    ext["bo1_col"] = EI("bo1_col", [128, 16])
    ext["wfcT"] = EI("wfcT", [D, HID1])
    ext["wpjT"] = EI("wpjT", [HID1, D])
    ext["idxE"] = EI("idxE", [128, 64], I32)
    ext["x2T"] = nc.dram_tensor("x2T", [D, NOWN], F32, kind="ExternalOutput").ap()
    ext["DEBUG"] = DEBUG_L1
    if DEBUG_L1:
        ext["dbg_gam"] = nc.dram_tensor("dbg_gam", [128, 8], F32, kind="ExternalOutput").ap()
        ext["dbg_wf"] = nc.dram_tensor("dbg_wf", [128, HID1], F32, kind="ExternalOutput").ap()
        ext["dbg_attc"] = nc.dram_tensor("dbg_attc", [1024, NFULL], F32, kind="ExternalOutput").ap()
        ext["dbg_q"] = nc.dram_tensor("dbg_q", [128, NFULL], F32, kind="ExternalOutput").ap()
        ext["dbg_h"] = nc.dram_tensor("dbg_h", [128, NH1 * 512], F32, kind="ExternalOutput").ap()
        ext["dbg_xnf"] = nc.dram_tensor("dbg_xnf", [128, NC * 512], F32, kind="ExternalOutput").ap()
        ext["dbg_atE"] = nc.dram_tensor("dbg_atE", [128, NC * 512], F32, kind="ExternalOutput").ap()
    ext["xn1c"] = nc.dram_tensor("xn1c", [D, NOWN], BF16, kind="Internal").ap()
    ext["xn1g"] = nc.dram_tensor("xn1g", [2 * D, NOWN], BF16, kind="Internal").ap()
    ext["attc"] = nc.dram_tensor("attc", [1024, NFULL], BF16, kind="Internal").ap()
    ext["attg"] = nc.dram_tensor("attg", [2048, NFULL], BF16, kind="Internal").ap()
    ext["whatfc"] = nc.dram_tensor("whatfc", [D, HID1], BF16, kind="Internal").ap()
    ext["whatpj"] = nc.dram_tensor("whatpj", [HID1, D], BF16, kind="Internal").ap()

    B = Buf(nc)
    with nc.psum_tensor("PS", [128, 4096], F32) as PS:
        S = Sched(nc)
        build_common(S, B, nc)
        build_l1(S, B, ext, PS[:])
        S.finalize()
    return nc

def _l0_inmaps(inputs):
    x = np.asarray(inputs["L0_attn_x"])
    C = np.ascontiguousarray
    wqT = C(np.asarray(inputs["L0_attn_Wq"]).T)
    wkT = C(np.asarray(inputs["L0_attn_Wk"]).T)
    wvT = C(np.asarray(inputs["L0_attn_Wv"]).T)
    woT = C(np.asarray(inputs["L0_attn_Wo"]).T)
    bq = np.asarray(inputs["L0_attn_bq"])
    bk = np.asarray(inputs["L0_attn_bk"])
    bv = np.asarray(inputs["L0_attn_bv"])
    bo = np.asarray(inputs["L0_attn_bo"])
    wqgT = C(np.asarray(inputs["L0_ffn_wq_gate"]).T)
    wquT = C(np.asarray(inputs["L0_ffn_wq_up"]).T)
    wqdT = C(np.asarray(inputs["L0_ffn_wq_down"]).T)
    sg = np.asarray(inputs["L0_ffn_s_gate"])
    su = np.asarray(inputs["L0_ffn_s_up"])
    sd = np.asarray(inputs["L0_ffn_s_down"])
    col = lambda v, n: C(v.reshape(n, 128).T.astype(np.float32))
    maps = []
    for core in range(8):
        b, h = core // 2, core % 2
        xT = x[b].T
        own = xT[:, h * 2048:(h + 1) * 2048]
        oth = xT[:, (1 - h) * 2048:(2 - h) * 2048]
        hs = h * 1024
        maps.append({
            "x0T": C(np.concatenate([own, oth], axis=1)),
            "pos2": np.array([[h * 2048.0, (1 - h) * 2048.0]], np.float32),
            "wqT": wqT, "bq_col": col(bq, 16),
            "wkT_oh": C(wkT[:, hs:hs + 1024]), "bk_row": C(bk[None, hs:hs + 1024]),
            "wvT_oh": C(wvT[:, hs:hs + 1024]), "bv_row": C(bv[None, hs:hs + 1024]),
            "woT": woT, "bo_col": col(bo, 16),
            "wqgT": wqgT, "sg_col": col(sg, 44),
            "wquT": wquT, "su_col": col(su, 44),
            "wqdT": wqdT, "sd_col": col(sd, 16),
        })
    return maps


def _build_l0_nc():
    nc = bass.Bass("TRN2", target_bir_lowering=False, debug=False)
    ext = {}
    EI = lambda n, s, d=F32: nc.dram_tensor(n, s, d, kind="ExternalInput").ap()
    ext["x0T"] = EI("x0T", [D, NFULL])
    ext["pos2"] = EI("pos2", [1, 2])
    ext["wqT"] = EI("wqT", [D, D])
    ext["bq_col"] = EI("bq_col", [128, 16])
    ext["wkT_oh"] = EI("wkT_oh", [D, 1024])
    ext["bk_row"] = EI("bk_row", [1, 1024])
    ext["wvT_oh"] = EI("wvT_oh", [D, 1024])
    ext["bv_row"] = EI("bv_row", [1, 1024])
    ext["woT"] = EI("woT", [D, D])
    ext["bo_col"] = EI("bo_col", [128, 16])
    ext["wqgT"] = EI("wqgT", [D, HID0], I32)
    ext["sg_col"] = EI("sg_col", [128, 44])
    ext["wquT"] = EI("wquT", [D, HID0], I32)
    ext["su_col"] = EI("su_col", [128, 44])
    ext["wqdT"] = EI("wqdT", [HID0, D], I32)
    ext["sd_col"] = EI("sd_col", [128, 16])
    ext["x1T"] = nc.dram_tensor("x1T", [D, NOWN], F32, kind="ExternalOutput").ap()
    ext["xn1T"] = nc.dram_tensor("xn1T", [D, NOWN], BF16, kind="ExternalOutput").ap()
    ext["dbg_kv"] = nc.dram_tensor("dbg_kv", [128, 2048], F32, kind="ExternalOutput").ap()
    ext["dbg_ks"] = nc.dram_tensor("dbg_ks", [1, 2048], F32, kind="ExternalOutput").ap()
    ext["dbg_at"] = nc.dram_tensor("dbg_at", [D, NOWN], F32, kind="ExternalOutput").ap()
    ext["dbg_f0"] = nc.dram_tensor("dbg_f0", [D, NOWN], F32, kind="ExternalOutput").ap()
    ext["dbg_q"] = nc.dram_tensor("dbg_q", [D, NOWN], F32, kind="ExternalOutput").ap()
    ext["kvc"] = nc.dram_tensor("kvc", [1032, 128], F32, kind="Internal").ap()
    ext["kvg"] = nc.dram_tensor("kvg", [2064, 128], F32, kind="Internal").ap()
    ext["f0s"] = nc.dram_tensor("f0s", [D, NOWN], BF16, kind="Internal").ap()

    ext["DEBUG"] = DEBUG_L0
    B = Buf(nc)
    with nc.psum_tensor("PS", [128, 4096], F32) as PS:
        S = Sched(nc)
        build_common(S, B, nc)
        build_l0(S, B, ext, PS[:])
        S.finalize()
    return nc


def _build_fused_nc():
    nc = bass.Bass("TRN2", target_bir_lowering=False, debug=False)
    ext = {}
    EI = lambda n, s, d=F32: nc.dram_tensor(n, s, d, kind="ExternalInput").ap()
    IN = lambda n, s, d=F32: nc.dram_tensor(n, s, d, kind="Internal").ap()
    ext["x0T"] = EI("x0T", [D, NFULL])
    ext["pos2"] = EI("pos2", [1, 2])
    ext["wqT"] = EI("wqT", [D, D])
    ext["bq_col"] = EI("bq_col", [128, 16])
    ext["wkT_oh"] = EI("wkT_oh", [D, 1024])
    ext["bk_row"] = EI("bk_row", [1, 1024])
    ext["wvT_oh"] = EI("wvT_oh", [D, 1024])
    ext["bv_row"] = EI("bv_row", [1, 1024])
    ext["woT"] = EI("woT", [D, D])
    ext["bo_col"] = EI("bo_col", [128, 16])
    ext["wqgT"] = EI("wqgT", [D, HID0], I32)
    ext["sg_col"] = EI("sg_col", [128, 44])
    ext["wquT"] = EI("wquT", [D, HID0], I32)
    ext["su_col"] = EI("su_col", [128, 44])
    ext["wqdT"] = EI("wqdT", [HID0, D], I32)
    ext["sd_col"] = EI("sd_col", [128, 16])
    ext["wq1T"] = EI("wq1T", [D, 1024])
    ext["b1q"] = EI("b1q", [128, 8])
    ext["wk1T"] = EI("wk1T", [D, 1024])
    ext["b1k"] = EI("b1k", [128, 8])
    ext["wv1T"] = EI("wv1T", [D, 1024])
    ext["b1v"] = EI("b1v", [128, 8])
    ext["wo1T"] = EI("wo1T", [D, D])
    ext["bo1_col"] = EI("bo1_col", [128, 16])
    ext["wfcT"] = EI("wfcT", [D, HID1])
    ext["wpjT"] = EI("wpjT", [HID1, D])
    ext["idxE"] = EI("idxE", [128, 64], I32)
    ext["x2T"] = nc.dram_tensor("x2T", [D, NOWN], F32, kind="ExternalOutput").ap()
    ext["x1T"] = IN("x1s", [D, NOWN])
    ext["xn1T"] = IN("xn1s", [D, NOWN], BF16)
    ext["kvc"] = IN("kvc", [1032, 128])
    ext["kvg"] = IN("kvg", [2064, 128])
    ext["f0s"] = IN("f0s", [D, NOWN], BF16)
    ext["gus"] = IN("gus", [HID0, NOWN], BF16)
    ext["xn1c"] = IN("xn1c", [D, NOWN], BF16)
    ext["xn1g"] = IN("xn1g", [2 * D, NOWN], BF16)
    ext["attc"] = IN("attc", [1024, NFULL], BF16)
    ext["attg"] = IN("attg", [2048, NFULL], BF16)
    ext["whatfc"] = IN("whatfc", [D, HID1], BF16)
    ext["whatpj"] = IN("whatpj", [HID1, D], BF16)
    ext["DEBUG"] = False

    B = Buf(nc)
    with nc.psum_tensor("PS", [128, 4096], F32) as PS:
        S = Sched(nc)
        build_common(S, B, nc)
        build_l0(S, B, ext, PS[:])
        build_l1(S, B, ext, PS[:])
        S.finalize()
    return nc


FUSED = True
_NC_CACHE = {}


def kernel(**inputs):
    maps0 = _l0_inmaps(inputs)
    if FUSED:
        maps1 = _l1_inmaps(inputs, None)
        maps = [dict(m0, **m1) for m0, m1 in zip(maps0, maps1)]
        if "fused" not in _NC_CACHE:
            _NC_CACHE["fused"] = _build_fused_nc()
        res1 = run_bass_kernel_spmd(_NC_CACHE["fused"], maps, core_ids=list(range(8)))
    else:
        nc0 = _build_l0_nc()
        res0 = run_bass_kernel_spmd(nc0, maps0, core_ids=list(range(8)))
        maps1 = _l1_inmaps(inputs, res0.results)
        nc1 = _build_l1_nc()
        res1 = run_bass_kernel_spmd(nc1, maps1, core_ids=list(range(8)))
    outs = []
    for b in range(NB):
        xa = np.asarray(res1.results[2 * b]["x2T"], np.float32)
        xb_ = np.asarray(res1.results[2 * b + 1]["x2T"], np.float32)
        outs.append(np.concatenate([xa, xb_], axis=1).T)
    return np.stack(outs).astype(np.float32)


if __name__ == "__main__":
    nc = _build_l0_nc()
    n = sum(len(v) for v in Sched(bass.Bass("TRN2", target_bir_lowering=False)).ops.values())
    print("L0 build OK")


# version 16
# speedup vs baseline: 1.1624x; 1.1419x over previous
import sys
sys.path.insert(0, "/opt/trn_rl_repo")
import numpy as np
import ml_dtypes
import concourse.bass as bass
import concourse.mybir as mybir
from concourse.bass_utils import run_bass_kernel_spmd

F32 = mybir.dt.float32
BF16 = mybir.dt.bfloat16
I32 = mybir.dt.int32
U8 = mybir.dt.uint8
AF = mybir.ActivationFunctionType
ALU = mybir.AluOpType

D = 2048
NC = 16
NOWN = 2048
NFULL = 4096
NB = 4
HID0 = 5632
NH0 = 44
HID1 = 8192
NH1 = 64
EPS_LN = 1e-5
EPS_Z = 1e-6
EPS_B = 1e-6
CQK = float(1.0 / np.sqrt(128.0))
GROUPS = [[0, 1], [2, 3], [4, 5], [6, 7]]
DEBUG_L0 = False
DEBUG_L1 = False
N_W_FC = float(HID1 * D)


class Sched:
    ENGS = ("t", "v", "s", "g", "q")

    def __init__(self, nc):
        self.nc = nc
        self.ops = {e: [] for e in self.ENGS}
        self.cnt = {}
        self.sems = {}
        self.lastw = {e: {} for e in self.ENGS}
        self._ctx = []

    def sem(self, name):
        if name not in self.sems:
            g = self.nc.semaphore(name)
            self.sems[name] = g.__enter__()
            self._ctx.append(g)
            self.cnt[name] = 0
        return self.sems[name]

    def on(self, e, fn, waits=(), inc=None, incn=1):
        wl = []
        for sname, v in waits:
            if v <= 0:
                continue
            self.sem(sname)
            if self.lastw[e].get(sname, -1) >= v:
                continue
            self.lastw[e][sname] = v
            wl.append((self.sems[sname], v))
        ih = self.sem(inc) if inc else None

        def run(eng, fn=fn, wl=tuple(wl), ih=ih, incn=incn):
            for s, v in wl:
                eng.wait_ge(s, v)
            r = fn(eng)
            if ih is not None:
                r.then_inc(ih, incn)
        self.ops[e].append(run)
        if inc:
            self.cnt[inc] += incn
            return self.cnt[inc]
        return None

    def dma(self, e, out, in_, waits=()):
        sem = "dg" if e == "g" else "dq"
        self.sem(sem)
        return self.on(e, lambda eng, o=out, i=in_: eng.dma_start(out=o, in_=i),
                       waits, inc=sem, incn=16)

    def c(self, name):
        return self.cnt.get(name, 0)

    def fence(self, engines="tvsgq"):
        vals = [(s, self.cnt[s]) for s in list(self.cnt)]
        for e in engines:
            self.on(e, lambda eng: eng.nop(), waits=vals)

    def finalize(self):
        nc = self.nc
        with nc.Block() as blk:
            @blk.tensor
            def _(t):
                for f in self.ops["t"]:
                    f(t)

            @blk.vector
            def _(v):
                for f in self.ops["v"]:
                    f(v)

            @blk.scalar
            def _(s):
                for f in self.ops["s"]:
                    f(s)

            @blk.gpsimd
            def _(g):
                for f in self.ops["g"]:
                    f(g)

            @blk.sync
            def _(q):
                for f in self.ops["q"]:
                    f(q)


def colload_half(wT, t, half):
    K = wT.shape[0]
    return wT[half * K // 2:(half + 1) * K // 2,
              t * 128:(t + 1) * 128].rearrange("(c p) o -> p c o", p=128)


def colload_ap(wT, t, w=128):
    return wT[:, t * w:(t + 1) * w].rearrange("(c p) o -> p c o", p=128)


def slab_ap(wT, c0, c1):
    return wT[:, c0:c1].rearrange("(c p) o -> p c o", p=128)


class FM:

    def __init__(self, ap, nchunk, w):
        self.ap, self.nchunk, self.w = ap, nchunk, w

    def ch(self, c, a=0, b=None):
        b = self.w if b is None else b
        return self.ap[:, c * self.w + a: c * self.w + b]

    def all3(self):
        return self.ap[:, 0:self.nchunk * self.w].rearrange("p (c n) -> p c n", n=self.w)


class Buf:

    def __init__(self, nc):
        A = lambda name, shape, dt: nc.alloc_sbuf_tensor(name, shape, dt).ap()
        self.AR1 = A("sb_AR1", [128, 65536], U8)
        self.AR3 = A("sb_AR3", [128, 65536], U8)
        self.ST = A("sb_ST", [128, 34816], U8)
        self.ROWS = A("sb_rows", [1, 4096], F32)
        self.drow = A("sb_drow", [1, 2048], BF16)
        self.ksum_sb = A("sb_ksum_sb", [1, 2048], BF16)
        self.kv16 = A("sb_kv16", [128, 2048], BF16)
        self.tmpf = A("sb_tmpf", [128, 512], F32)
        self.mb = A("sb_mb", [128, 512], F32)
        self.rb = A("sb_rb", [128, 512], F32)
        self.sqt = [A("sb_sqt0", [128, 512], BF16), A("sb_sqt1", [128, 512], BF16)]
        self.mstrip = A("sb_mstrip", [128, 896], BF16)
        self.ident = A("sb_ident", [128, 128], BF16)
        self.cst = A("sb_cst", [128, 8], F32)
        self.ones_k1 = A("sb_ones_k1", [1, 128], BF16)
        self.n128_k1 = A("sb_n128_k1", [1, 128], BF16)
        self.ones_m1 = A("sb_ones_m1", [128, 1], BF16)
        self.ones_k1f = A("sb_ones_k1f", [1, 128], F32)
        self.ones_m1f = A("sb_ones_m1f", [128, 1], F32)
        self.posf = A("sb_posf", [128, 16], F32)
        self.decay_col = A("sb_decay_col", [128, 32], F32)
        self.poso = A("sb_poso", [1, 2], F32)
        self.offb = A("sb_offb", [1, 2], F32)
        self.bq_col = A("sb_bq_col", [128, 16], F32)
        self.bo_col = A("sb_bo_col", [128, 16], F32)
        self.sg_col = A("sb_sg_col", [128, 44], F32)
        self.su_col = A("sb_su_col", [128, 44], F32)
        self.sd_col = A("sb_sd_col", [128, 16], F32)
        self.bk_row = A("sb_bk_row", [1, 1024], BF16)
        self.bv_row = A("sb_bv_row", [1, 1024], BF16)
        self.b1q = A("sb_b1q", [128, 8], F32)
        self.b1k = A("sb_b1k", [128, 8], F32)
        self.b1v = A("sb_b1v", [128, 8], F32)
        self.bo1_col = A("sb_bo1_col", [128, 16], F32)
        self.gam = A("sb_gam", [1, 4], F32)
        self.gcol = A("sb_gcol", [128, 4], F32)
        self.acc = A("sb_acc", [128, 2], F32)
        self.r11 = A("sb_r11", [1, 8], F32)
        self.negC = A("sb_negC", [128, 1], F32)
        AR3f = self.AR3.bitcast(F32)
        AR3b_ = self.AR3.bitcast(BF16)
        self.tA = AR3f[:, 13824:14336]
        self.tB = AR3f[:, 14336:14848]
        self.tC = AR3f[:, 14848:15360]
        self.tD = AR3b_[:, 30720:31232]
        self.tE = AR3b_[:, 31232:31744]


def bf(ap):
    return ap.bitcast(BF16)


def f32(ap):
    return ap.bitcast(F32)


def build_common(S, B, nc):
    CONSTS = [EPS_LN, EPS_Z, EPS_B, -128.0, -0.25, 0.5, 1.0, 0.0]
    for j, cv in enumerate(CONSTS):
        S.on("g", lambda e, j=j, cv=cv: e.memset(B.cst[:, j:j + 1], cv))
        nc.const_aps.register(F32, cv, B.cst[:, j:j + 1])
    S.on("g", lambda e: e.memset(B.ones_k1, 1.0))
    S.on("g", lambda e: e.memset(B.n128_k1, -128.0))
    S.on("g", lambda e: e.memset(B.ones_m1, 1.0))
    S.on("g", lambda e: e.memset(B.ones_k1f, 1.0))
    S.on("g", lambda e: e.memset(B.ones_m1f, 1.0))
    S.on("g", lambda e: e.memset(B.ident, 0.0))
    S.on("g", lambda e: e.affine_select(out=B.ident, in_=B.ident, compare_op=ALU.not_equal,
                                        fill=1.0, base=0, pattern=[[-1, 128]],
                                        channel_multiplier=1))
    S.on("g", lambda e: e.memset(B.mstrip, 1.0))
    S.on("g", lambda e: e.affine_select(out=B.mstrip, in_=B.mstrip, compare_op=ALU.is_ge,
                                        fill=0.0, base=-384, pattern=[[1, 896]],
                                        channel_multiplier=-1))


def build_l0(S, B, ext, PS):
    nc = S.nc
    P = lambda b, n=1: PS[:, b * 512:(b + n) * 512]
    PR = lambda b, w=512: PS[0:1, b * 512:b * 512 + w]
    W = lambda *a: list(a)

    xn0 = FM(bf(B.AR1), NC, NOWN)
    xnO = FM(bf(B.AR3), NC, NOWN)
    guA = FM(bf(B.AR3), NH0, 512)
    qA = FM(bf(B.AR3), NC, NOWN)
    atA = FM(bf(B.AR1), NC, NOWN)

    STb = bf(B.ST)
    xslab = FM(STb[:, 0:NC * 512], NC, 512)
    ktile16 = FM(STb[:, 0:NC * 512], NC, 512)
    wg_c = [FM(STb[:, i * 2048:(i + 1) * 2048], NC, 128) for i in (0, 1)]
    wu_c = [FM(STb[:, (2 + i) * 2048:(3 + i) * 2048], NC, 128) for i in (0, 1)]
    wd_c = FM(STb[:, 0:NH0 * 128], NH0, 128)
    wo_c = [FM(STb[:, i * 2048:(i + 1) * 2048], NC, 128) for i in (0, 1)]
    wbuf = FM(STb[:, 8192:8192 + NC * 512], NC, 512)
    wgd = FM(STb[:, 8192:8192 + 2048], NC, 128)
    wud = FM(STb[:, 8192 + 2048:8192 + 4096], NC, 128)
    wdd = FM(STb[:, 8192:8192 + NH0 * 128], NH0, 128)
    vtile = STb[:, 16384:16896]
    gtmp = STb[:, 16896:17408]
    x1colb = STb[:, 8192:16384]

    ROWS = B.ROWS
    r_mean, r_msq, r_t1, r_rstd = (ROWS[0:1, i * 512:(i + 1) * 512] for i in range(4))
    r_u = ROWS[0:1, 0:2048]
    r_zb = ROWS[0:1, 2048:4096]

    gP0 = S.on("g", lambda e: e.iota(B.posf, pattern=[[128, 16]], base=0,
                                     channel_multiplier=1,
                                     allow_small_or_imprecise_dtypes=True), inc="gg")
    gP1 = S.on("g", lambda e: e.iota(B.drow, pattern=[[1, 2048]], base=0,
                                     channel_multiplier=0,
                                     allow_small_or_imprecise_dtypes=True), inc="gg")
    d0 = S.dma("g", B.poso, ext["pos2"])
    S.dma("g", B.bq_col, ext["bq_col"])
    S.dma("g", B.bo_col, ext["bo_col"])
    S.dma("g", B.sg_col, ext["sg_col"])
    S.dma("g", B.su_col, ext["su_col"])
    S.dma("g", B.sd_col, ext["sd_col"])
    S.dma("g", B.bk_row, ext["bk_row"])
    dsm = S.dma("g", B.bv_row, ext["bv_row"])
    s0 = S.on("s", lambda e: e.activation(B.offb, B.poso, AF.Copy, scale=-1.0 / NFULL),
              waits=W(("dg", d0)), inc="ss")
    tof = S.on("t", lambda e: e.matmul(PS[:, 0:2], B.ones_k1f, B.offb,
                                       start=True, stop=True),
               waits=W(("ss", s0)), inc="tt")
    vof = S.on("v", lambda e: e.tensor_copy(B.gcol[:, 0:2], PS[:, 0:2]),
               waits=W(("tt", tof)), inc="vv")
    S.on("s", lambda e: e.activation(B.decay_col[:, 0:16], B.posf, AF.Exp,
                                     scale=-1.0 / NFULL, bias=B.gcol[:, 0:1]),
         waits=W(("gg", gP0), ("vv", vof)), inc="ss")
    sdc = S.on("s", lambda e: e.activation(B.decay_col[:, 16:32], B.posf, AF.Exp,
                                           scale=-1.0 / NFULL, bias=B.gcol[:, 1:2]),
               inc="ss")
    sdr = S.on("s", lambda e: e.activation(B.drow, B.drow, AF.Exp,
                                           scale=-1.0 / NFULL, bias=B.offb[0:1, 0:1]),
               waits=W(("gg", gP1), ("ss", s0)), inc="ss")

    for half in (0, 1):
        cb = half * 2048
        xdst = xn0 if half == 0 else xnO
        for s4 in range(4):
            dx = S.dma("g", xslab.all3(),
                       slab_ap(ext["x0T"], cb + s4 * 512, cb + s4 * 512 + 512),
                       waits=W(("tt", S.c("tt")), ("vv", S.c("vv")), ("ss", S.c("ss"))))
            tcs = 0
            tsq = 0
            for c in range(16):
                tcs = S.on("t", lambda e, c=c: e.matmul(PR(0), B.ones_m1, xslab.ch(c),
                                                        start=(c == 0), stop=(c == 15)),
                           waits=W(("dg", dx), ("ss", S.c("ss")), ("vv", S.c("vv"))),
                           inc="tt")
            for c in range(16):
                sqv = S.on("s", lambda e, c=c: e.activation(B.sqt[c % 2], xslab.ch(c),
                                                            AF.Square),
                           waits=W(("dg", dx), ("tt", tsq)), inc="ss")
                tsq = S.on("t", lambda e, c=c: e.matmul(PR(1), B.ones_m1, B.sqt[c % 2],
                                                        start=(c == 0), stop=(c == 15)),
                           waits=W(("ss", sqv)), inc="tt")
            sm1 = S.on("s", lambda e: e.activation(r_mean, PR(0), AF.Copy, scale=1.0 / D),
                       waits=W(("tt", tcs)), inc="ss")
            sm2 = S.on("s", lambda e: e.activation(r_msq, PR(1), AF.Copy, scale=1.0 / D),
                       waits=W(("tt", tsq)), inc="ss")
            v1 = S.on("v", lambda e: e.tensor_mul(r_t1, r_mean, r_mean),
                      waits=W(("ss", sm2)), inc="vv")
            v2 = S.on("v", lambda e: e.tensor_sub(r_t1, r_msq, r_t1), inc="vv")
            sv1 = S.on("s", lambda e: e.activation(r_t1, r_t1, AF.Ln, bias=EPS_LN),
                       waits=W(("vv", v2)), inc="ss")
            sv2 = S.on("s", lambda e: e.activation(r_rstd, r_t1, AF.Exp, scale=-0.5),
                       inc="ss")
            tb1 = S.on("t", lambda e: e.matmul(P(2), B.ones_k1f, r_mean,
                                               start=True, stop=True),
                       waits=W(("ss", sm1), ("vv", S.c("vv"))), inc="tt")
            tb2 = S.on("t", lambda e: e.matmul(P(3), B.ones_k1f, r_rstd,
                                               start=True, stop=True),
                       waits=W(("ss", sv2)), inc="tt")
            vb1 = S.on("v", lambda e: e.tensor_copy(B.mb, P(2)), waits=W(("tt", tb1)),
                       inc="vv")
            vb2 = S.on("v", lambda e: e.tensor_copy(B.rb, P(3)), waits=W(("tt", tb2)),
                       inc="vv")
            for c in range(16):
                dst = xdst.ch(c, s4 * 512, s4 * 512 + 512)
                S.on("v", lambda e, c=c: e.tensor_sub(B.tmpf[:, 0:512], xslab.ch(c), B.mb),
                     waits=W(("vv", vb2)), inc="vv")
                S.on("v", lambda e, dst=dst: e.tensor_mul(dst, B.tmpf[:, 0:512], B.rb),
                     inc="vv")
    vxn_done = S.c("vv")

    kv_first = {4: True, 5: True}
    ks_first = {6: True, 7: True}
    for half in (0, 1):
        xsrc = xn0 if half == 0 else xnO
        for sc in range(2):
            dwk = S.dma("g", wbuf.all3(), slab_ap(ext["wkT_oh"], sc * 512, sc * 512 + 512),
                        waits=W(("tt", S.c("tt"))))
            ek = [0] * 16
            for i in range(16):
                bk_ = i % 2
                tk = 0
                for c in range(16):
                    tk = S.on("t", lambda e, c=c, i=i, bk_=bk_, xsrc=xsrc:
                              e.matmul(P(bk_), xsrc.ch(c, i * 128, i * 128 + 128),
                                       wbuf.ch(c), start=(c == 0), stop=False),
                              waits=W(("dg", dwk), ("vv", vxn_done),
                                      ("ss", ek[i - 2] if i >= 2 else 0)), inc="tt")
                tk = S.on("t", lambda e, sc=sc, bk_=bk_:
                          e.matmul(P(bk_), B.ones_k1, B.bk_row[0:1, sc * 512:sc * 512 + 512],
                                   start=False, stop=True),
                          waits=W(("dg", dsm)), inc="tt")
                dci = half * 16 + i
                ek[i] = S.on("s", lambda e, i=i, dci=dci, bk_=bk_:
                             e.activation(ktile16.ch(i), P(bk_), AF.Relu,
                                          scale=B.decay_col[:, dci:dci + 1]),
                             waits=W(("tt", tk)), inc="ss")
            dwv = S.dma("g", wbuf.all3(), slab_ap(ext["wvT_oh"], sc * 512, sc * 512 + 512),
                        waits=W(("tt", S.c("tt"))))
            evp = 0
            for i in range(16):
                bv_ = 2 + i % 2
                tv = 0
                for c in range(16):
                    tv = S.on("t", lambda e, c=c, i=i, bv_=bv_, xsrc=xsrc:
                              e.matmul(P(bv_), xsrc.ch(c, i * 128, i * 128 + 128),
                                       wbuf.ch(c), start=(c == 0), stop=False),
                              waits=W(("dg", dwv), ("vv", S.c("vv")),
                                      ("ss", evp if i >= 1 else 0)), inc="tt")
                tv = S.on("t", lambda e, sc=sc, bv_=bv_:
                          e.matmul(P(bv_), B.ones_k1, B.bv_row[0:1, sc * 512:sc * 512 + 512],
                                   start=False, stop=True), inc="tt")
                ev = S.on("s", lambda e, bv_=bv_: e.activation(vtile, P(bv_), AF.Copy),
                          waits=W(("tt", tv)), inc="ss")
                evp = ev
                for hh in range(4):
                    o = hh * 128
                    ft = kv_first[4 + sc]
                    kv_first[4 + sc] = False
                    S.on("t", lambda e, o=o, sc=sc, i=i, ft=ft:
                         e.matmul(PS[:, (4 + sc) * 512 + o:(4 + sc) * 512 + o + 128],
                                  ktile16.ap[:, i * 512 + o:i * 512 + o + 128],
                                  vtile[:, o:o + 128],
                                  start=ft, stop=False, skip_group_check=True),
                         waits=W(("ss", ev)), inc="tt")
                ft = ks_first[6 + sc]
                ks_first[6 + sc] = False
                S.on("t", lambda e, i=i, sc=sc, ft=ft:
                     e.matmul(PS[0:1, (6 + sc) * 512:(6 + sc) * 512 + 512],
                              B.ones_m1, ktile16.ch(i),
                              start=ft, stop=False, skip_group_check=True),
                     inc="tt")
    tkv_done = S.c("tt")

    vk1 = S.on("v", lambda e: e.tensor_copy(B.tmpf[:, 0:512], P(4)),
               waits=W(("tt", tkv_done)), inc="vv")
    dk1 = S.dma("q", ext["kvc"][0:512].rearrange("(h d) e -> d h e", d=128),
                B.tmpf[:, 0:512].rearrange("p (h e) -> p h e", e=128),
                waits=W(("vv", vk1)))
    vk2 = S.on("v", lambda e: e.tensor_copy(B.tmpf[:, 0:512], P(5)),
               waits=W(("dq", dk1)), inc="vv")
    dk2 = S.dma("q", ext["kvc"][512:1024].rearrange("(h d) e -> d h e", d=128),
                B.tmpf[:, 0:512].rearrange("p (h e) -> p h e", e=128),
                waits=W(("vv", vk2)))
    vk3 = S.on("v", lambda e: e.tensor_copy(r_u[0:1, 0:1024], PS[0:1, 6 * 512:6 * 512 + 1024]),
               inc="vv")
    dk3 = S.dma("q", ext["kvc"][1024:1032].rearrange("h e -> (h e)")[None, :],
                r_u[0:1, 0:1024], waits=W(("vv", vk3)))
    c1 = S.on("g", lambda e: e.collective_compute(
        "AllGather", ALU.bypass, replica_groups=GROUPS,
        ins=[ext["kvc"][:]], outs=[ext["kvg"][:]]),
        waits=W(("dq", max(dk1, dk2, dk3))), inc="c1")

    csxn = B.ksum_sb
    csgu = B.bk_row
    tcx = 0
    vcx = S.c("vv")
    for q4 in range(4):
        for c in range(16):
            tcx = S.on("t", lambda e, c=c, q4=q4: e.matmul(
                PR(0), B.ones_m1, xn0.ch(c, q4 * 512, q4 * 512 + 512),
                start=(c == 0), stop=(c == 15)),
                waits=W(("vv", max(vxn_done, vcx)), ("ss", S.c("ss"))), inc="tt")
        vcx = S.on("v", lambda e, q4=q4: e.tensor_copy(csxn[0:1, q4 * 512:q4 * 512 + 512],
                                                       PR(0)),
                   waits=W(("tt", tcx)), inc="vv")
    csxn_v = S.c("vv")
    gu_sb = STb[:, 16384:16896]
    gtmp = STb[:, 16896:17408]
    evg = {b: S.c("ss") for b in range(4)}
    evu = {b: S.c("vv") for b in range(8)}
    dgu_p = 0
    wrd = {0: 0, 1: 0}
    for ht in range(NH0):
        bsel = ht % 2
        dgc = S.dma("g", wg_c[bsel].all3(), colload_ap(ext["wqgT"], ht),
                    waits=W(("tt", wrd[bsel])))
        duc = S.dma("g", wu_c[bsel].all3(), colload_ap(ext["wquT"], ht))
        tg = {}
        for q4 in range(4):
            t_ = 0
            for c in range(16):
                t_ = S.on("t", lambda e, c=c, q4=q4, bsel=bsel: e.matmul(
                    P(q4), wg_c[bsel].ch(c), xn0.ch(c, q4 * 512, q4 * 512 + 512),
                    start=(c == 0), stop=False),
                    waits=W(("dg", dgc), ("ss", evg[q4]), ("vv", csxn_v)), inc="tt")
            tg[q4] = S.on("t", lambda e, q4=q4: e.matmul(
                P(q4), B.n128_k1, csxn[0:1, q4 * 512:q4 * 512 + 512],
                start=False, stop=True), inc="tt")
        tu = {}
        for q4 in range(4):
            t_ = 0
            for c in range(16):
                t_ = S.on("t", lambda e, c=c, q4=q4, bsel=bsel: e.matmul(
                    P(4 + q4), wu_c[bsel].ch(c), xn0.ch(c, q4 * 512, q4 * 512 + 512),
                    start=(c == 0), stop=False),
                    waits=W(("dg", duc), ("vv", evu[4 + q4])), inc="tt")
            tu[q4] = S.on("t", lambda e, q4=q4: e.matmul(
                P(4 + q4), B.n128_k1, csxn[0:1, q4 * 512:q4 * 512 + 512],
                start=False, stop=True), inc="tt")
        wrd[bsel] = tu[3]
        vg = 0
        for q4 in range(4):
            sg = S.on("s", lambda e, ht=ht, q4=q4: e.activation(
                gtmp, P(q4), AF.Silu, scale=B.sg_col[:, ht:ht + 1]),
                waits=W(("tt", tg[q4]), ("vv", vg if q4 else S.c("vv"))), inc="ss")
            evg[q4] = sg
            vu = S.on("v", lambda e, ht=ht, q4=q4: e.tensor_scalar(
                out=B.tmpf[:, 0:512], in0=P(4 + q4), scalar1=B.su_col[:, ht:ht + 1],
                scalar2=None, op0=ALU.mult), waits=W(("tt", tu[q4])), inc="vv")
            evu[4 + q4] = vu
            vg = S.on("v", lambda e: e.tensor_mul(gu_sb, gtmp, B.tmpf[:, 0:512]),
                      waits=W(("ss", sg), ("dq", dgu_p)), inc="vv")
            dgu_p = S.dma("q", ext["gus"][ht * 128:(ht + 1) * 128, q4 * 512:q4 * 512 + 512],
                          gu_sb, waits=W(("vv", vg)))
    wd_c2 = [FM(STb[:, 0:NH0 * 128], NH0, 128),
             FM(STb[:, 8192:8192 + NH0 * 128], NH0, 128)]
    evd = {b: S.c("ss") for b in (2, 3)}
    drd = {0: S.c("tt"), 1: S.c("tt")}
    for ns in range(4):
        n0 = ns * 512
        dgs = S.dma("g", guA.all3(), slab_ap(ext["gus"], n0, n0 + 512),
                    waits=W(("tt", S.c("tt")), ("dq", S.c("dq"))))
        tcg = 0
        for hc in range(NH0):
            tcg = S.on("t", lambda e, hc=hc: e.matmul(
                PR(6), B.ones_m1, guA.ch(hc), start=(hc == 0), stop=(hc == NH0 - 1)),
                waits=W(("dg", dgs), ("vv", S.c("vv")), ("ss", S.c("ss"))), inc="tt")
        vcg = S.on("v", lambda e: e.tensor_copy(csgu[0:1, 0:512], PR(6)),
                   waits=W(("tt", tcg)), inc="vv")
        for dt in range(16):
            bsel = dt % 2
            ddc = S.dma("g", wd_c2[bsel].all3(), colload_ap(ext["wqdT"], dt),
                        waits=W(("tt", drd.get(bsel, 0))))
            pb = 2 + dt % 2
            td = 0
            for hc in range(NH0):
                td = S.on("t", lambda e, hc=hc, pb=pb, bsel=bsel: e.matmul(
                    P(pb), wd_c2[bsel].ch(hc), guA.ch(hc),
                    start=(hc == 0), stop=False),
                    waits=W(("dg", ddc), ("ss", evd[pb]), ("vv", vcg)), inc="tt")
            td = S.on("t", lambda e, pb=pb: e.matmul(
                P(pb), B.n128_k1, csgu[0:1, 0:512], start=False, stop=True), inc="tt")
            drd[bsel] = td
            sf = S.on("s", lambda e, dt=dt, pb=pb: e.activation(
                B.tD, P(pb), AF.Copy, scale=B.sd_col[:, dt:dt + 1]),
                waits=W(("tt", td)), inc="ss")
            evd[pb] = sf
            S.dma("q", ext["f0s"][dt * 128:(dt + 1) * 128, n0:n0 + 512], B.tD,
                  waits=W(("ss", sf)))

    qev = {0: S.c("ss"), 1: S.c("ss")}
    qrd = {0: S.c("tt"), 1: S.c("tt")}
    for t16 in range(16):
        bsel = t16 % 2
        dqc = S.dma("g", wo_c[bsel].all3(), colload_ap(ext["wqT"], t16),
                    waits=W(("tt", qrd[bsel])))
        for s4 in range(4):
            pb = s4 % 2
            tq = 0
            for c in range(16):
                tq = S.on("t", lambda e, c=c, s4=s4, bsel=bsel, pb=pb:
                          e.matmul(P(pb), wo_c[bsel].ch(c),
                                   xn0.ch(c, s4 * 512, s4 * 512 + 512),
                                   start=(c == 0), stop=(c == 15)),
                          waits=W(("dg", dqc), ("ss", qev[pb])), inc="tt")
            qev[pb] = S.on("s", lambda e, t16=t16, s4=s4, pb=pb: e.activation(
                qA.ch(t16, s4 * 512, s4 * 512 + 512), P(pb), AF.Relu,
                bias=B.bq_col[:, t16:t16 + 1]),
                waits=W(("tt", tq)), inc="ss")
            qrd[bsel] = tq
    q_done = S.c("ss")

    S.on("g", lambda e: e.nop(), waits=W(("c1", 1)))
    dk16 = 0
    for H in range(16):
        r, hp = H // 8, H % 8
        dk16 = S.dma("g", B.kv16[:, H * 128:(H + 1) * 128],
                     ext["kvg"][r * 1032 + hp * 128: r * 1032 + (hp + 1) * 128, :],
                     waits=W(("c1", 1)))
    dks = 0
    for r in range(2):
        dks = S.dma("g", B.ksum_sb[0:1, r * 1024:(r + 1) * 1024],
                    ext["kvg"][r * 1032 + 1024: r * 1032 + 1032, :].rearrange("h e -> (h e)")[None, :],
                    waits=W(("c1", 1), ("tt", S.c("tt"))))
    kc_t = 0
    for H in range(16):
        kc_t = S.on("t", lambda e, H=H: e.matmul(
            PS[:, 6 * 512 + H:6 * 512 + H + 1],
            B.ksum_sb[0:1, H * 128:(H + 1) * 128], B.ones_k1[0:1, 0:1],
            start=(H == 0), stop=(H == 15), skip_group_check=True),
            waits=W(("dg", dks), ("ss", q_done)), inc="tt")
    vkc = S.on("v", lambda e: e.tensor_copy(B.sqt[0][:, 0:16], PS[:, 6 * 512:6 * 512 + 16]),
               waits=W(("tt", kc_t)), inc="vv")
    kcols = B.sqt[0][:, 0:16]

    rd = {b: S.c("vv") for b in range(8)}
    for H in range(16):
        for s4 in range(4):
            ub = 6 + (s4 % 2)
            tu4 = S.on("t", lambda e, H=H, s4=s4, ub=ub: e.matmul(
                PR(ub, 512), kcols[:, H:H + 1],
                qA.ch(H, s4 * 512, s4 * 512 + 512),
                start=True, stop=True), waits=W(("vv", max(vkc, rd[ub]))), inc="tt")
            rd[ub] = S.on("v", lambda e, s4=s4, ub=ub: e.tensor_copy(
                r_u[0:1, s4 * 512:s4 * 512 + 512], PR(ub, 512)),
                waits=W(("tt", tu4)), inc="vv")
        v5 = S.on("v", lambda e: e.tensor_mul(r_zb, r_u, B.drow),
                  waits=W(("ss", sdr), ("vv", S.c("vv"))), inc="vv")
        s5 = S.on("s", lambda e: e.activation(r_zb, r_zb, AF.Copy, bias=EPS_Z),
                  waits=W(("vv", v5)), inc="ss")
        v6a = S.on("v", lambda e: e.reciprocal(r_zb, r_zb),
                   waits=W(("ss", s5)), inc="vv")
        v6 = S.on("v", lambda e: e.tensor_mul(r_u, r_zb, B.drow), inc="vv")
        for s4 in range(4):
            pb = s4 % 2
            to = S.on("t", lambda e, H=H, s4=s4, pb=pb: e.matmul(
                P(pb), B.kv16[:, H * 128:(H + 1) * 128],
                qA.ch(H, s4 * 512, s4 * 512 + 512), start=True, stop=True),
                waits=W(("vv", max(v6, rd[pb])), ("dg", dk16)), inc="tt")
            tz = S.on("t", lambda e, s4=s4, pb=pb: e.matmul(
                P(2 + pb), B.ones_k1f, r_u[0:1, s4 * 512:s4 * 512 + 512],
                start=True, stop=True), waits=W(("vv", rd[2 + pb])), inc="tt")
            vz = S.on("v", lambda e, pb=pb: e.tensor_copy(B.mb, P(2 + pb)),
                      waits=W(("tt", tz)), inc="vv")
            rd[2 + pb] = vz
            rd[pb] = S.on("v", lambda e, H=H, s4=s4, pb=pb: e.tensor_mul(
                atA.ch(H, s4 * 512, s4 * 512 + 512), P(pb), B.mb),
                waits=W(("tt", to), ("vv", vz)), inc="vv")
    at_done = S.c("vv")

    swv = {0: 0, 1: 0}
    ord_ = {0: at_done, 1: at_done}
    vb_p = 0
    dx1_p = 0
    dxn_p = 0
    tsq_p = S.c("tt")
    for s4 in range(4):
        x1v = [0] * 16
        for t16 in range(16):
            bsel = t16 % 2
            dwo = S.dma("g", wo_c[bsel].all3(), colload_ap(ext["woT"], t16),
                        waits=W(("tt", ord_[bsel])))
            pb = t16 % 2
            tw = 0
            for c in range(16):
                tw = S.on("t", lambda e, c=c, s4=s4, bsel=bsel, pb=pb:
                          e.matmul(P(pb), wo_c[bsel].ch(c),
                                   atA.ch(c, s4 * 512, s4 * 512 + 512),
                                   start=(c == 0), stop=(c == 15)),
                          waits=W(("dg", dwo), ("vv", max(at_done, swv[pb]))),
                          inc="tt")
            ord_[bsel] = tw
            sw = S.on("v", lambda e, t16=t16, pb=pb: e.tensor_scalar(
                out=B.tA, in0=P(pb), scalar1=B.bo_col[:, t16:t16 + 1],
                scalar2=None, op0=ALU.add),
                waits=W(("tt", tw), ("vv", vb_p), ("dq", dx1_p)), inc="vv")
            swv[pb] = sw
            df0 = S.dma("g", B.tD, ext["f0s"][t16 * 128:(t16 + 1) * 128,
                                              s4 * 512:s4 * 512 + 512],
                        waits=W(("vv", S.c("vv")), ("dq", S.c("dq"))))
            dx0 = S.dma("q", B.tB, ext["x0T"][t16 * 128:(t16 + 1) * 128,
                                              s4 * 512:s4 * 512 + 512],
                        waits=W(("vv", max(vb_p, at_done))))
            va = S.on("v", lambda e: e.tensor_add(B.tC, B.tA, B.tD),
                      waits=W(("dg", df0)), inc="vv")
            vb = S.on("v", lambda e: e.tensor_add(B.tA, B.tC, B.tB),
                      waits=W(("dq", dx0)), inc="vv")
            vb_p = vb
            dx1 = S.dma("q", ext["x1T"][t16 * 128:(t16 + 1) * 128, s4 * 512:s4 * 512 + 512],
                        B.tA, waits=W(("vv", vb)))
            dx1_p = dx1
            x1v[t16] = S.on("s", lambda e, t16=t16: e.activation(
                x1colb[:, t16 * 512:(t16 + 1) * 512], B.tA, AF.Copy),
                waits=W(("vv", vb), ("tt", tsq_p)), inc="ss")
        tcs = tsq = 0
        for t16 in range(16):
            tcs = S.on("t", lambda e, t16=t16: e.matmul(
                PR(2), B.ones_m1, x1colb[:, t16 * 512:(t16 + 1) * 512],
                start=(t16 == 0), stop=(t16 == 15)),
                waits=W(("ss", x1v[t16])), inc="tt")
        for t16 in range(16):
            sq2 = S.on("s", lambda e, t16=t16: e.activation(
                B.sqt[t16 % 2], x1colb[:, t16 * 512:(t16 + 1) * 512], AF.Square),
                waits=W(("tt", tsq)), inc="ss")
            tsq = S.on("t", lambda e, t16=t16: e.matmul(
                PR(3), B.ones_m1, B.sqt[t16 % 2], start=(t16 == 0), stop=(t16 == 15)),
                waits=W(("ss", sq2)), inc="tt")
        tsq_p = tsq
        sm1 = S.on("s", lambda e: e.activation(r_mean, PR(2), AF.Copy, scale=1.0 / D),
                   waits=W(("tt", tcs)), inc="ss")
        sm2 = S.on("s", lambda e: e.activation(r_msq, PR(3), AF.Copy, scale=1.0 / D),
                   waits=W(("tt", tsq)), inc="ss")
        v1 = S.on("v", lambda e: e.tensor_mul(r_t1, r_mean, r_mean),
                  waits=W(("ss", sm2)), inc="vv")
        v2 = S.on("v", lambda e: e.tensor_sub(r_t1, r_msq, r_t1), inc="vv")
        S.on("s", lambda e: e.activation(r_t1, r_t1, AF.Ln, bias=EPS_LN),
             waits=W(("vv", v2)), inc="ss")
        sv2 = S.on("s", lambda e: e.activation(r_rstd, r_t1, AF.Exp, scale=-0.5), inc="ss")
        tb1 = S.on("t", lambda e: e.matmul(P(4), B.ones_k1f, r_mean, start=True, stop=True),
                   waits=W(("ss", sv2)), inc="tt")
        tb2 = S.on("t", lambda e: e.matmul(P(5), B.ones_k1f, r_rstd, start=True, stop=True),
                   inc="tt")
        vb1 = S.on("v", lambda e: e.tensor_copy(B.mb, P(4)), waits=W(("tt", tb1)), inc="vv")
        vb2 = S.on("v", lambda e: e.tensor_copy(B.rb, P(5)), waits=W(("tt", tb2)), inc="vv")
        for t16 in range(16):
            S.on("v", lambda e, t16=t16: e.tensor_sub(
                B.tmpf[:, 0:512], x1colb[:, t16 * 512:(t16 + 1) * 512], B.mb),
                waits=W(("vv", vb2)), inc="vv")
            vx = S.on("v", lambda e: e.tensor_mul(B.tE, B.tmpf[:, 0:512], B.rb),
                      waits=W(("dq", dxn_p)), inc="vv")
            dxn_p = S.dma("q", ext["xn1T"][t16 * 128:(t16 + 1) * 128,
                                           s4 * 512:s4 * 512 + 512],
                          B.tE, waits=W(("vv", vx)))
    S.on("q", lambda e: e.nop(), waits=W(("dq", S.c("dq")), ("dg", S.c("dg"))))



def build_l1(S, B, ext, PS):
    nc = S.nc
    P = lambda b, n=1: PS[:, b * 512:(b + n) * 512]
    PR = lambda b, w=512: PS[0:1, b * 512:b * 512 + w]
    W = lambda *a: list(a)

    AR1b, AR1f = bf(B.AR1), f32(B.AR1)
    AR3b = bf(B.AR3)
    STb = bf(B.ST)

    wstage = AR1f[:, 0:8192]
    qtile = AR1f[:, 8192:16384]
    swt = AR3b[:, 0:8192]
    m01t = AR3b[:, 8192:16384]
    whatt = AR3b[:, 16384:24576]
    xg = FM(AR1b, NC, 2048)
    Qh = AR3b[:, 0:4096]
    Kh = AR3b[:, 4096:8192]
    Vth = AR3b[:, 8192:12288]
    Vtok = AR3b[:, 12288:16384]
    q2t = AR3b[:, 16384:20480]
    Psl = [AR3b[:, 20480 + i * 512:20480 + (i + 1) * 512] for i in range(4)]
    wqc = FM(STb[:, 0:2048], NC, 128)
    wkc = FM(STb[:, 2048:4096], NC, 128)
    wvc = FM(STb[:, 4096:6144], NC, 128)
    PB7 = PS[:, 7 * 512:8 * 512].bitcast(BF16)
    harena = FM(AR1b, NH1, 512)
    xnf = FM(AR3b[:, 0:8192], NC, 512)
    attE = FM(AR3b[:, 8192:16384], NC, 512)
    wpjh = [FM(AR3b[:, 16384 + j * 4096:16384 + (j + 1) * 4096], 32, 128) for j in (0, 1)]
    xslab4 = FM(STb[:, 0:8192], NC, 512)
    wfc_col = [FM(STb[:, 8192 + i * 2048:8192 + (i + 1) * 2048], NC, 128) for i in (0, 1)]
    wo_c4 = [FM(STb[:, 12288 + i * 2048:12288 + (i + 1) * 2048], NC, 128) for i in (0, 1)]
    idxE = B.mstrip.bitcast(I32)[:, 0:64]

    ROWS = B.ROWS
    r_mean, r_msq, r_t1, r_rstd = (ROWS[0:1, i * 512:(i + 1) * 512] for i in range(4))
    r_sum = ROWS[0:1, 2048:2560]

    b1qc = B.decay_col[:, 0:8]

    S.fence()
    dc0 = S.dma("q", ext["xn1c"][:], ext["xn1T"][:], waits=W(("dq", S.c("dq"))))
    c2 = S.on("g", lambda e: e.collective_compute(
        "AllGather", ALU.bypass, replica_groups=GROUPS,
        ins=[ext["xn1c"][:]], outs=[ext["xn1g"][:]]),
        waits=W(("dq", dc0)), inc="c2")

    db1 = S.dma("g", B.b1q, ext["b1q"])
    S.dma("g", B.b1k, ext["b1k"])
    S.dma("g", B.b1v, ext["b1v"])
    dbo = S.dma("g", B.bo1_col, ext["bo1_col"])
    sb1 = S.on("s", lambda e: e.activation(b1qc, B.b1q, AF.Copy, scale=CQK),
               waits=W(("dg", db1)), inc="ss")

    S.on("v", lambda e: e.memset(B.acc, 0.0), inc="vv")
    gam_cols = {}
    for m, (wT, nch, F) in enumerate((("wfcT", 16, 8192), ("wpjT", 64, 2048))):
        vlast = 0
        for ch in range(nch):
            half = ch % 2
            dw = S.dma("q", wstage[:, 0:F],
                       ext[wT][ch * 128:(ch + 1) * 128, :],
                       waits=W(("ss", S.c("ss")), ("vv", S.c("vv"))))
            sa = S.on("s", lambda e, F=F: e.activation(swt[:, 0:F], wstage[:, 0:F], AF.Abs,
                                                       accum_out=B.gcol[:, 3:4]),
                      waits=W(("dq", dw)), inc="ss")
            vlast = S.on("v", lambda e, m=m: e.tensor_add(B.acc[:, m:m + 1], B.acc[:, m:m + 1],
                                                          B.gcol[:, 3:4]),
                         waits=W(("ss", sa)), inc="vv")
        tg1 = S.on("t", lambda e, m=m: e.matmul(PS[0:1, 6 * 512:6 * 512 + 1],
                                                B.acc[:, m:m + 1], B.ones_m1f,
                                                start=True, stop=True),
                   waits=W(("vv", vlast)), inc="tt")
        sg1 = S.on("s", lambda e, m=m: e.activation(B.r11[0:1, m:m + 1],
                                                    PS[0:1, 6 * 512:6 * 512 + 1],
                                                    AF.Copy, scale=1.0 / N_W_FC),
                   waits=W(("tt", tg1)), inc="ss")
        sg2 = S.on("s", lambda e, m=m: e.activation(B.r11[0:1, 2 + m:3 + m],
                                                    B.r11[0:1, m:m + 1], AF.Copy,
                                                    bias=EPS_B),
                   inc="ss")
        vg1 = S.on("v", lambda e, m=m: e.reciprocal(B.r11[0:1, 4 + m:5 + m],
                                                    B.r11[0:1, 2 + m:3 + m]),
                   waits=W(("ss", sg2)), inc="vv")
        tb = S.on("t", lambda e, m=m: e.matmul(PS[:, 6 * 512:6 * 512 + 1], B.ones_k1f,
                                               B.r11[0:1, 4 + m:5 + m], start=True, stop=True),
                  waits=W(("vv", vg1)), inc="tt")
        vb = S.on("v", lambda e, m=m: e.tensor_copy(B.gcol[:, 2 * m:2 * m + 1],
                                                    PS[:, 6 * 512:6 * 512 + 1]),
                  waits=W(("tt", tb)), inc="vv")
        tb2 = S.on("t", lambda e, m=m: e.matmul(PS[:, 6 * 512:6 * 512 + 1], B.ones_k1f,
                                                B.r11[0:1, m:m + 1], start=True, stop=True),
                   waits=W(("vv", vb)), inc="tt")
        vb2 = S.on("v", lambda e, m=m: e.tensor_copy(B.gcol[:, 2 * m + 1:2 * m + 2],
                                                     PS[:, 6 * 512:6 * 512 + 1]),
                   waits=W(("tt", tb2)), inc="vv")
        gam_cols[m] = vb2

    wst_p = 0
    for m, (wT, outT, nch, F) in enumerate((("wfcT", "whatfc", 16, 8192),
                                            ("wpjT", "whatpj", 64, 2048))):
        for ch in range(nch):
            dw = S.dma("q", wstage[:, 0:F], ext[wT][ch * 128:(ch + 1) * 128, :],
                       waits=W(("ss", S.c("ss")), ("vv", S.c("vv"))))
            sq_ = S.on("s", lambda e, m=m, F=F: e.activation(
                qtile[:, 0:F], wstage[:, 0:F], AF.Square, scale=B.gcol[:, 2 * m:2 * m + 1]),
                waits=W(("dq", dw), ("vv", gam_cols[1]), ("vv2" if False else "vv", S.c("vv"))),
                inc="ss")
            vm = S.on("v", lambda e, F=F: e.tensor_scalar(
                out=m01t[:, 0:F], in0=qtile[:, 0:F], scalar1=0.25, scalar2=None,
                op0=ALU.is_ge), waits=W(("ss", sq_)), inc="vv")
            ss_ = S.on("s", lambda e, F=F: e.activation(swt[:, 0:F], wstage[:, 0:F], AF.Sign),
                       inc="ss")
            vw = S.on("v", lambda e, m=m, F=F: e.scalar_tensor_tensor(
                out=whatt[:, 0:F], in0=swt[:, 0:F], scalar=B.gcol[:, 2 * m + 1:2 * m + 2],
                in1=m01t[:, 0:F], op0=ALU.mult, op1=ALU.mult),
                waits=W(("ss", ss_), ("vv", vm), ("dq", wst_p)), inc="vv")
            wst_p = S.dma("q", ext[outT][ch * 128:(ch + 1) * 128, :], whatt[:, 0:F],
                          waits=W(("vv", vw)))

    S.on("g", lambda e: e.nop(), waits=W(("c2", 1)))
    S.fence()
    head_tt = S.c("tt")
    head_ss = S.c("ss")
    head_vv = S.c("vv")
    for hp in range(8):
        dq_ = S.dma("g", wqc.all3(), colload_ap(ext["wq1T"], hp), waits=W(("tt", head_tt)))
        dk_ = S.dma("g", wkc.all3(), colload_ap(ext["wk1T"], hp))
        dv_ = S.dma("g", wvc.all3(), colload_ap(ext["wv1T"], hp))
        rdq = {0: head_vv, 1: head_vv, 2: head_vv}
        evq = S.c("vv")
        for nh in range(2):
            dxg = 0
            for c in range(16):
                dxg = S.dma("g", xg.ch(c),
                            ext["xn1g"][nh * 2048 + c * 128: nh * 2048 + (c + 1) * 128, :],
                            waits=W(("tt", S.c("tt"))))
            for s4 in range(4):
                col = nh * 2048 + s4 * 512
                tq = tk = tv = 0
                for c in range(16):
                    tq = S.on("t", lambda e, c=c, s4=s4: e.matmul(
                        P(0), wqc.ch(c), xg.ch(c, s4 * 512, s4 * 512 + 512),
                        start=(c == 0), stop=(c == 15)),
                        waits=W(("dg", max(dq_, dxg)), ("vv", rdq[0]),
                                ("ss", sb1)), inc="tt")
                rdq[0] = S.on("v", lambda e, hp=hp, col=col: e.tensor_scalar(
                    out=Qh[:, col:col + 512], in0=P(0), scalar1=CQK, scalar2=b1qc[:, hp:hp + 1],
                    op0=ALU.mult, op1=ALU.add), waits=W(("tt", tq)), inc="vv")
                for c in range(16):
                    tk = S.on("t", lambda e, c=c, s4=s4: e.matmul(
                        P(1), wkc.ch(c), xg.ch(c, s4 * 512, s4 * 512 + 512),
                        start=(c == 0), stop=(c == 15)),
                        waits=W(("dg", max(dk_, dxg)), ("vv", rdq[1])), inc="tt")
                rdq[1] = S.on("v", lambda e, hp=hp, col=col: e.tensor_scalar(
                    out=Kh[:, col:col + 512], in0=P(1), scalar1=B.b1k[:, hp:hp + 1],
                    scalar2=None, op0=ALU.add), waits=W(("tt", tk)), inc="vv")
                for c in range(16):
                    tv = S.on("t", lambda e, c=c, s4=s4: e.matmul(
                        P(2), wvc.ch(c), xg.ch(c, s4 * 512, s4 * 512 + 512),
                        start=(c == 0), stop=(c == 15)),
                        waits=W(("dg", max(dv_, dxg)), ("vv", rdq[2])), inc="tt")
                rdq[2] = S.on("v", lambda e, hp=hp, col=col: e.tensor_scalar(
                    out=Vth[:, col:col + 512], in0=P(2), scalar1=B.b1v[:, hp:hp + 1],
                    scalar2=None, op0=ALU.add), waits=W(("tt", tv)), inc="vv")
        proj_vv = S.c("vv")
        PB6 = PS[:, 6 * 512:7 * 512].bitcast(BF16)
        PBS = [PB6, PB7]
        vtr = {0: proj_vv, 1: proj_vv}
        for kt in range(32):
            sl = kt % 2
            ttr = S.on("t", lambda e, kt=kt, sl=sl: e.transpose(
                PBS[sl][:, 0:128], Vth[:, kt * 128:(kt + 1) * 128], B.ident),
                waits=W(("vv", max(proj_vv, vtr[sl]))), inc="tt")
            vtr[sl] = S.on("v", lambda e, kt=kt, sl=sl: e.tensor_copy(
                Vtok[:, kt * 128:(kt + 1) * 128], PBS[sl][:, 0:128]),
                waits=W(("tt", ttr)), inc="vv")
        sqq = S.on("s", lambda e: e.activation(q2t, Qh, AF.Square), waits=W(("vv", proj_vv)),
                   inc="ss")
        vi1 = S.on("v", lambda e: e.memset(B.r11[0:1, 0:2], 0.0), inc="vv")
        mxv = vi1
        for part, src_ in ((0, "q"), (1, "k")):
            if part == 1:
                sqq = S.on("s", lambda e: e.activation(q2t, Kh, AF.Square),
                           waits=W(("tt", S.c("tt"))), inc="ss")
            for s8 in range(8):
                tsum = S.on("t", lambda e, s8=s8: e.matmul(
                    PR(6), B.ones_m1, q2t[:, s8 * 512:(s8 + 1) * 512],
                    start=True, stop=True),
                    waits=W(("ss", sqq), ("vv", max(mxv, vtr[0], vtr[1]))), inc="tt")
                vr = S.on("v", lambda e: e.reduce_max(B.r11[0:1, 6:7], PR(6),
                                                      axis=mybir.AxisListType.X),
                          waits=W(("tt", tsum)), inc="vv")
                mxv = S.on("v", lambda e, part=part: e.tensor_max(
                    B.r11[0:1, part:part + 1], B.r11[0:1, part:part + 1], B.r11[0:1, 6:7]),
                    inc="vv")
        s_c1 = S.on("s", lambda e: e.activation(B.r11[0:1, 2:3], B.r11[0:1, 0:1], AF.Sqrt),
                    waits=W(("vv", mxv)), inc="ss")
        s_c2 = S.on("s", lambda e: e.activation(B.r11[0:1, 3:4], B.r11[0:1, 1:2], AF.Sqrt),
                    inc="ss")
        v_c = S.on("v", lambda e: e.tensor_mul(B.r11[0:1, 4:5], B.r11[0:1, 2:3],
                                               B.r11[0:1, 3:4]),
                   waits=W(("ss", s_c2)), inc="vv")
        t_c = S.on("t", lambda e: e.matmul(PS[:, 6 * 512:6 * 512 + 1], B.ones_k1f,
                                           B.r11[0:1, 4:5], start=True, stop=True),
                   waits=W(("vv", v_c)), inc="tt")
        s_c3 = S.on("s", lambda e: e.activation(B.negC, PS[:, 6 * 512:6 * 512 + 1],
                                                AF.Copy, scale=-1.0),
                    waits=W(("tt", t_c)), inc="ss")
        pread = {i: ("vv", S.c("vv")) for i in range(4)}
        pslot_rd = {i: S.c("tt") for i in range(4)}
        av_rd = S.c("vv")
        zb_rd = S.c("vv")
        tE_rd = S.c("vv")
        tD_dma = S.c("dq")
        for s8 in range(8):
            ktmax = 4 * (s8 + 1)
            tsum = tav = 0
            for kt in range(ktmax):
                sl = kt % 4
                tS = S.on("t", lambda e, kt=kt, s8=s8, sl=sl: e.matmul(
                    P(sl), Kh[:, kt * 128:(kt + 1) * 128],
                    Qh[:, s8 * 512:(s8 + 1) * 512], start=True, stop=True),
                    waits=W(pread[sl], ("ss", s_c3)), inc="tt")
                boundary = kt >= 4 * s8
                if not boundary:
                    sE = S.on("s", lambda e, sl=sl: e.activation(
                        Psl[sl], P(sl), AF.Exp, bias=B.negC),
                        waits=W(("tt", max(tS, pslot_rd[sl]))), inc="ss")
                    pr = ("ss", sE)
                    pread[sl] = ("ss", sE)
                else:
                    off = kt * 128 - s8 * 512
                    sE = S.on("s", lambda e, sl=sl: e.activation(
                        B.tE, P(sl), AF.Exp, bias=B.negC),
                        waits=W(("tt", tS), ("vv", tE_rd)), inc="ss")
                    pread[sl] = ("ss", sE)
                    vE = S.on("v", lambda e, sl=sl, off=off: e.tensor_mul(
                        Psl[sl], B.tE, B.mstrip[:, 384 - off:384 - off + 512]),
                        waits=W(("ss", sE), ("tt", pslot_rd[sl])), inc="vv")
                    tE_rd = vE
                    pr = ("vv", vE)
                tsum = S.on("t", lambda e, sl=sl, kt=kt, ktmax=ktmax: e.matmul(
                    PR(4), B.ones_m1, Psl[sl], start=(kt == 0), stop=(kt == ktmax - 1)),
                    waits=W(pr, ("vv", zb_rd)), inc="tt")
                tav = S.on("t", lambda e, sl=sl, kt=kt, ktmax=ktmax: e.matmul(
                    P(5), Vtok[:, kt * 128:(kt + 1) * 128], Psl[sl],
                    start=(kt == 0), stop=(kt == ktmax - 1)),
                    waits=W(("vv", av_rd)), inc="tt")
                pslot_rd[sl] = tav
            vz1 = S.on("v", lambda e: e.tensor_copy(r_sum, PR(4)), waits=W(("tt", tsum)),
                       inc="vv")
            zb_rd = vz1
            vz2 = S.on("v", lambda e: e.reciprocal(r_sum, r_sum), inc="vv")
            tzb = S.on("t", lambda e: e.matmul(P(6), B.ones_k1f, r_sum, start=True, stop=True),
                       waits=W(("vv", vz2)), inc="tt")
            vz3 = S.on("v", lambda e: e.tensor_copy(B.mb, P(6)), waits=W(("tt", tzb)),
                       inc="vv")
            av_rd = S.on("v", lambda e: e.tensor_mul(B.tD, P(5), B.mb),
                         waits=W(("tt", tav), ("vv", vz3), ("dq", tD_dma)), inc="vv")
            tD_dma = S.dma("q", ext["attc"][hp * 128:(hp + 1) * 128,
                                            s8 * 512:(s8 + 1) * 512],
                           B.tD, waits=W(("vv", av_rd)))
        head_tt = S.c("tt")
        head_vv = S.c("vv")
    c3 = S.on("g", lambda e: e.collective_compute(
        "AllGather", ALU.bypass, replica_groups=GROUPS,
        ins=[ext["attc"][:]], outs=[ext["attg"][:]]),
        waits=W(("dq", S.c("dq"))), inc="c3")

    S.fence()
    dix = S.dma("g", idxE, ext["idxE"], waits=W(("ss", S.c("ss")), ("vv", S.c("vv"))))
    swv = {4: S.c("ss"), 5: S.c("ss")}
    frd = {0: S.c("tt"), 1: S.c("tt")}
    prd = {0: S.c("tt"), 1: S.c("tt")}
    ord4 = {0: S.c("tt"), 1: S.c("tt")}
    jrd = {6: S.c("vv"), 7: S.c("vv")}
    vb_p = 0
    dx2_p = 0
    tsq_p = S.c("tt")
    hrd = S.c("tt")
    for sl in range(4):
        dxs = S.dma("q", xslab4.all3(), slab_ap(ext["xn1T"], sl * 512, sl * 512 + 512),
                    waits=W(("tt", S.c("tt")), ("ss", S.c("ss")), ("vv", S.c("vv"))))
        tcs = tsq = 0
        for c in range(16):
            tcs = S.on("t", lambda e, c=c: e.matmul(PR(0), B.ones_m1, xslab4.ch(c),
                                                    start=(c == 0), stop=(c == 15)),
                       waits=W(("dq", dxs)), inc="tt")
        for c in range(16):
            sqv = S.on("s", lambda e, c=c: e.activation(B.sqt[c % 2], xslab4.ch(c), AF.Square),
                       waits=W(("dq", dxs), ("tt", tsq)), inc="ss")
            tsq = S.on("t", lambda e, c=c: e.matmul(PR(1), B.ones_m1, B.sqt[c % 2],
                                                    start=(c == 0), stop=(c == 15)),
                       waits=W(("ss", sqv)), inc="tt")
        sm1 = S.on("s", lambda e: e.activation(r_mean, PR(0), AF.Copy, scale=1.0 / D),
                   waits=W(("tt", tcs)), inc="ss")
        sm2 = S.on("s", lambda e: e.activation(r_msq, PR(1), AF.Copy, scale=1.0 / D),
                   waits=W(("tt", tsq)), inc="ss")
        v1 = S.on("v", lambda e: e.tensor_mul(r_t1, r_mean, r_mean), waits=W(("ss", sm2)),
                  inc="vv")
        v2 = S.on("v", lambda e: e.tensor_sub(r_t1, r_msq, r_t1), inc="vv")
        S.on("s", lambda e: e.activation(r_t1, r_t1, AF.Ln, bias=EPS_B),
             waits=W(("vv", v2)), inc="ss")
        sv2 = S.on("s", lambda e: e.activation(r_rstd, r_t1, AF.Exp, scale=-0.5), inc="ss")
        tb1 = S.on("t", lambda e: e.matmul(P(2), B.ones_k1f, r_mean, start=True, stop=True),
                   waits=W(("ss", sm1), ("vv", S.c("vv"))), inc="tt")
        tb2 = S.on("t", lambda e: e.matmul(P(3), B.ones_k1f, r_rstd, start=True, stop=True),
                   waits=W(("ss", sv2)), inc="tt")
        vb1 = S.on("v", lambda e: e.tensor_copy(B.mb, P(2)), waits=W(("tt", tb1)), inc="vv")
        vb2 = S.on("v", lambda e: e.tensor_copy(B.rb, P(3)), waits=W(("tt", tb2)), inc="vv")
        vxn = 0
        for c in range(16):
            S.on("v", lambda e, c=c: e.tensor_sub(B.tmpf[:, 0:512], xslab4.ch(c), B.mb),
                 waits=W(("vv", vb2)), inc="vv")
            vxn = S.on("v", lambda e, c=c: e.tensor_mul(xnf.ch(c), B.tmpf[:, 0:512], B.rb),
                       inc="vv")
        for ot in range(NH1):
            bsel = ot % 2
            pb = 4 + ot % 2
            dwf = S.dma("g", wfc_col[bsel].all3(), colload_ap(ext["whatfc"], ot),
                        waits=W(("tt", frd[bsel])))
            th = 0
            for c in range(16):
                th = S.on("t", lambda e, c=c, bsel=bsel, pb=pb: e.matmul(
                    P(pb), wfc_col[bsel].ch(c), xnf.ch(c),
                    start=(c == 0), stop=(c == 15)),
                    waits=W(("dg", dwf), ("vv", vxn), ("ss", swv[pb])), inc="tt")
            frd[bsel] = th
            swv[pb] = S.on("s", lambda e, ot=ot, pb=pb: e.activation(
                harena.ch(ot), P(pb), AF.Gelu), waits=W(("tt", th)), inc="ss")
        tcs = tsq = 0
        for ot in range(NH1):
            tcs = S.on("t", lambda e, ot=ot: e.matmul(PR(0), B.ones_m1, harena.ch(ot),
                                                      start=(ot == 0), stop=(ot == NH1 - 1)),
                       waits=W(("ss", swv[4] if ot == 0 else 0), ("ss", swv[5])), inc="tt")
        for ot in range(NH1):
            sqv = S.on("s", lambda e, ot=ot: e.activation(B.sqt[ot % 2], harena.ch(ot),
                                                          AF.Square),
                       waits=W(("tt", tsq)), inc="ss")
            tsq = S.on("t", lambda e, ot=ot: e.matmul(PR(1), B.ones_m1, B.sqt[ot % 2],
                                                      start=(ot == 0), stop=(ot == NH1 - 1)),
                       waits=W(("ss", sqv)), inc="tt")
        sm1 = S.on("s", lambda e: e.activation(r_mean, PR(0), AF.Copy, scale=1.0 / HID1),
                   waits=W(("tt", tcs)), inc="ss")
        sm2 = S.on("s", lambda e: e.activation(r_msq, PR(1), AF.Copy, scale=1.0 / HID1),
                   waits=W(("tt", tsq)), inc="ss")
        v1 = S.on("v", lambda e: e.tensor_mul(r_t1, r_mean, r_mean), waits=W(("ss", sm2)),
                  inc="vv")
        v2 = S.on("v", lambda e: e.tensor_sub(r_t1, r_msq, r_t1), inc="vv")
        S.on("s", lambda e: e.activation(r_t1, r_t1, AF.Ln, bias=EPS_B),
             waits=W(("vv", v2)), inc="ss")
        sv2 = S.on("s", lambda e: e.activation(r_rstd, r_t1, AF.Exp, scale=-0.5), inc="ss")
        tb1 = S.on("t", lambda e: e.matmul(P(2), B.ones_k1f, r_mean, start=True, stop=True),
                   waits=W(("ss", sm1), ("vv", S.c("vv"))), inc="tt")
        tb2 = S.on("t", lambda e: e.matmul(P(3), B.ones_k1f, r_rstd, start=True, stop=True),
                   waits=W(("ss", sv2)), inc="tt")
        vb1 = S.on("v", lambda e: e.tensor_copy(B.mb, P(2)), waits=W(("tt", tb1)), inc="vv")
        vb2 = S.on("v", lambda e: e.tensor_copy(B.rb, P(3)), waits=W(("tt", tb2)), inc="vv")
        hnorm = 0
        for ot in range(NH1):
            S.on("v", lambda e, ot=ot: e.tensor_sub(B.tmpf[:, 0:512], harena.ch(ot), B.mb),
                 waits=W(("vv", vb2)), inc="vv")
            hnorm = S.on("v", lambda e, ot=ot: e.tensor_mul(harena.ch(ot), B.tmpf[:, 0:512],
                                                            B.rb), inc="vv")
        S.on("g", lambda e: e.nop(), waits=W(("c3", 1)))
        dga = 0
        attg_v = ext["attg"].rearrange("e (b n) -> (e b) n", n=512)
        for c in range(16):
            dga = S.on("g", lambda e, c=c, sl=sl: e.indirect_dma_start(
                out=attE.ch(c), out_offset=None, in_=attg_v,
                in_offset=bass.IndirectOffsetOnAxis(idxE[:, c * 4 + sl:c * 4 + sl + 1], 0)),
                waits=W(("dg", dix), ("tt", hrd)), inc="dg", incn=16)
        for dt in range(16):
            pb = 6 + dt % 2
            hsel = dt % 2
            dwpA = S.dma("g", wpjh[0].all3(), colload_half(ext["whatpj"], dt, 0),
                         waits=W(("tt", prd[0])))
            dwo = S.dma("g", wo_c4[hsel].all3(), colload_ap(ext["wo1T"], dt),
                        waits=W(("tt", ord4[hsel])))
            dwpB = S.dma("g", wpjh[1].all3(), colload_half(ext["whatpj"], dt, 1),
                         waits=W(("tt", prd[1])))
            tj = 0
            for c in range(16):
                tj = S.on("t", lambda e, c=c, pb=pb, hsel=hsel: e.matmul(
                    P(pb), wo_c4[hsel].ch(c), attE.ch(c), start=(c == 0), stop=False),
                    waits=W(("dg", max(dwo, dga)), ("vv", jrd[pb])), inc="tt")
            ord4[hsel] = tj
            for hc in range(NH1):
                tj = S.on("t", lambda e, hc=hc, pb=pb: e.matmul(
                    P(pb), wpjh[hc // 32].ch(hc % 32), harena.ch(hc),
                    start=False, stop=(hc == NH1 - 1)),
                    waits=W(("dg", dwpA if hc < 32 else dwpB), ("vv", hnorm)), inc="tt")
                if hc == 31:
                    prd[0] = tj
            prd[1] = tj
            hrd = tj
            vo = S.on("v", lambda e, dt=dt, pb=pb: e.tensor_scalar(
                out=B.tA, in0=P(pb), scalar1=B.bo1_col[:, dt:dt + 1], scalar2=None,
                op0=ALU.add), waits=W(("tt", tj), ("dg", dbo), ("vv", vb_p),
                                      ("dq", dx2_p)), inc="vv")
            jrd[pb] = vo
            dx1_ = S.dma("q", B.tB, ext["x1T"][dt * 128:(dt + 1) * 128,
                                               sl * 512:sl * 512 + 512],
                         waits=W(("vv", vb_p)))
            vb_p = S.on("v", lambda e: e.tensor_add(B.tC, B.tA, B.tB),
                        waits=W(("dq", dx1_)), inc="vv")
            dx2_p = S.dma("q", ext["x2T"][dt * 128:(dt + 1) * 128, sl * 512:sl * 512 + 512],
                          B.tC.bitcast(F32) if False else B.tC, waits=W(("vv", vb_p)))
    if ext.get("DEBUG"):
        S.dma("g", ext["dbg_gam"][:, 0:4], B.gcol,
              waits=W(("vv", S.c("vv")), ("ss", S.c("ss"))))
        S.dma("g", ext["dbg_gam"][0:1, 4:8], B.r11[0:1, 0:4])
        S.dma("g", ext["dbg_wf"], ext["whatfc"][0:128, :])
        S.dma("g", ext["dbg_attc"], ext["attc"][:, :])
        S.dma("g", ext["dbg_q"], Qh)
        S.dma("g", ext["dbg_h"], harena.ap[:, 0:NH1 * 512])
        S.dma("g", ext["dbg_xnf"], xnf.ap[:, 0:NC * 512])
        S.dma("g", ext["dbg_atE"], attE.ap[:, 0:NC * 512])
    S.on("q", lambda e: e.nop(), waits=W(("dq", S.c("dq")), ("dg", S.c("dg"))))


def _l1_inmaps(inputs, l0_results):
    C = np.ascontiguousarray
    wqT = C(np.asarray(inputs["L1_attn_wq"]).T)
    wkT = C(np.asarray(inputs["L1_attn_wk"]).T)
    wvT = C(np.asarray(inputs["L1_attn_wv"]).T)
    woT = C(np.asarray(inputs["L1_attn_wo"]).T)
    bq = np.asarray(inputs["L1_attn_bq"])
    bk = np.asarray(inputs["L1_attn_bk"])
    bv = np.asarray(inputs["L1_attn_bv"])
    bo = np.asarray(inputs["L1_attn_bo"])
    wfcT = C(np.asarray(inputs["L1_ffn_w_fc"]).T)
    wpjT = C(np.asarray(inputs["L1_ffn_w_proj"]).T)
    col = lambda v, n: C(v.reshape(n, 128).T.astype(np.float32))
    maps = []
    for core in range(8):
        h = core % 2
        hs = h * 1024
        p = np.arange(128)[:, None]
        cs = np.arange(64)[None, :]
        idxE = ((cs // 4) * 128 + p) * 8 + h * 4 + (cs % 4)
        maps.append({
            **({"xn1T": np.asarray(l0_results[core]["xn1T"]),
                "x1T": np.asarray(l0_results[core]["x1T"])} if l0_results is not None else {}),
            "wq1T": C(wqT[:, hs:hs + 1024]), "b1q": col(bq[hs:hs + 1024], 8),
            "wk1T": C(wkT[:, hs:hs + 1024]), "b1k": col(bk[hs:hs + 1024], 8),
            "wv1T": C(wvT[:, hs:hs + 1024]), "b1v": col(bv[hs:hs + 1024], 8),
            "wo1T": woT, "bo1_col": col(bo, 16),
            "wfcT": wfcT, "wpjT": wpjT,
            "idxE": idxE.astype(np.int32),
        })
    return maps


def _build_l1_nc():
    nc = bass.Bass("TRN2", target_bir_lowering=False, debug=False)
    ext = {}
    EI = lambda n, s, d=F32: nc.dram_tensor(n, s, d, kind="ExternalInput").ap()
    ext["xn1T"] = EI("xn1T", [D, NOWN], BF16)
    ext["x1T"] = EI("x1T", [D, NOWN])
    ext["wq1T"] = EI("wq1T", [D, 1024])
    ext["b1q"] = EI("b1q", [128, 8])
    ext["wk1T"] = EI("wk1T", [D, 1024])
    ext["b1k"] = EI("b1k", [128, 8])
    ext["wv1T"] = EI("wv1T", [D, 1024])
    ext["b1v"] = EI("b1v", [128, 8])
    ext["wo1T"] = EI("wo1T", [D, D])
    ext["bo1_col"] = EI("bo1_col", [128, 16])
    ext["wfcT"] = EI("wfcT", [D, HID1])
    ext["wpjT"] = EI("wpjT", [HID1, D])
    ext["idxE"] = EI("idxE", [128, 64], I32)
    ext["x2T"] = nc.dram_tensor("x2T", [D, NOWN], F32, kind="ExternalOutput").ap()
    ext["DEBUG"] = DEBUG_L1
    if DEBUG_L1:
        ext["dbg_gam"] = nc.dram_tensor("dbg_gam", [128, 8], F32, kind="ExternalOutput").ap()
        ext["dbg_wf"] = nc.dram_tensor("dbg_wf", [128, HID1], F32, kind="ExternalOutput").ap()
        ext["dbg_attc"] = nc.dram_tensor("dbg_attc", [1024, NFULL], F32, kind="ExternalOutput").ap()
        ext["dbg_q"] = nc.dram_tensor("dbg_q", [128, NFULL], F32, kind="ExternalOutput").ap()
        ext["dbg_h"] = nc.dram_tensor("dbg_h", [128, NH1 * 512], F32, kind="ExternalOutput").ap()
        ext["dbg_xnf"] = nc.dram_tensor("dbg_xnf", [128, NC * 512], F32, kind="ExternalOutput").ap()
        ext["dbg_atE"] = nc.dram_tensor("dbg_atE", [128, NC * 512], F32, kind="ExternalOutput").ap()
    ext["xn1c"] = nc.dram_tensor("xn1c", [D, NOWN], BF16, kind="Internal").ap()
    ext["xn1g"] = nc.dram_tensor("xn1g", [2 * D, NOWN], BF16, kind="Internal").ap()
    ext["attc"] = nc.dram_tensor("attc", [1024, NFULL], BF16, kind="Internal").ap()
    ext["attg"] = nc.dram_tensor("attg", [2048, NFULL], BF16, kind="Internal").ap()
    ext["whatfc"] = nc.dram_tensor("whatfc", [D, HID1], BF16, kind="Internal").ap()
    ext["whatpj"] = nc.dram_tensor("whatpj", [HID1, D], BF16, kind="Internal").ap()

    B = Buf(nc)
    with nc.psum_tensor("PS", [128, 4096], F32) as PS:
        S = Sched(nc)
        build_common(S, B, nc)
        build_l1(S, B, ext, PS[:])
        S.finalize()
    return nc

def _l0_inmaps(inputs):
    x = np.asarray(inputs["L0_attn_x"])
    C = np.ascontiguousarray
    wqT = C(np.asarray(inputs["L0_attn_Wq"]).T)
    wkT = C(np.asarray(inputs["L0_attn_Wk"]).T)
    wvT = C(np.asarray(inputs["L0_attn_Wv"]).T)
    woT = C(np.asarray(inputs["L0_attn_Wo"]).T)
    bq = np.asarray(inputs["L0_attn_bq"])
    bk = np.asarray(inputs["L0_attn_bk"])
    bv = np.asarray(inputs["L0_attn_bv"])
    bo = np.asarray(inputs["L0_attn_bo"])
    wqgT = C(np.asarray(inputs["L0_ffn_wq_gate"]).T)
    wquT = C(np.asarray(inputs["L0_ffn_wq_up"]).T)
    wqdT = C(np.asarray(inputs["L0_ffn_wq_down"]).T)
    sg = np.asarray(inputs["L0_ffn_s_gate"])
    su = np.asarray(inputs["L0_ffn_s_up"])
    sd = np.asarray(inputs["L0_ffn_s_down"])
    col = lambda v, n: C(v.reshape(n, 128).T.astype(np.float32))
    maps = []
    for core in range(8):
        b, h = core // 2, core % 2
        xT = x[b].T
        own = xT[:, h * 2048:(h + 1) * 2048]
        oth = xT[:, (1 - h) * 2048:(2 - h) * 2048]
        hs = h * 1024
        maps.append({
            "x0T": C(np.concatenate([own, oth], axis=1)),
            "pos2": np.array([[h * 2048.0, (1 - h) * 2048.0]], np.float32),
            "wqT": wqT, "bq_col": col(bq, 16),
            "wkT_oh": C(wkT[:, hs:hs + 1024]), "bk_row": C(bk[None, hs:hs + 1024]),
            "wvT_oh": C(wvT[:, hs:hs + 1024]), "bv_row": C(bv[None, hs:hs + 1024]),
            "woT": woT, "bo_col": col(bo, 16),
            "wqgT": wqgT, "sg_col": col(sg, 44),
            "wquT": wquT, "su_col": col(su, 44),
            "wqdT": wqdT, "sd_col": col(sd, 16),
        })
    return maps


def _build_l0_nc():
    nc = bass.Bass("TRN2", target_bir_lowering=False, debug=False)
    ext = {}
    EI = lambda n, s, d=F32: nc.dram_tensor(n, s, d, kind="ExternalInput").ap()
    ext["x0T"] = EI("x0T", [D, NFULL])
    ext["pos2"] = EI("pos2", [1, 2])
    ext["wqT"] = EI("wqT", [D, D])
    ext["bq_col"] = EI("bq_col", [128, 16])
    ext["wkT_oh"] = EI("wkT_oh", [D, 1024])
    ext["bk_row"] = EI("bk_row", [1, 1024])
    ext["wvT_oh"] = EI("wvT_oh", [D, 1024])
    ext["bv_row"] = EI("bv_row", [1, 1024])
    ext["woT"] = EI("woT", [D, D])
    ext["bo_col"] = EI("bo_col", [128, 16])
    ext["wqgT"] = EI("wqgT", [D, HID0], I32)
    ext["sg_col"] = EI("sg_col", [128, 44])
    ext["wquT"] = EI("wquT", [D, HID0], I32)
    ext["su_col"] = EI("su_col", [128, 44])
    ext["wqdT"] = EI("wqdT", [HID0, D], I32)
    ext["sd_col"] = EI("sd_col", [128, 16])
    ext["x1T"] = nc.dram_tensor("x1T", [D, NOWN], F32, kind="ExternalOutput").ap()
    ext["xn1T"] = nc.dram_tensor("xn1T", [D, NOWN], BF16, kind="ExternalOutput").ap()
    ext["dbg_kv"] = nc.dram_tensor("dbg_kv", [128, 2048], F32, kind="ExternalOutput").ap()
    ext["dbg_ks"] = nc.dram_tensor("dbg_ks", [1, 2048], F32, kind="ExternalOutput").ap()
    ext["dbg_at"] = nc.dram_tensor("dbg_at", [D, NOWN], F32, kind="ExternalOutput").ap()
    ext["dbg_f0"] = nc.dram_tensor("dbg_f0", [D, NOWN], F32, kind="ExternalOutput").ap()
    ext["dbg_q"] = nc.dram_tensor("dbg_q", [D, NOWN], F32, kind="ExternalOutput").ap()
    ext["kvc"] = nc.dram_tensor("kvc", [1032, 128], F32, kind="Internal").ap()
    ext["kvg"] = nc.dram_tensor("kvg", [2064, 128], F32, kind="Internal").ap()
    ext["f0s"] = nc.dram_tensor("f0s", [D, NOWN], BF16, kind="Internal").ap()

    ext["DEBUG"] = DEBUG_L0
    B = Buf(nc)
    with nc.psum_tensor("PS", [128, 4096], F32) as PS:
        S = Sched(nc)
        build_common(S, B, nc)
        build_l0(S, B, ext, PS[:])
        S.finalize()
    return nc


def _build_fused_nc():
    nc = bass.Bass("TRN2", target_bir_lowering=False, debug=False)
    ext = {}
    EI = lambda n, s, d=F32: nc.dram_tensor(n, s, d, kind="ExternalInput").ap()
    IN = lambda n, s, d=F32: nc.dram_tensor(n, s, d, kind="Internal").ap()
    ext["x0T"] = EI("x0T", [D, NFULL])
    ext["pos2"] = EI("pos2", [1, 2])
    ext["wqT"] = EI("wqT", [D, D])
    ext["bq_col"] = EI("bq_col", [128, 16])
    ext["wkT_oh"] = EI("wkT_oh", [D, 1024])
    ext["bk_row"] = EI("bk_row", [1, 1024])
    ext["wvT_oh"] = EI("wvT_oh", [D, 1024])
    ext["bv_row"] = EI("bv_row", [1, 1024])
    ext["woT"] = EI("woT", [D, D])
    ext["bo_col"] = EI("bo_col", [128, 16])
    ext["wqgT"] = EI("wqgT", [D, HID0], I32)
    ext["sg_col"] = EI("sg_col", [128, 44])
    ext["wquT"] = EI("wquT", [D, HID0], I32)
    ext["su_col"] = EI("su_col", [128, 44])
    ext["wqdT"] = EI("wqdT", [HID0, D], I32)
    ext["sd_col"] = EI("sd_col", [128, 16])
    ext["wq1T"] = EI("wq1T", [D, 1024])
    ext["b1q"] = EI("b1q", [128, 8])
    ext["wk1T"] = EI("wk1T", [D, 1024])
    ext["b1k"] = EI("b1k", [128, 8])
    ext["wv1T"] = EI("wv1T", [D, 1024])
    ext["b1v"] = EI("b1v", [128, 8])
    ext["wo1T"] = EI("wo1T", [D, D])
    ext["bo1_col"] = EI("bo1_col", [128, 16])
    ext["wfcT"] = EI("wfcT", [D, HID1])
    ext["wpjT"] = EI("wpjT", [HID1, D])
    ext["idxE"] = EI("idxE", [128, 64], I32)
    ext["x2T"] = nc.dram_tensor("x2T", [D, NOWN], F32, kind="ExternalOutput").ap()
    ext["x1T"] = IN("x1s", [D, NOWN])
    ext["xn1T"] = IN("xn1s", [D, NOWN], BF16)
    ext["kvc"] = IN("kvc", [1032, 128])
    ext["kvg"] = IN("kvg", [2064, 128])
    ext["f0s"] = IN("f0s", [D, NOWN], BF16)
    ext["gus"] = IN("gus", [HID0, NOWN], BF16)
    ext["xn1c"] = IN("xn1c", [D, NOWN], BF16)
    ext["xn1g"] = IN("xn1g", [2 * D, NOWN], BF16)
    ext["attc"] = IN("attc", [1024, NFULL], BF16)
    ext["attg"] = IN("attg", [2048, NFULL], BF16)
    ext["whatfc"] = IN("whatfc", [D, HID1], BF16)
    ext["whatpj"] = IN("whatpj", [HID1, D], BF16)
    ext["DEBUG"] = False

    B = Buf(nc)
    with nc.psum_tensor("PS", [128, 4096], F32) as PS:
        S = Sched(nc)
        build_common(S, B, nc)
        build_l0(S, B, ext, PS[:])
        build_l1(S, B, ext, PS[:])
        S.finalize()
    return nc


FUSED = True
_NC_CACHE = {}


def kernel(**inputs):
    maps0 = _l0_inmaps(inputs)
    if FUSED:
        maps1 = _l1_inmaps(inputs, None)
        maps = [dict(m0, **m1) for m0, m1 in zip(maps0, maps1)]
        if "fused" not in _NC_CACHE:
            _NC_CACHE["fused"] = _build_fused_nc()
        res1 = run_bass_kernel_spmd(_NC_CACHE["fused"], maps, core_ids=list(range(8)))
    else:
        nc0 = _build_l0_nc()
        res0 = run_bass_kernel_spmd(nc0, maps0, core_ids=list(range(8)))
        maps1 = _l1_inmaps(inputs, res0.results)
        nc1 = _build_l1_nc()
        res1 = run_bass_kernel_spmd(nc1, maps1, core_ids=list(range(8)))
    outs = []
    for b in range(NB):
        xa = np.asarray(res1.results[2 * b]["x2T"], np.float32)
        xb_ = np.asarray(res1.results[2 * b + 1]["x2T"], np.float32)
        outs.append(np.concatenate([xa, xb_], axis=1).T)
    return np.stack(outs).astype(np.float32)


if __name__ == "__main__":
    nc = _build_l0_nc()
    n = sum(len(v) for v in Sched(bass.Bass("TRN2", target_bir_lowering=False)).ops.values())
    print("L0 build OK")


# version 18
# speedup vs baseline: 1.2268x; 1.0554x over previous
import sys
sys.path.insert(0, "/opt/trn_rl_repo")
import numpy as np
import ml_dtypes
import concourse.bass as bass
import concourse.mybir as mybir
from concourse.bass_utils import run_bass_kernel_spmd

F32 = mybir.dt.float32
BF16 = mybir.dt.bfloat16
I32 = mybir.dt.int32
U8 = mybir.dt.uint8
AF = mybir.ActivationFunctionType
ALU = mybir.AluOpType

D = 2048
NC = 16
NOWN = 2048
NFULL = 4096
NB = 4
HID0 = 5632
NH0 = 44
HID1 = 8192
NH1 = 64
EPS_LN = 1e-5
EPS_Z = 1e-6
EPS_B = 1e-6
CQK = float(1.0 / np.sqrt(128.0))
GROUPS = [[0, 1], [2, 3], [4, 5], [6, 7]]
DEBUG_L0 = False
DEBUG_L1 = False
N_W_FC = float(HID1 * D)


class Sched:
    ENGS = ("t", "v", "s", "g", "q")

    def __init__(self, nc):
        self.nc = nc
        self.ops = {e: [] for e in self.ENGS}
        self.cnt = {}
        self.sems = {}
        self.lastw = {e: {} for e in self.ENGS}
        self._ctx = []

    def sem(self, name):
        if name not in self.sems:
            g = self.nc.semaphore(name)
            self.sems[name] = g.__enter__()
            self._ctx.append(g)
            self.cnt[name] = 0
        return self.sems[name]

    def on(self, e, fn, waits=(), inc=None, incn=1):
        wl = []
        for sname, v in waits:
            if v <= 0:
                continue
            self.sem(sname)
            if self.lastw[e].get(sname, -1) >= v:
                continue
            self.lastw[e][sname] = v
            wl.append((self.sems[sname], v))
        ih = self.sem(inc) if inc else None

        def run(eng, fn=fn, wl=tuple(wl), ih=ih, incn=incn):
            for s, v in wl:
                eng.wait_ge(s, v)
            r = fn(eng)
            if ih is not None:
                r.then_inc(ih, incn)
        self.ops[e].append(run)
        if inc:
            self.cnt[inc] += incn
            return self.cnt[inc]
        return None

    def dma(self, e, out, in_, waits=()):
        sem = "dg" if e == "g" else "dq"
        self.sem(sem)
        return self.on(e, lambda eng, o=out, i=in_: eng.dma_start(out=o, in_=i),
                       waits, inc=sem, incn=16)

    def c(self, name):
        return self.cnt.get(name, 0)

    def fence(self, engines="tvsgq"):
        vals = [(s, self.cnt[s]) for s in list(self.cnt)]
        for e in engines:
            self.on(e, lambda eng: eng.nop(), waits=vals)

    def finalize(self):
        nc = self.nc
        with nc.Block() as blk:
            @blk.tensor
            def _(t):
                for f in self.ops["t"]:
                    f(t)

            @blk.vector
            def _(v):
                for f in self.ops["v"]:
                    f(v)

            @blk.scalar
            def _(s):
                for f in self.ops["s"]:
                    f(s)

            @blk.gpsimd
            def _(g):
                for f in self.ops["g"]:
                    f(g)

            @blk.sync
            def _(q):
                for f in self.ops["q"]:
                    f(q)


def colload_half(wT, t, half):
    K = wT.shape[0]
    return wT[half * K // 2:(half + 1) * K // 2,
              t * 128:(t + 1) * 128].rearrange("(c p) o -> p c o", p=128)


def colload_ap(wT, t, w=128):
    return wT[:, t * w:(t + 1) * w].rearrange("(c p) o -> p c o", p=128)


def slab_ap(wT, c0, c1):
    return wT[:, c0:c1].rearrange("(c p) o -> p c o", p=128)


class FM:

    def __init__(self, ap, nchunk, w):
        self.ap, self.nchunk, self.w = ap, nchunk, w

    def ch(self, c, a=0, b=None):
        b = self.w if b is None else b
        return self.ap[:, c * self.w + a: c * self.w + b]

    def all3(self):
        return self.ap[:, 0:self.nchunk * self.w].rearrange("p (c n) -> p c n", n=self.w)


class Buf:

    def __init__(self, nc):
        A = lambda name, shape, dt: nc.alloc_sbuf_tensor(name, shape, dt).ap()
        self.AR1 = A("sb_AR1", [128, 65536], U8)
        self.AR3 = A("sb_AR3", [128, 65536], U8)
        self.ST = A("sb_ST", [128, 34816], U8)
        self.ROWS = A("sb_rows", [1, 4096], F32)
        self.drow = A("sb_drow", [1, 2048], BF16)
        self.ksum_sb = A("sb_ksum_sb", [1, 2048], BF16)
        self.kv16 = A("sb_kv16", [128, 2048], BF16)
        self.tmpf = A("sb_tmpf", [128, 512], F32)
        self.mb = A("sb_mb", [128, 512], F32)
        self.rb = A("sb_rb", [128, 512], F32)
        self.sqt = [A("sb_sqt0", [128, 512], BF16), A("sb_sqt1", [128, 512], BF16)]
        self.mstrip = A("sb_mstrip", [128, 896], BF16)
        self.ident = A("sb_ident", [128, 128], BF16)
        self.cst = A("sb_cst", [128, 8], F32)
        self.ones_k1 = A("sb_ones_k1", [1, 128], BF16)
        self.n128_k1 = A("sb_n128_k1", [1, 128], BF16)
        self.ones_m1 = A("sb_ones_m1", [128, 1], BF16)
        self.ones_k1f = A("sb_ones_k1f", [1, 128], F32)
        self.ones_m1f = A("sb_ones_m1f", [128, 1], F32)
        self.posf = A("sb_posf", [128, 16], F32)
        self.decay_col = A("sb_decay_col", [128, 32], F32)
        self.poso = A("sb_poso", [1, 2], F32)
        self.offb = A("sb_offb", [1, 2], F32)
        self.bq_col = A("sb_bq_col", [128, 16], F32)
        self.bo_col = A("sb_bo_col", [128, 16], F32)
        self.sg_col = A("sb_sg_col", [128, 44], F32)
        self.su_col = A("sb_su_col", [128, 44], F32)
        self.sd_col = A("sb_sd_col", [128, 16], F32)
        self.bk_row = A("sb_bk_row", [1, 1024], BF16)
        self.bv_row = A("sb_bv_row", [1, 1024], BF16)
        self.b1q = A("sb_b1q", [128, 8], F32)
        self.b1k = A("sb_b1k", [128, 8], F32)
        self.b1v = A("sb_b1v", [128, 8], F32)
        self.bo1_col = A("sb_bo1_col", [128, 16], F32)
        self.gam = A("sb_gam", [1, 4], F32)
        self.gcol = A("sb_gcol", [128, 4], F32)
        self.acc = A("sb_acc", [128, 2], F32)
        self.r11 = A("sb_r11", [1, 8], F32)
        self.negC = A("sb_negC", [128, 1], F32)
        AR3f = self.AR3.bitcast(F32)
        AR3b_ = self.AR3.bitcast(BF16)
        self.tA = AR3f[:, 13824:14336]
        self.tB = AR3f[:, 14336:14848]
        self.tC = AR3f[:, 14848:15360]
        self.tD = AR3b_[:, 30720:31232]
        self.tE = AR3b_[:, 31232:31744]


def bf(ap):
    return ap.bitcast(BF16)


def f32(ap):
    return ap.bitcast(F32)


def build_common(S, B, nc):
    CONSTS = [EPS_LN, EPS_Z, EPS_B, -128.0, -0.25, 0.5, 1.0, 0.0]
    for j, cv in enumerate(CONSTS):
        S.on("g", lambda e, j=j, cv=cv: e.memset(B.cst[:, j:j + 1], cv))
        nc.const_aps.register(F32, cv, B.cst[:, j:j + 1])
    S.on("g", lambda e: e.memset(B.ones_k1, 1.0))
    S.on("g", lambda e: e.memset(B.n128_k1, -128.0))
    S.on("g", lambda e: e.memset(B.ones_m1, 1.0))
    S.on("g", lambda e: e.memset(B.ones_k1f, 1.0))
    S.on("g", lambda e: e.memset(B.ones_m1f, 1.0))
    S.on("g", lambda e: e.memset(B.ident, 0.0))
    S.on("g", lambda e: e.affine_select(out=B.ident, in_=B.ident, compare_op=ALU.not_equal,
                                        fill=1.0, base=0, pattern=[[-1, 128]],
                                        channel_multiplier=1))
    S.on("g", lambda e: e.memset(B.mstrip, 1.0))
    S.on("g", lambda e: e.affine_select(out=B.mstrip, in_=B.mstrip, compare_op=ALU.is_ge,
                                        fill=0.0, base=-384, pattern=[[1, 896]],
                                        channel_multiplier=-1))


def build_l0(S, B, ext, PS):
    nc = S.nc
    P = lambda b, n=1: PS[:, b * 512:(b + n) * 512]
    PR = lambda b, w=512: PS[0:1, b * 512:b * 512 + w]
    W = lambda *a: list(a)

    xn0 = FM(bf(B.AR1), NC, NOWN)
    xnO = FM(bf(B.AR3), NC, NOWN)
    guA = FM(bf(B.AR3), NH0, 512)
    qA = FM(bf(B.AR3), NC, NOWN)
    atA = FM(bf(B.AR1), NC, NOWN)

    STb = bf(B.ST)
    xslab = FM(STb[:, 0:NC * 512], NC, 512)
    ktile16 = FM(STb[:, 0:NC * 512], NC, 512)
    wg_c = [FM(STb[:, i * 2048:(i + 1) * 2048], NC, 128) for i in (0, 1)]
    wu_c = [FM(STb[:, (2 + i) * 2048:(3 + i) * 2048], NC, 128) for i in (0, 1)]
    wd_c = FM(STb[:, 0:NH0 * 128], NH0, 128)
    wo_c = [FM(STb[:, i * 2048:(i + 1) * 2048], NC, 128) for i in (0, 1)]
    wbuf = FM(STb[:, 8192:8192 + NC * 512], NC, 512)
    wgd = FM(STb[:, 8192:8192 + 2048], NC, 128)
    wud = FM(STb[:, 8192 + 2048:8192 + 4096], NC, 128)
    wdd = FM(STb[:, 8192:8192 + NH0 * 128], NH0, 128)
    vtile = STb[:, 16384:16896]
    gtmp = STb[:, 16896:17408]
    x1colb = STb[:, 8192:16384]

    ROWS = B.ROWS
    r_mean, r_msq, r_t1, r_rstd = (ROWS[0:1, i * 512:(i + 1) * 512] for i in range(4))
    r_u = ROWS[0:1, 0:2048]
    r_zb = ROWS[0:1, 2048:4096]

    gP0 = S.on("g", lambda e: e.iota(B.posf, pattern=[[128, 16]], base=0,
                                     channel_multiplier=1,
                                     allow_small_or_imprecise_dtypes=True), inc="gg")
    gP1 = S.on("g", lambda e: e.iota(B.drow, pattern=[[1, 2048]], base=0,
                                     channel_multiplier=0,
                                     allow_small_or_imprecise_dtypes=True), inc="gg")
    d0 = S.dma("g", B.poso, ext["pos2"])
    S.dma("g", B.bq_col, ext["bq_col"])
    S.dma("g", B.bo_col, ext["bo_col"])
    S.dma("g", B.sg_col, ext["sg_col"])
    S.dma("g", B.su_col, ext["su_col"])
    S.dma("g", B.sd_col, ext["sd_col"])
    S.dma("g", B.bk_row, ext["bk_row"])
    dsm = S.dma("g", B.bv_row, ext["bv_row"])
    s0 = S.on("s", lambda e: e.activation(B.offb, B.poso, AF.Copy, scale=-1.0 / NFULL),
              waits=W(("dg", d0)), inc="ss")
    tof = S.on("t", lambda e: e.matmul(PS[:, 0:2], B.ones_k1f, B.offb,
                                       start=True, stop=True),
               waits=W(("ss", s0)), inc="tt")
    vof = S.on("v", lambda e: e.tensor_copy(B.gcol[:, 0:2], PS[:, 0:2]),
               waits=W(("tt", tof)), inc="vv")
    S.on("s", lambda e: e.activation(B.decay_col[:, 0:16], B.posf, AF.Exp,
                                     scale=-1.0 / NFULL, bias=B.gcol[:, 0:1]),
         waits=W(("gg", gP0), ("vv", vof)), inc="ss")
    sdc = S.on("s", lambda e: e.activation(B.decay_col[:, 16:32], B.posf, AF.Exp,
                                           scale=-1.0 / NFULL, bias=B.gcol[:, 1:2]),
               inc="ss")
    sdr = S.on("s", lambda e: e.activation(B.drow, B.drow, AF.Exp,
                                           scale=-1.0 / NFULL, bias=B.offb[0:1, 0:1]),
               waits=W(("gg", gP1), ("ss", s0)), inc="ss")

    for half in (0, 1):
        cb = half * 2048
        xdst = xn0 if half == 0 else xnO
        for s4 in range(4):
            dx = S.dma("g", xslab.all3(),
                       slab_ap(ext["x0T"], cb + s4 * 512, cb + s4 * 512 + 512),
                       waits=W(("tt", S.c("tt")), ("vv", S.c("vv")), ("ss", S.c("ss"))))
            tcs = 0
            tsq = 0
            for c in range(16):
                tcs = S.on("t", lambda e, c=c: e.matmul(PR(0), B.ones_m1, xslab.ch(c),
                                                        start=(c == 0), stop=(c == 15)),
                           waits=W(("dg", dx), ("ss", S.c("ss")), ("vv", S.c("vv"))),
                           inc="tt")
            for c in range(16):
                sqv = S.on("s", lambda e, c=c: e.activation(B.sqt[c % 2], xslab.ch(c),
                                                            AF.Square),
                           waits=W(("dg", dx), ("tt", tsq)), inc="ss")
                tsq = S.on("t", lambda e, c=c: e.matmul(PR(1), B.ones_m1, B.sqt[c % 2],
                                                        start=(c == 0), stop=(c == 15)),
                           waits=W(("ss", sqv)), inc="tt")
            sm1 = S.on("s", lambda e: e.activation(r_mean, PR(0), AF.Copy, scale=1.0 / D),
                       waits=W(("tt", tcs)), inc="ss")
            sm2 = S.on("s", lambda e: e.activation(r_msq, PR(1), AF.Copy, scale=1.0 / D),
                       waits=W(("tt", tsq)), inc="ss")
            v1 = S.on("v", lambda e: e.tensor_mul(r_t1, r_mean, r_mean),
                      waits=W(("ss", sm2)), inc="vv")
            v2 = S.on("v", lambda e: e.tensor_sub(r_t1, r_msq, r_t1), inc="vv")
            sv1 = S.on("s", lambda e: e.activation(r_t1, r_t1, AF.Ln, bias=EPS_LN),
                       waits=W(("vv", v2)), inc="ss")
            sv2 = S.on("s", lambda e: e.activation(r_rstd, r_t1, AF.Exp, scale=-0.5),
                       inc="ss")
            tb1 = S.on("t", lambda e: e.matmul(P(2), B.ones_k1f, r_mean,
                                               start=True, stop=True),
                       waits=W(("ss", sm1), ("vv", S.c("vv"))), inc="tt")
            tb2 = S.on("t", lambda e: e.matmul(P(3), B.ones_k1f, r_rstd,
                                               start=True, stop=True),
                       waits=W(("ss", sv2)), inc="tt")
            vb1 = S.on("v", lambda e: e.tensor_copy(B.mb, P(2)), waits=W(("tt", tb1)),
                       inc="vv")
            vb2 = S.on("v", lambda e: e.tensor_copy(B.rb, P(3)), waits=W(("tt", tb2)),
                       inc="vv")
            for c in range(16):
                dst = xdst.ch(c, s4 * 512, s4 * 512 + 512)
                S.on("v", lambda e, c=c: e.tensor_sub(B.tmpf[:, 0:512], xslab.ch(c), B.mb),
                     waits=W(("vv", vb2)), inc="vv")
                S.on("v", lambda e, dst=dst: e.tensor_mul(dst, B.tmpf[:, 0:512], B.rb),
                     inc="vv")
    vxn_done = S.c("vv")

    kv_first = {4: True, 5: True}
    ks_first = {6: True, 7: True}
    for half in (0, 1):
        xsrc = xn0 if half == 0 else xnO
        for sc in range(2):
            dwk = S.dma("g", wbuf.all3(), slab_ap(ext["wkT_oh"], sc * 512, sc * 512 + 512),
                        waits=W(("tt", S.c("tt"))))
            ek = [0] * 16
            for i in range(16):
                bk_ = i % 2
                tk = 0
                for c in range(16):
                    tk = S.on("t", lambda e, c=c, i=i, bk_=bk_, xsrc=xsrc:
                              e.matmul(P(bk_), xsrc.ch(c, i * 128, i * 128 + 128),
                                       wbuf.ch(c), start=(c == 0), stop=False),
                              waits=W(("dg", dwk), ("vv", vxn_done),
                                      ("ss", ek[i - 2] if i >= 2 else 0)), inc="tt")
                tk = S.on("t", lambda e, sc=sc, bk_=bk_:
                          e.matmul(P(bk_), B.ones_k1, B.bk_row[0:1, sc * 512:sc * 512 + 512],
                                   start=False, stop=True),
                          waits=W(("dg", dsm)), inc="tt")
                dci = half * 16 + i
                ek[i] = S.on("s", lambda e, i=i, dci=dci, bk_=bk_:
                             e.activation(ktile16.ch(i), P(bk_), AF.Relu,
                                          scale=B.decay_col[:, dci:dci + 1]),
                             waits=W(("tt", tk)), inc="ss")
            dwv = S.dma("g", wbuf.all3(), slab_ap(ext["wvT_oh"], sc * 512, sc * 512 + 512),
                        waits=W(("tt", S.c("tt"))))
            evp = 0
            for i in range(16):
                bv_ = 2 + i % 2
                tv = 0
                for c in range(16):
                    tv = S.on("t", lambda e, c=c, i=i, bv_=bv_, xsrc=xsrc:
                              e.matmul(P(bv_), xsrc.ch(c, i * 128, i * 128 + 128),
                                       wbuf.ch(c), start=(c == 0), stop=False),
                              waits=W(("dg", dwv), ("vv", S.c("vv")),
                                      ("ss", evp if i >= 1 else 0)), inc="tt")
                tv = S.on("t", lambda e, sc=sc, bv_=bv_:
                          e.matmul(P(bv_), B.ones_k1, B.bv_row[0:1, sc * 512:sc * 512 + 512],
                                   start=False, stop=True), inc="tt")
                ev = S.on("s", lambda e, bv_=bv_: e.activation(vtile, P(bv_), AF.Copy),
                          waits=W(("tt", tv)), inc="ss")
                evp = ev
                for hh in range(4):
                    o = hh * 128
                    ft = kv_first[4 + sc]
                    kv_first[4 + sc] = False
                    S.on("t", lambda e, o=o, sc=sc, i=i, ft=ft:
                         e.matmul(PS[:, (4 + sc) * 512 + o:(4 + sc) * 512 + o + 128],
                                  ktile16.ap[:, i * 512 + o:i * 512 + o + 128],
                                  vtile[:, o:o + 128],
                                  start=ft, stop=False, skip_group_check=True),
                         waits=W(("ss", ev)), inc="tt")
                ft = ks_first[6 + sc]
                ks_first[6 + sc] = False
                S.on("t", lambda e, i=i, sc=sc, ft=ft:
                     e.matmul(PS[0:1, (6 + sc) * 512:(6 + sc) * 512 + 512],
                              B.ones_m1, ktile16.ch(i),
                              start=ft, stop=False, skip_group_check=True),
                     inc="tt")
    tkv_done = S.c("tt")

    vk1 = S.on("v", lambda e: e.tensor_copy(B.tmpf[:, 0:512], P(4)),
               waits=W(("tt", tkv_done)), inc="vv")
    dk1 = S.dma("q", ext["kvc"][0:512].rearrange("(h d) e -> d h e", d=128),
                B.tmpf[:, 0:512].rearrange("p (h e) -> p h e", e=128),
                waits=W(("vv", vk1)))
    vk2 = S.on("v", lambda e: e.tensor_copy(B.tmpf[:, 0:512], P(5)),
               waits=W(("dq", dk1)), inc="vv")
    dk2 = S.dma("q", ext["kvc"][512:1024].rearrange("(h d) e -> d h e", d=128),
                B.tmpf[:, 0:512].rearrange("p (h e) -> p h e", e=128),
                waits=W(("vv", vk2)))
    vk3 = S.on("v", lambda e: e.tensor_copy(r_u[0:1, 0:1024], PS[0:1, 6 * 512:6 * 512 + 1024]),
               inc="vv")
    dk3 = S.dma("q", ext["kvc"][1024:1032].rearrange("h e -> (h e)")[None, :],
                r_u[0:1, 0:1024], waits=W(("vv", vk3)))
    c1 = S.on("g", lambda e: e.collective_compute(
        "AllGather", ALU.bypass, replica_groups=GROUPS,
        ins=[ext["kvc"][:]], outs=[ext["kvg"][:]]),
        waits=W(("dq", max(dk1, dk2, dk3))), inc="c1")

    csxn = B.ksum_sb
    csgu = B.bk_row
    tcx = 0
    vcx = S.c("vv")
    for q4 in range(4):
        for c in range(16):
            tcx = S.on("t", lambda e, c=c, q4=q4: e.matmul(
                PR(0), B.ones_m1, xn0.ch(c, q4 * 512, q4 * 512 + 512),
                start=(c == 0), stop=(c == 15)),
                waits=W(("vv", max(vxn_done, vcx)), ("ss", S.c("ss"))), inc="tt")
        vcx = S.on("v", lambda e, q4=q4: e.tensor_copy(csxn[0:1, q4 * 512:q4 * 512 + 512],
                                                       PR(0)),
                   waits=W(("tt", tcx)), inc="vv")
    csxn_v = S.c("vv")
    gu_sb = STb[:, 16384:16896]
    gtmp = STb[:, 16896:17408]
    evg = {b: S.c("ss") for b in range(4)}
    evu = {b: S.c("vv") for b in range(8)}
    dgu_p = 0
    wrd = {0: 0, 1: 0}
    for ht in range(NH0):
        bsel = ht % 2
        dgc = S.dma("g", wg_c[bsel].all3(), colload_ap(ext["wqgT"], ht),
                    waits=W(("tt", wrd[bsel])))
        duc = S.dma("g", wu_c[bsel].all3(), colload_ap(ext["wquT"], ht))
        tg = {}
        for q4 in range(4):
            t_ = 0
            for c in range(16):
                t_ = S.on("t", lambda e, c=c, q4=q4, bsel=bsel: e.matmul(
                    P(q4), wg_c[bsel].ch(c), xn0.ch(c, q4 * 512, q4 * 512 + 512),
                    start=(c == 0), stop=False),
                    waits=W(("dg", dgc), ("ss", evg[q4]), ("vv", csxn_v)), inc="tt")
            tg[q4] = S.on("t", lambda e, q4=q4: e.matmul(
                P(q4), B.n128_k1, csxn[0:1, q4 * 512:q4 * 512 + 512],
                start=False, stop=True), inc="tt")
        tu = {}
        for q4 in range(4):
            t_ = 0
            for c in range(16):
                t_ = S.on("t", lambda e, c=c, q4=q4, bsel=bsel: e.matmul(
                    P(4 + q4), wu_c[bsel].ch(c), xn0.ch(c, q4 * 512, q4 * 512 + 512),
                    start=(c == 0), stop=False),
                    waits=W(("dg", duc), ("vv", evu[4 + q4])), inc="tt")
            tu[q4] = S.on("t", lambda e, q4=q4: e.matmul(
                P(4 + q4), B.n128_k1, csxn[0:1, q4 * 512:q4 * 512 + 512],
                start=False, stop=True), inc="tt")
        wrd[bsel] = tu[3]
        vg = 0
        for q4 in range(4):
            sg = S.on("s", lambda e, ht=ht, q4=q4: e.activation(
                gtmp, P(q4), AF.Silu, scale=B.sg_col[:, ht:ht + 1]),
                waits=W(("tt", tg[q4]), ("vv", vg if q4 else S.c("vv"))), inc="ss")
            evg[q4] = sg
            vu = S.on("v", lambda e, ht=ht, q4=q4: e.tensor_scalar(
                out=B.tmpf[:, 0:512], in0=P(4 + q4), scalar1=B.su_col[:, ht:ht + 1],
                scalar2=None, op0=ALU.mult), waits=W(("tt", tu[q4])), inc="vv")
            evu[4 + q4] = vu
            vg = S.on("v", lambda e: e.tensor_mul(gu_sb, gtmp, B.tmpf[:, 0:512]),
                      waits=W(("ss", sg), ("dq", dgu_p)), inc="vv")
            dgu_p = S.dma("q", ext["gus"][ht * 128:(ht + 1) * 128, q4 * 512:q4 * 512 + 512],
                          gu_sb, waits=W(("vv", vg)))
    wd_c2 = [FM(STb[:, 0:NH0 * 128], NH0, 128),
             FM(STb[:, 8192:8192 + NH0 * 128], NH0, 128)]
    evd = {b: S.c("ss") for b in (2, 3)}
    drd = {0: S.c("tt"), 1: S.c("tt")}
    for ns in range(4):
        n0 = ns * 512
        dgs = S.dma("g", guA.all3(), slab_ap(ext["gus"], n0, n0 + 512),
                    waits=W(("tt", S.c("tt")), ("dq", S.c("dq"))))
        tcg = 0
        for hc in range(NH0):
            tcg = S.on("t", lambda e, hc=hc: e.matmul(
                PR(6), B.ones_m1, guA.ch(hc), start=(hc == 0), stop=(hc == NH0 - 1)),
                waits=W(("dg", dgs), ("vv", S.c("vv")), ("ss", S.c("ss"))), inc="tt")
        vcg = S.on("v", lambda e: e.tensor_copy(csgu[0:1, 0:512], PR(6)),
                   waits=W(("tt", tcg)), inc="vv")
        for dt in range(16):
            bsel = dt % 2
            ddc = S.dma("g", wd_c2[bsel].all3(), colload_ap(ext["wqdT"], dt),
                        waits=W(("tt", drd.get(bsel, 0))))
            pb = 2 + dt % 2
            td = 0
            for hc in range(NH0):
                td = S.on("t", lambda e, hc=hc, pb=pb, bsel=bsel: e.matmul(
                    P(pb), wd_c2[bsel].ch(hc), guA.ch(hc),
                    start=(hc == 0), stop=False),
                    waits=W(("dg", ddc), ("ss", evd[pb]), ("vv", vcg)), inc="tt")
            td = S.on("t", lambda e, pb=pb: e.matmul(
                P(pb), B.n128_k1, csgu[0:1, 0:512], start=False, stop=True), inc="tt")
            drd[bsel] = td
            sf = S.on("s", lambda e, dt=dt, pb=pb: e.activation(
                B.tD, P(pb), AF.Copy, scale=B.sd_col[:, dt:dt + 1]),
                waits=W(("tt", td)), inc="ss")
            evd[pb] = sf
            S.dma("q", ext["f0s"][dt * 128:(dt + 1) * 128, n0:n0 + 512], B.tD,
                  waits=W(("ss", sf)))

    qev = {0: S.c("ss"), 1: S.c("ss")}
    qrd = {0: S.c("tt"), 1: S.c("tt")}
    for t16 in range(16):
        bsel = t16 % 2
        dqc = S.dma("g", wo_c[bsel].all3(), colload_ap(ext["wqT"], t16),
                    waits=W(("tt", qrd[bsel])))
        for s4 in range(4):
            pb = s4 % 2
            tq = 0
            for c in range(16):
                tq = S.on("t", lambda e, c=c, s4=s4, bsel=bsel, pb=pb:
                          e.matmul(P(pb), wo_c[bsel].ch(c),
                                   xn0.ch(c, s4 * 512, s4 * 512 + 512),
                                   start=(c == 0), stop=(c == 15)),
                          waits=W(("dg", dqc), ("ss", qev[pb])), inc="tt")
            qev[pb] = S.on("s", lambda e, t16=t16, s4=s4, pb=pb: e.activation(
                qA.ch(t16, s4 * 512, s4 * 512 + 512), P(pb), AF.Relu,
                bias=B.bq_col[:, t16:t16 + 1]),
                waits=W(("tt", tq)), inc="ss")
            qrd[bsel] = tq
    q_done = S.c("ss")

    S.on("g", lambda e: e.nop(), waits=W(("c1", 1)))
    dk16 = 0
    for H in range(16):
        r, hp = H // 8, H % 8
        dk16 = S.dma("g", B.kv16[:, H * 128:(H + 1) * 128],
                     ext["kvg"][r * 1032 + hp * 128: r * 1032 + (hp + 1) * 128, :],
                     waits=W(("c1", 1)))
    dks = 0
    for r in range(2):
        dks = S.dma("g", B.ksum_sb[0:1, r * 1024:(r + 1) * 1024],
                    ext["kvg"][r * 1032 + 1024: r * 1032 + 1032, :].rearrange("h e -> (h e)")[None, :],
                    waits=W(("c1", 1), ("tt", S.c("tt"))))
    kc_t = 0
    for H in range(16):
        kc_t = S.on("t", lambda e, H=H: e.matmul(
            PS[:, 6 * 512 + H:6 * 512 + H + 1],
            B.ksum_sb[0:1, H * 128:(H + 1) * 128], B.ones_k1[0:1, 0:1],
            start=(H == 0), stop=(H == 15), skip_group_check=True),
            waits=W(("dg", dks), ("ss", q_done)), inc="tt")
    vkc = S.on("v", lambda e: e.tensor_copy(B.sqt[0][:, 0:16], PS[:, 6 * 512:6 * 512 + 16]),
               waits=W(("tt", kc_t)), inc="vv")
    kcols = B.sqt[0][:, 0:16]

    rd = {b: S.c("vv") for b in range(8)}
    for H in range(16):
        for s4 in range(4):
            ub = 6 + (s4 % 2)
            tu4 = S.on("t", lambda e, H=H, s4=s4, ub=ub: e.matmul(
                PR(ub, 512), kcols[:, H:H + 1],
                qA.ch(H, s4 * 512, s4 * 512 + 512),
                start=True, stop=True), waits=W(("vv", max(vkc, rd[ub]))), inc="tt")
            rd[ub] = S.on("v", lambda e, s4=s4, ub=ub: e.tensor_copy(
                r_u[0:1, s4 * 512:s4 * 512 + 512], PR(ub, 512)),
                waits=W(("tt", tu4)), inc="vv")
        v5 = S.on("v", lambda e: e.tensor_mul(r_zb, r_u, B.drow),
                  waits=W(("ss", sdr), ("vv", S.c("vv"))), inc="vv")
        s5 = S.on("s", lambda e: e.activation(r_zb, r_zb, AF.Copy, bias=EPS_Z),
                  waits=W(("vv", v5)), inc="ss")
        v6a = S.on("v", lambda e: e.reciprocal(r_zb, r_zb),
                   waits=W(("ss", s5)), inc="vv")
        v6 = S.on("v", lambda e: e.tensor_mul(r_u, r_zb, B.drow), inc="vv")
        for s4 in range(4):
            pb = s4 % 2
            to = S.on("t", lambda e, H=H, s4=s4, pb=pb: e.matmul(
                P(pb), B.kv16[:, H * 128:(H + 1) * 128],
                qA.ch(H, s4 * 512, s4 * 512 + 512), start=True, stop=True),
                waits=W(("vv", max(v6, rd[pb])), ("dg", dk16)), inc="tt")
            tz = S.on("t", lambda e, s4=s4, pb=pb: e.matmul(
                P(2 + pb), B.ones_k1f, r_u[0:1, s4 * 512:s4 * 512 + 512],
                start=True, stop=True), waits=W(("vv", rd[2 + pb])), inc="tt")
            vz = S.on("v", lambda e, pb=pb: e.tensor_copy(B.mb, P(2 + pb)),
                      waits=W(("tt", tz)), inc="vv")
            rd[2 + pb] = vz
            rd[pb] = S.on("v", lambda e, H=H, s4=s4, pb=pb: e.tensor_mul(
                atA.ch(H, s4 * 512, s4 * 512 + 512), P(pb), B.mb),
                waits=W(("tt", to), ("vv", vz)), inc="vv")
    at_done = S.c("vv")

    swv = {0: 0, 1: 0}
    ord_ = {0: at_done, 1: at_done}
    vb_p = 0
    dx1_p = 0
    dxn_p = 0
    tsq_p = S.c("tt")
    for s4 in range(4):
        x1v = [0] * 16
        for t16 in range(16):
            bsel = t16 % 2
            dwo = S.dma("g", wo_c[bsel].all3(), colload_ap(ext["woT"], t16),
                        waits=W(("tt", ord_[bsel])))
            pb = t16 % 2
            tw = 0
            for c in range(16):
                tw = S.on("t", lambda e, c=c, s4=s4, bsel=bsel, pb=pb:
                          e.matmul(P(pb), wo_c[bsel].ch(c),
                                   atA.ch(c, s4 * 512, s4 * 512 + 512),
                                   start=(c == 0), stop=(c == 15)),
                          waits=W(("dg", dwo), ("vv", max(at_done, swv[pb]))),
                          inc="tt")
            ord_[bsel] = tw
            sw = S.on("v", lambda e, t16=t16, pb=pb: e.tensor_scalar(
                out=B.tA, in0=P(pb), scalar1=B.bo_col[:, t16:t16 + 1],
                scalar2=None, op0=ALU.add),
                waits=W(("tt", tw), ("vv", vb_p), ("dq", dx1_p)), inc="vv")
            swv[pb] = sw
            df0 = S.dma("g", B.tD, ext["f0s"][t16 * 128:(t16 + 1) * 128,
                                              s4 * 512:s4 * 512 + 512],
                        waits=W(("vv", S.c("vv")), ("dq", S.c("dq"))))
            dx0 = S.dma("q", B.tB, ext["x0T"][t16 * 128:(t16 + 1) * 128,
                                              s4 * 512:s4 * 512 + 512],
                        waits=W(("vv", max(vb_p, at_done))))
            va = S.on("v", lambda e: e.tensor_add(B.tC, B.tA, B.tD),
                      waits=W(("dg", df0)), inc="vv")
            vb = S.on("v", lambda e: e.tensor_add(B.tA, B.tC, B.tB),
                      waits=W(("dq", dx0)), inc="vv")
            vb_p = vb
            dx1 = S.dma("q", ext["x1T"][t16 * 128:(t16 + 1) * 128, s4 * 512:s4 * 512 + 512],
                        B.tA, waits=W(("vv", vb)))
            dx1_p = dx1
            x1v[t16] = S.on("s", lambda e, t16=t16: e.activation(
                x1colb[:, t16 * 512:(t16 + 1) * 512], B.tA, AF.Copy),
                waits=W(("vv", vb), ("tt", tsq_p)), inc="ss")
        tcs = tsq = 0
        for t16 in range(16):
            tcs = S.on("t", lambda e, t16=t16: e.matmul(
                PR(2), B.ones_m1, x1colb[:, t16 * 512:(t16 + 1) * 512],
                start=(t16 == 0), stop=(t16 == 15)),
                waits=W(("ss", x1v[t16])), inc="tt")
        for t16 in range(16):
            sq2 = S.on("s", lambda e, t16=t16: e.activation(
                B.sqt[t16 % 2], x1colb[:, t16 * 512:(t16 + 1) * 512], AF.Square),
                waits=W(("tt", tsq)), inc="ss")
            tsq = S.on("t", lambda e, t16=t16: e.matmul(
                PR(3), B.ones_m1, B.sqt[t16 % 2], start=(t16 == 0), stop=(t16 == 15)),
                waits=W(("ss", sq2)), inc="tt")
        tsq_p = tsq
        sm1 = S.on("s", lambda e: e.activation(r_mean, PR(2), AF.Copy, scale=1.0 / D),
                   waits=W(("tt", tcs)), inc="ss")
        sm2 = S.on("s", lambda e: e.activation(r_msq, PR(3), AF.Copy, scale=1.0 / D),
                   waits=W(("tt", tsq)), inc="ss")
        v1 = S.on("v", lambda e: e.tensor_mul(r_t1, r_mean, r_mean),
                  waits=W(("ss", sm2)), inc="vv")
        v2 = S.on("v", lambda e: e.tensor_sub(r_t1, r_msq, r_t1), inc="vv")
        S.on("s", lambda e: e.activation(r_t1, r_t1, AF.Ln, bias=EPS_LN),
             waits=W(("vv", v2)), inc="ss")
        sv2 = S.on("s", lambda e: e.activation(r_rstd, r_t1, AF.Exp, scale=-0.5), inc="ss")
        tb1 = S.on("t", lambda e: e.matmul(P(4), B.ones_k1f, r_mean, start=True, stop=True),
                   waits=W(("ss", sv2)), inc="tt")
        tb2 = S.on("t", lambda e: e.matmul(P(5), B.ones_k1f, r_rstd, start=True, stop=True),
                   inc="tt")
        vb1 = S.on("v", lambda e: e.tensor_copy(B.mb, P(4)), waits=W(("tt", tb1)), inc="vv")
        vb2 = S.on("v", lambda e: e.tensor_copy(B.rb, P(5)), waits=W(("tt", tb2)), inc="vv")
        for t16 in range(16):
            S.on("v", lambda e, t16=t16: e.tensor_sub(
                B.tmpf[:, 0:512], x1colb[:, t16 * 512:(t16 + 1) * 512], B.mb),
                waits=W(("vv", vb2)), inc="vv")
            vx = S.on("v", lambda e: e.tensor_mul(B.tE, B.tmpf[:, 0:512], B.rb),
                      waits=W(("dq", dxn_p)), inc="vv")
            dxn_p = S.dma("q", ext["xn1T"][t16 * 128:(t16 + 1) * 128,
                                           s4 * 512:s4 * 512 + 512],
                          B.tE, waits=W(("vv", vx)))
    S.on("q", lambda e: e.nop(), waits=W(("dq", S.c("dq")), ("dg", S.c("dg"))))



def build_l1(S, B, ext, PS):
    nc = S.nc
    P = lambda b, n=1: PS[:, b * 512:(b + n) * 512]
    PR = lambda b, w=512: PS[0:1, b * 512:b * 512 + w]
    W = lambda *a: list(a)

    AR1b, AR1f = bf(B.AR1), f32(B.AR1)
    AR3b = bf(B.AR3)
    STb = bf(B.ST)

    wstage = AR1f[:, 0:8192]
    qtile = AR1f[:, 8192:16384]
    swt = AR3b[:, 0:8192]
    m01t = AR3b[:, 8192:16384]
    whatt = AR3b[:, 16384:24576]
    xg = FM(AR1b, NC, 2048)
    Qh = AR3b[:, 0:4096]
    Kh = AR3b[:, 4096:8192]
    Vth = AR3b[:, 8192:12288]
    Vtok = AR3b[:, 12288:16384]
    q2t = AR3b[:, 16384:20480]
    Psl = [AR3b[:, 20480 + i * 512:20480 + (i + 1) * 512] for i in range(4)]
    wqc = FM(STb[:, 0:2048], NC, 128)
    wkc = FM(STb[:, 2048:4096], NC, 128)
    wvc = FM(STb[:, 4096:6144], NC, 128)
    PB7 = PS[:, 7 * 512:8 * 512].bitcast(BF16)
    harena = FM(AR1b, NH1, 512)
    xnf = FM(AR3b[:, 0:8192], NC, 512)
    attE = FM(AR3b[:, 8192:16384], NC, 512)
    wpjh = [FM(AR3b[:, 16384 + j * 4096:16384 + (j + 1) * 4096], 32, 128) for j in (0, 1)]
    xslab4 = FM(STb[:, 0:8192], NC, 512)
    wfc_col = [FM(STb[:, 8192 + i * 2048:8192 + (i + 1) * 2048], NC, 128) for i in (0, 1)]
    wo_c4 = [FM(STb[:, 12288 + i * 2048:12288 + (i + 1) * 2048], NC, 128) for i in (0, 1)]
    idxE = B.mstrip.bitcast(I32)[:, 0:64]

    ROWS = B.ROWS
    r_mean, r_msq, r_t1, r_rstd = (ROWS[0:1, i * 512:(i + 1) * 512] for i in range(4))
    r_sum = ROWS[0:1, 2048:2560]

    b1qc = B.decay_col[:, 0:8]

    S.fence()
    dc0 = S.dma("q", ext["xn1c"][:], ext["xn1T"][:], waits=W(("dq", S.c("dq"))))
    c2 = S.on("g", lambda e: e.collective_compute(
        "AllGather", ALU.bypass, replica_groups=GROUPS,
        ins=[ext["xn1c"][:]], outs=[ext["xn1g"][:]]),
        waits=W(("dq", dc0)), inc="c2")

    db1 = S.dma("g", B.b1q, ext["b1q"])
    S.dma("g", B.b1k, ext["b1k"])
    S.dma("g", B.b1v, ext["b1v"])
    dbo = S.dma("g", B.bo1_col, ext["bo1_col"])
    sb1 = S.on("s", lambda e: e.activation(b1qc, B.b1q, AF.Copy, scale=CQK),
               waits=W(("dg", db1)), inc="ss")

    S.on("v", lambda e: e.memset(B.acc, 0.0), inc="vv")
    gam_cols = {}
    wbuf = [wstage, qtile]
    bcons = [0, 0]
    vlast = 0
    for m, (wT, nch, F) in enumerate((("wfcT", 16, 8192), ("wpjT", 64, 2048))):
        for ch in range(nch):
            b = ch % 2
            qe, qsem = ("q", "dq") if b == 0 else ("g", "dg")
            dw = S.dma(qe, wbuf[b][:, 0:F],
                       ext[wT][ch * 128:(ch + 1) * 128, :],
                       waits=W(("ss", bcons[b])))
            sa = S.on("s", lambda e, w_=wbuf[b], F=F: e.activation(
                swt[:, 0:F], w_[:, 0:F], AF.Abs, accum_out=B.gcol[:, 3:4]),
                waits=W((qsem, dw), ("vv", vlast)), inc="ss")
            bcons[b] = sa
            vlast = S.on("v", lambda e, m=m: e.tensor_add(B.acc[:, m:m + 1], B.acc[:, m:m + 1],
                                                          B.gcol[:, 3:4]),
                         waits=W(("ss", sa)), inc="vv")
        tg1 = S.on("t", lambda e, m=m: e.matmul(PS[0:1, 6 * 512:6 * 512 + 1],
                                                B.acc[:, m:m + 1], B.ones_m1f,
                                                start=True, stop=True),
                   waits=W(("vv", vlast)), inc="tt")
        sg1 = S.on("s", lambda e, m=m: e.activation(B.r11[0:1, m:m + 1],
                                                    PS[0:1, 6 * 512:6 * 512 + 1],
                                                    AF.Copy, scale=1.0 / N_W_FC),
                   waits=W(("tt", tg1)), inc="ss")
        sg2 = S.on("s", lambda e, m=m: e.activation(B.r11[0:1, 2 + m:3 + m],
                                                    B.r11[0:1, m:m + 1], AF.Copy,
                                                    bias=EPS_B),
                   inc="ss")
        vg1 = S.on("v", lambda e, m=m: e.reciprocal(B.r11[0:1, 4 + m:5 + m],
                                                    B.r11[0:1, 2 + m:3 + m]),
                   waits=W(("ss", sg2)), inc="vv")
        tb = S.on("t", lambda e, m=m: e.matmul(PS[:, 6 * 512:6 * 512 + 1], B.ones_k1f,
                                               B.r11[0:1, 4 + m:5 + m], start=True, stop=True),
                  waits=W(("vv", vg1)), inc="tt")
        vb = S.on("v", lambda e, m=m: e.tensor_copy(B.gcol[:, 2 * m:2 * m + 1],
                                                    PS[:, 6 * 512:6 * 512 + 1]),
                  waits=W(("tt", tb)), inc="vv")
        tb2 = S.on("t", lambda e, m=m: e.matmul(PS[:, 6 * 512:6 * 512 + 1], B.ones_k1f,
                                                B.r11[0:1, m:m + 1], start=True, stop=True),
                   waits=W(("vv", vb)), inc="tt")
        vb2 = S.on("v", lambda e, m=m: e.tensor_copy(B.gcol[:, 2 * m + 1:2 * m + 2],
                                                     PS[:, 6 * 512:6 * 512 + 1]),
                   waits=W(("tt", tb2)), inc="vv")
        gam_cols[m] = vb2

    in0b = [wstage[:, 0:8192], f32(B.ST)[:, 0:8192]]
    scons0 = [S.c("ss"), S.c("ss")]
    vm_p, vw_p = 0, 0
    stp_last = ("dq", 0)
    for ch in range(16):
        b = ch % 2
        qe, qsem = ("q", "dq") if b == 0 else ("g", "dg")
        dw = S.dma(qe, in0b[b], ext["wfcT"][ch * 128:(ch + 1) * 128, :],
                   waits=W(("ss", scons0[b])))
        sq_ = S.on("s", lambda e, i_=in0b[b]: e.activation(
            qtile[:, 0:8192], i_, AF.Square, scale=B.gcol[:, 0:1]),
            waits=W((qsem, dw), ("vv", gam_cols[1]), ("vv", vm_p)),
            inc="ss")
        vm = S.on("v", lambda e: e.tensor_scalar(
            out=m01t[:, 0:8192], in0=qtile[:, 0:8192], scalar1=0.25, scalar2=None,
            op0=ALU.is_ge), waits=W(("ss", sq_)), inc="vv")
        ss_ = S.on("s", lambda e, i_=in0b[b]: e.activation(swt[:, 0:8192], i_, AF.Sign),
                   waits=W(("vv", vw_p)), inc="ss")
        scons0[b] = ss_
        vw = S.on("v", lambda e: e.scalar_tensor_tensor(
            out=whatt[:, 0:8192], in0=swt[:, 0:8192], scalar=B.gcol[:, 1:2],
            in1=m01t[:, 0:8192], op0=ALU.mult, op1=ALU.mult),
            waits=W(("ss", ss_), ("vv", vm), (stp_last[0], stp_last[1])), inc="vv")
        vm_p, vw_p = vm, vw
        stc = S.dma(qe, ext["whatfc"][ch * 128:(ch + 1) * 128, :], whatt[:, 0:8192],
                    waits=W(("vv", vw)))
        stp_last = (qsem, stc)
    in2 = [wstage[:, 0:2048], wstage[:, 2048:4096]]
    qt2 = [qtile[:, 0:2048], qtile[:, 2048:4096]]
    sw2 = [swt[:, 0:2048], swt[:, 2048:4096]]
    m012 = [m01t[:, 0:2048], m01t[:, 2048:4096]]
    wh2 = [whatt[:, 0:2048], whatt[:, 2048:4096]]
    scons2 = [S.c("ss"), S.c("ss")]
    vmb = [S.c("vv"), S.c("vv")]
    vwb = [S.c("vv"), S.c("vv")]
    stp2 = [stp_last, stp_last]
    for ch in range(64):
        b = ch % 2
        qe, qsem = ("q", "dq") if b == 0 else ("g", "dg")
        dw = S.dma(qe, in2[b], ext["wpjT"][ch * 128:(ch + 1) * 128, :],
                   waits=W(("ss", scons2[b])))
        sq_ = S.on("s", lambda e, i_=in2[b], q_=qt2[b]: e.activation(
            q_, i_, AF.Square, scale=B.gcol[:, 2:3]),
            waits=W((qsem, dw), ("vv", gam_cols[1]), ("vv", vmb[b])), inc="ss")
        vm = S.on("v", lambda e, q_=qt2[b], m_=m012[b]: e.tensor_scalar(
            out=m_, in0=q_, scalar1=0.25, scalar2=None, op0=ALU.is_ge),
            waits=W(("ss", sq_)), inc="vv")
        vmb[b] = vm
        ss_ = S.on("s", lambda e, i_=in2[b], s_=sw2[b]: e.activation(s_, i_, AF.Sign),
                   waits=W(("vv", vwb[b])), inc="ss")
        scons2[b] = ss_
        vw = S.on("v", lambda e, s_=sw2[b], m_=m012[b], w_=wh2[b]: e.scalar_tensor_tensor(
            out=w_, in0=s_, scalar=B.gcol[:, 3:4],
            in1=m_, op0=ALU.mult, op1=ALU.mult),
            waits=W(("ss", ss_), ("vv", vm), (stp2[b][0], stp2[b][1])), inc="vv")
        vwb[b] = vw
        stc = S.dma(qe, ext["whatpj"][ch * 128:(ch + 1) * 128, :], wh2[b],
                    waits=W(("vv", vw)))
        stp2[b] = (qsem, stc)

    S.on("g", lambda e: e.nop(), waits=W(("c2", 1)))
    S.fence()
    head_tt = S.c("tt")
    head_ss = S.c("ss")
    head_vv = S.c("vv")
    for hp in range(8):
        dq_ = S.dma("g", wqc.all3(), colload_ap(ext["wq1T"], hp), waits=W(("tt", head_tt)))
        dk_ = S.dma("g", wkc.all3(), colload_ap(ext["wk1T"], hp))
        dv_ = S.dma("g", wvc.all3(), colload_ap(ext["wv1T"], hp))
        rdq = {0: head_vv, 1: head_vv, 2: head_vv}
        evq = S.c("vv")
        for nh in range(2):
            dxg = 0
            for c in range(16):
                dxg = S.dma("g", xg.ch(c),
                            ext["xn1g"][nh * 2048 + c * 128: nh * 2048 + (c + 1) * 128, :],
                            waits=W(("tt", S.c("tt"))))
            for s4 in range(4):
                col = nh * 2048 + s4 * 512
                tq = tk = tv = 0
                for c in range(16):
                    tq = S.on("t", lambda e, c=c, s4=s4: e.matmul(
                        P(0), wqc.ch(c), xg.ch(c, s4 * 512, s4 * 512 + 512),
                        start=(c == 0), stop=(c == 15)),
                        waits=W(("dg", max(dq_, dxg)), ("vv", rdq[0]),
                                ("ss", sb1)), inc="tt")
                rdq[0] = S.on("v", lambda e, hp=hp, col=col: e.tensor_scalar(
                    out=Qh[:, col:col + 512], in0=P(0), scalar1=CQK, scalar2=b1qc[:, hp:hp + 1],
                    op0=ALU.mult, op1=ALU.add), waits=W(("tt", tq)), inc="vv")
                for c in range(16):
                    tk = S.on("t", lambda e, c=c, s4=s4: e.matmul(
                        P(1), wkc.ch(c), xg.ch(c, s4 * 512, s4 * 512 + 512),
                        start=(c == 0), stop=(c == 15)),
                        waits=W(("dg", max(dk_, dxg)), ("vv", rdq[1])), inc="tt")
                rdq[1] = S.on("v", lambda e, hp=hp, col=col: e.tensor_scalar(
                    out=Kh[:, col:col + 512], in0=P(1), scalar1=B.b1k[:, hp:hp + 1],
                    scalar2=None, op0=ALU.add), waits=W(("tt", tk)), inc="vv")
                for c in range(16):
                    tv = S.on("t", lambda e, c=c, s4=s4: e.matmul(
                        P(2), wvc.ch(c), xg.ch(c, s4 * 512, s4 * 512 + 512),
                        start=(c == 0), stop=(c == 15)),
                        waits=W(("dg", max(dv_, dxg)), ("vv", rdq[2])), inc="tt")
                rdq[2] = S.on("v", lambda e, hp=hp, col=col: e.tensor_scalar(
                    out=Vth[:, col:col + 512], in0=P(2), scalar1=B.b1v[:, hp:hp + 1],
                    scalar2=None, op0=ALU.add), waits=W(("tt", tv)), inc="vv")
        proj_vv = S.c("vv")
        vtr = S.c("vv")
        for kt in range(32):
            sl = kt % 2
            ttr = S.on("t", lambda e, kt=kt, sl=sl: e.transpose(
                PB7[:, sl * 128:(sl + 1) * 128], Vth[:, kt * 128:(kt + 1) * 128], B.ident),
                waits=W(("vv", max(proj_vv, vtr))), inc="tt")
            vtr = S.on("v", lambda e, kt=kt, sl=sl: e.tensor_copy(
                Vtok[:, kt * 128:(kt + 1) * 128], PB7[:, sl * 128:(sl + 1) * 128]),
                waits=W(("tt", ttr)), inc="vv")
        sqq = S.on("s", lambda e: e.activation(q2t, Qh, AF.Square), waits=W(("vv", proj_vv)),
                   inc="ss")
        vi1 = S.on("v", lambda e: e.memset(B.r11[0:1, 0:2], 0.0), inc="vv")
        mxv = vi1
        for part, src_ in ((0, "q"), (1, "k")):
            if part == 1:
                sqq = S.on("s", lambda e: e.activation(q2t, Kh, AF.Square),
                           waits=W(("tt", S.c("tt"))), inc="ss")
            for s8 in range(8):
                tsum = S.on("t", lambda e, s8=s8: e.matmul(
                    PR(6), B.ones_m1, q2t[:, s8 * 512:(s8 + 1) * 512],
                    start=True, stop=True), waits=W(("ss", sqq), ("vv", mxv)), inc="tt")
                vr = S.on("v", lambda e: e.reduce_max(B.r11[0:1, 6:7], PR(6),
                                                      axis=mybir.AxisListType.X),
                          waits=W(("tt", tsum)), inc="vv")
                mxv = S.on("v", lambda e, part=part: e.tensor_max(
                    B.r11[0:1, part:part + 1], B.r11[0:1, part:part + 1], B.r11[0:1, 6:7]),
                    inc="vv")
        s_c1 = S.on("s", lambda e: e.activation(B.r11[0:1, 2:3], B.r11[0:1, 0:1], AF.Sqrt),
                    waits=W(("vv", mxv)), inc="ss")
        s_c2 = S.on("s", lambda e: e.activation(B.r11[0:1, 3:4], B.r11[0:1, 1:2], AF.Sqrt),
                    inc="ss")
        v_c = S.on("v", lambda e: e.tensor_mul(B.r11[0:1, 4:5], B.r11[0:1, 2:3],
                                               B.r11[0:1, 3:4]),
                   waits=W(("ss", s_c2)), inc="vv")
        t_c = S.on("t", lambda e: e.matmul(PS[:, 6 * 512:6 * 512 + 1], B.ones_k1f,
                                           B.r11[0:1, 4:5], start=True, stop=True),
                   waits=W(("vv", v_c)), inc="tt")
        s_c3 = S.on("s", lambda e: e.activation(B.negC, PS[:, 6 * 512:6 * 512 + 1],
                                                AF.Copy, scale=-1.0),
                    waits=W(("tt", t_c)), inc="ss")
        pread = {i: ("vv", S.c("vv")) for i in range(4)}
        pslot_rd = {i: S.c("tt") for i in range(4)}
        av_rd = S.c("vv")
        zb_rd = S.c("vv")
        tE_rd = S.c("vv")
        tD_dma = S.c("dq")
        for s8 in range(8):
            ktmax = 4 * (s8 + 1)
            tsum = tav = 0
            for kt in range(ktmax):
                sl = kt % 4
                tS = S.on("t", lambda e, kt=kt, s8=s8, sl=sl: e.matmul(
                    P(sl), Kh[:, kt * 128:(kt + 1) * 128],
                    Qh[:, s8 * 512:(s8 + 1) * 512], start=True, stop=True),
                    waits=W(pread[sl], ("ss", s_c3)), inc="tt")
                boundary = kt >= 4 * s8
                if not boundary:
                    sE = S.on("s", lambda e, sl=sl: e.activation(
                        Psl[sl], P(sl), AF.Exp, bias=B.negC),
                        waits=W(("tt", max(tS, pslot_rd[sl]))), inc="ss")
                    pr = ("ss", sE)
                    pread[sl] = ("ss", sE)
                else:
                    off = kt * 128 - s8 * 512
                    sE = S.on("s", lambda e, sl=sl: e.activation(
                        B.tE, P(sl), AF.Exp, bias=B.negC),
                        waits=W(("tt", tS), ("vv", tE_rd)), inc="ss")
                    pread[sl] = ("ss", sE)
                    vE = S.on("v", lambda e, sl=sl, off=off: e.tensor_mul(
                        Psl[sl], B.tE, B.mstrip[:, 384 - off:384 - off + 512]),
                        waits=W(("ss", sE), ("tt", pslot_rd[sl])), inc="vv")
                    tE_rd = vE
                    pr = ("vv", vE)
                tsum = S.on("t", lambda e, sl=sl, kt=kt, ktmax=ktmax: e.matmul(
                    PR(4), B.ones_m1, Psl[sl], start=(kt == 0), stop=(kt == ktmax - 1)),
                    waits=W(pr, ("vv", zb_rd)), inc="tt")
                tav = S.on("t", lambda e, sl=sl, kt=kt, ktmax=ktmax: e.matmul(
                    P(5), Vtok[:, kt * 128:(kt + 1) * 128], Psl[sl],
                    start=(kt == 0), stop=(kt == ktmax - 1)),
                    waits=W(("vv", av_rd)), inc="tt")
                pslot_rd[sl] = tav
            vz1 = S.on("v", lambda e: e.tensor_copy(r_sum, PR(4)), waits=W(("tt", tsum)),
                       inc="vv")
            zb_rd = vz1
            vz2 = S.on("v", lambda e: e.reciprocal(r_sum, r_sum), inc="vv")
            tzb = S.on("t", lambda e: e.matmul(P(6), B.ones_k1f, r_sum, start=True, stop=True),
                       waits=W(("vv", vz2)), inc="tt")
            vz3 = S.on("v", lambda e: e.tensor_copy(B.mb, P(6)), waits=W(("tt", tzb)),
                       inc="vv")
            av_rd = S.on("v", lambda e: e.tensor_mul(B.tD, P(5), B.mb),
                         waits=W(("tt", tav), ("vv", vz3), ("dq", tD_dma)), inc="vv")
            tD_dma = S.dma("q", ext["attc"][hp * 128:(hp + 1) * 128,
                                            s8 * 512:(s8 + 1) * 512],
                           B.tD, waits=W(("vv", av_rd)))
        head_tt = S.c("tt")
        head_vv = S.c("vv")
    c3 = S.on("g", lambda e: e.collective_compute(
        "AllGather", ALU.bypass, replica_groups=GROUPS,
        ins=[ext["attc"][:]], outs=[ext["attg"][:]]),
        waits=W(("dq", S.c("dq"))), inc="c3")

    S.fence()
    dix = S.dma("g", idxE, ext["idxE"], waits=W(("ss", S.c("ss")), ("vv", S.c("vv"))))
    swv = {4: S.c("ss"), 5: S.c("ss")}
    frd = {0: S.c("tt"), 1: S.c("tt")}
    prd = {0: S.c("tt"), 1: S.c("tt")}
    ord4 = {0: S.c("tt"), 1: S.c("tt")}
    jrd = {6: S.c("vv"), 7: S.c("vv")}
    vb_p = 0
    dx2_p = 0
    tsq_p = S.c("tt")
    hrd = S.c("tt")
    for sl in range(4):
        dxs = S.dma("q", xslab4.all3(), slab_ap(ext["xn1T"], sl * 512, sl * 512 + 512),
                    waits=W(("tt", S.c("tt")), ("ss", S.c("ss")), ("vv", S.c("vv"))))
        tcs = tsq = 0
        for c in range(16):
            tcs = S.on("t", lambda e, c=c: e.matmul(PR(0), B.ones_m1, xslab4.ch(c),
                                                    start=(c == 0), stop=(c == 15)),
                       waits=W(("dq", dxs)), inc="tt")
        for c in range(16):
            sqv = S.on("s", lambda e, c=c: e.activation(B.sqt[c % 2], xslab4.ch(c), AF.Square),
                       waits=W(("dq", dxs), ("tt", tsq)), inc="ss")
            tsq = S.on("t", lambda e, c=c: e.matmul(PR(1), B.ones_m1, B.sqt[c % 2],
                                                    start=(c == 0), stop=(c == 15)),
                       waits=W(("ss", sqv)), inc="tt")
        sm1 = S.on("s", lambda e: e.activation(r_mean, PR(0), AF.Copy, scale=1.0 / D),
                   waits=W(("tt", tcs)), inc="ss")
        sm2 = S.on("s", lambda e: e.activation(r_msq, PR(1), AF.Copy, scale=1.0 / D),
                   waits=W(("tt", tsq)), inc="ss")
        v1 = S.on("v", lambda e: e.tensor_mul(r_t1, r_mean, r_mean), waits=W(("ss", sm2)),
                  inc="vv")
        v2 = S.on("v", lambda e: e.tensor_sub(r_t1, r_msq, r_t1), inc="vv")
        S.on("s", lambda e: e.activation(r_t1, r_t1, AF.Ln, bias=EPS_B),
             waits=W(("vv", v2)), inc="ss")
        sv2 = S.on("s", lambda e: e.activation(r_rstd, r_t1, AF.Exp, scale=-0.5), inc="ss")
        tb1 = S.on("t", lambda e: e.matmul(P(2), B.ones_k1f, r_mean, start=True, stop=True),
                   waits=W(("ss", sm1), ("vv", S.c("vv"))), inc="tt")
        tb2 = S.on("t", lambda e: e.matmul(P(3), B.ones_k1f, r_rstd, start=True, stop=True),
                   waits=W(("ss", sv2)), inc="tt")
        vb1 = S.on("v", lambda e: e.tensor_copy(B.mb, P(2)), waits=W(("tt", tb1)), inc="vv")
        vb2 = S.on("v", lambda e: e.tensor_copy(B.rb, P(3)), waits=W(("tt", tb2)), inc="vv")
        vxn = 0
        for c in range(16):
            S.on("v", lambda e, c=c: e.tensor_sub(B.tmpf[:, 0:512], xslab4.ch(c), B.mb),
                 waits=W(("vv", vb2)), inc="vv")
            vxn = S.on("v", lambda e, c=c: e.tensor_mul(xnf.ch(c), B.tmpf[:, 0:512], B.rb),
                       inc="vv")
        for ot in range(NH1):
            bsel = ot % 2
            pb = 4 + ot % 2
            dwf = S.dma("g", wfc_col[bsel].all3(), colload_ap(ext["whatfc"], ot),
                        waits=W(("tt", frd[bsel])))
            th = 0
            for c in range(16):
                th = S.on("t", lambda e, c=c, bsel=bsel, pb=pb: e.matmul(
                    P(pb), wfc_col[bsel].ch(c), xnf.ch(c),
                    start=(c == 0), stop=(c == 15)),
                    waits=W(("dg", dwf), ("vv", vxn), ("ss", swv[pb])), inc="tt")
            frd[bsel] = th
            swv[pb] = S.on("s", lambda e, ot=ot, pb=pb: e.activation(
                harena.ch(ot), P(pb), AF.Gelu), waits=W(("tt", th)), inc="ss")
        tcs = tsq = 0
        for ot in range(NH1):
            tcs = S.on("t", lambda e, ot=ot: e.matmul(PR(0), B.ones_m1, harena.ch(ot),
                                                      start=(ot == 0), stop=(ot == NH1 - 1)),
                       waits=W(("ss", swv[4] if ot == 0 else 0), ("ss", swv[5])), inc="tt")
        for ot in range(NH1):
            sqv = S.on("s", lambda e, ot=ot: e.activation(B.sqt[ot % 2], harena.ch(ot),
                                                          AF.Square),
                       waits=W(("tt", tsq)), inc="ss")
            tsq = S.on("t", lambda e, ot=ot: e.matmul(PR(1), B.ones_m1, B.sqt[ot % 2],
                                                      start=(ot == 0), stop=(ot == NH1 - 1)),
                       waits=W(("ss", sqv)), inc="tt")
        sm1 = S.on("s", lambda e: e.activation(r_mean, PR(0), AF.Copy, scale=1.0 / HID1),
                   waits=W(("tt", tcs)), inc="ss")
        sm2 = S.on("s", lambda e: e.activation(r_msq, PR(1), AF.Copy, scale=1.0 / HID1),
                   waits=W(("tt", tsq)), inc="ss")
        v1 = S.on("v", lambda e: e.tensor_mul(r_t1, r_mean, r_mean), waits=W(("ss", sm2)),
                  inc="vv")
        v2 = S.on("v", lambda e: e.tensor_sub(r_t1, r_msq, r_t1), inc="vv")
        S.on("s", lambda e: e.activation(r_t1, r_t1, AF.Ln, bias=EPS_B),
             waits=W(("vv", v2)), inc="ss")
        sv2 = S.on("s", lambda e: e.activation(r_rstd, r_t1, AF.Exp, scale=-0.5), inc="ss")
        tb1 = S.on("t", lambda e: e.matmul(P(2), B.ones_k1f, r_mean, start=True, stop=True),
                   waits=W(("ss", sm1), ("vv", S.c("vv"))), inc="tt")
        tb2 = S.on("t", lambda e: e.matmul(P(3), B.ones_k1f, r_rstd, start=True, stop=True),
                   waits=W(("ss", sv2)), inc="tt")
        vb1 = S.on("v", lambda e: e.tensor_copy(B.mb, P(2)), waits=W(("tt", tb1)), inc="vv")
        vb2 = S.on("v", lambda e: e.tensor_copy(B.rb, P(3)), waits=W(("tt", tb2)), inc="vv")
        hnorm = 0
        for ot in range(NH1):
            S.on("v", lambda e, ot=ot: e.tensor_sub(B.tmpf[:, 0:512], harena.ch(ot), B.mb),
                 waits=W(("vv", vb2)), inc="vv")
            hnorm = S.on("v", lambda e, ot=ot: e.tensor_mul(harena.ch(ot), B.tmpf[:, 0:512],
                                                            B.rb), inc="vv")
        S.on("g", lambda e: e.nop(), waits=W(("c3", 1)))
        dga = 0
        attg_v = ext["attg"].rearrange("e (b n) -> (e b) n", n=512)
        for c in range(16):
            dga = S.on("g", lambda e, c=c, sl=sl: e.indirect_dma_start(
                out=attE.ch(c), out_offset=None, in_=attg_v,
                in_offset=bass.IndirectOffsetOnAxis(idxE[:, c * 4 + sl:c * 4 + sl + 1], 0)),
                waits=W(("dg", dix), ("tt", hrd)), inc="dg", incn=16)
        for dt in range(16):
            pb = 6 + dt % 2
            hsel = dt % 2
            dwpA = S.dma("g", wpjh[0].all3(), colload_half(ext["whatpj"], dt, 0),
                         waits=W(("tt", prd[0])))
            dwo = S.dma("g", wo_c4[hsel].all3(), colload_ap(ext["wo1T"], dt),
                        waits=W(("tt", ord4[hsel])))
            dwpB = S.dma("g", wpjh[1].all3(), colload_half(ext["whatpj"], dt, 1),
                         waits=W(("tt", prd[1])))
            tj = 0
            for c in range(16):
                tj = S.on("t", lambda e, c=c, pb=pb, hsel=hsel: e.matmul(
                    P(pb), wo_c4[hsel].ch(c), attE.ch(c), start=(c == 0), stop=False),
                    waits=W(("dg", max(dwo, dga)), ("vv", jrd[pb])), inc="tt")
            ord4[hsel] = tj
            for hc in range(NH1):
                tj = S.on("t", lambda e, hc=hc, pb=pb: e.matmul(
                    P(pb), wpjh[hc // 32].ch(hc % 32), harena.ch(hc),
                    start=False, stop=(hc == NH1 - 1)),
                    waits=W(("dg", dwpA if hc < 32 else dwpB), ("vv", hnorm)), inc="tt")
                if hc == 31:
                    prd[0] = tj
            prd[1] = tj
            hrd = tj
            vo = S.on("v", lambda e, dt=dt, pb=pb: e.tensor_scalar(
                out=B.tA, in0=P(pb), scalar1=B.bo1_col[:, dt:dt + 1], scalar2=None,
                op0=ALU.add), waits=W(("tt", tj), ("dg", dbo), ("vv", vb_p),
                                      ("dq", dx2_p)), inc="vv")
            jrd[pb] = vo
            dx1_ = S.dma("q", B.tB, ext["x1T"][dt * 128:(dt + 1) * 128,
                                               sl * 512:sl * 512 + 512],
                         waits=W(("vv", vb_p)))
            vb_p = S.on("v", lambda e: e.tensor_add(B.tC, B.tA, B.tB),
                        waits=W(("dq", dx1_)), inc="vv")
            dx2_p = S.dma("q", ext["x2T"][dt * 128:(dt + 1) * 128, sl * 512:sl * 512 + 512],
                          B.tC.bitcast(F32) if False else B.tC, waits=W(("vv", vb_p)))
    if ext.get("DEBUG"):
        S.dma("g", ext["dbg_gam"][:, 0:4], B.gcol,
              waits=W(("vv", S.c("vv")), ("ss", S.c("ss"))))
        S.dma("g", ext["dbg_gam"][0:1, 4:8], B.r11[0:1, 0:4])
        S.dma("g", ext["dbg_wf"], ext["whatfc"][0:128, :])
        S.dma("g", ext["dbg_attc"], ext["attc"][:, :])
        S.dma("g", ext["dbg_q"], Qh)
        S.dma("g", ext["dbg_h"], harena.ap[:, 0:NH1 * 512])
        S.dma("g", ext["dbg_xnf"], xnf.ap[:, 0:NC * 512])
        S.dma("g", ext["dbg_atE"], attE.ap[:, 0:NC * 512])
    S.on("q", lambda e: e.nop(), waits=W(("dq", S.c("dq")), ("dg", S.c("dg"))))


def _l1_inmaps(inputs, l0_results):
    C = np.ascontiguousarray
    wqT = C(np.asarray(inputs["L1_attn_wq"]).T)
    wkT = C(np.asarray(inputs["L1_attn_wk"]).T)
    wvT = C(np.asarray(inputs["L1_attn_wv"]).T)
    woT = C(np.asarray(inputs["L1_attn_wo"]).T)
    bq = np.asarray(inputs["L1_attn_bq"])
    bk = np.asarray(inputs["L1_attn_bk"])
    bv = np.asarray(inputs["L1_attn_bv"])
    bo = np.asarray(inputs["L1_attn_bo"])
    wfcT = C(np.asarray(inputs["L1_ffn_w_fc"]).T)
    wpjT = C(np.asarray(inputs["L1_ffn_w_proj"]).T)
    col = lambda v, n: C(v.reshape(n, 128).T.astype(np.float32))
    maps = []
    for core in range(8):
        h = core % 2
        hs = h * 1024
        p = np.arange(128)[:, None]
        cs = np.arange(64)[None, :]
        idxE = ((cs // 4) * 128 + p) * 8 + h * 4 + (cs % 4)
        maps.append({
            **({"xn1T": np.asarray(l0_results[core]["xn1T"]),
                "x1T": np.asarray(l0_results[core]["x1T"])} if l0_results is not None else {}),
            "wq1T": C(wqT[:, hs:hs + 1024]), "b1q": col(bq[hs:hs + 1024], 8),
            "wk1T": C(wkT[:, hs:hs + 1024]), "b1k": col(bk[hs:hs + 1024], 8),
            "wv1T": C(wvT[:, hs:hs + 1024]), "b1v": col(bv[hs:hs + 1024], 8),
            "wo1T": woT, "bo1_col": col(bo, 16),
            "wfcT": wfcT, "wpjT": wpjT,
            "idxE": idxE.astype(np.int32),
        })
    return maps


def _build_l1_nc():
    nc = bass.Bass("TRN2", target_bir_lowering=False, debug=False)
    ext = {}
    EI = lambda n, s, d=F32: nc.dram_tensor(n, s, d, kind="ExternalInput").ap()
    ext["xn1T"] = EI("xn1T", [D, NOWN], BF16)
    ext["x1T"] = EI("x1T", [D, NOWN])
    ext["wq1T"] = EI("wq1T", [D, 1024])
    ext["b1q"] = EI("b1q", [128, 8])
    ext["wk1T"] = EI("wk1T", [D, 1024])
    ext["b1k"] = EI("b1k", [128, 8])
    ext["wv1T"] = EI("wv1T", [D, 1024])
    ext["b1v"] = EI("b1v", [128, 8])
    ext["wo1T"] = EI("wo1T", [D, D])
    ext["bo1_col"] = EI("bo1_col", [128, 16])
    ext["wfcT"] = EI("wfcT", [D, HID1])
    ext["wpjT"] = EI("wpjT", [HID1, D])
    ext["idxE"] = EI("idxE", [128, 64], I32)
    ext["x2T"] = nc.dram_tensor("x2T", [D, NOWN], F32, kind="ExternalOutput").ap()
    ext["DEBUG"] = DEBUG_L1
    if DEBUG_L1:
        ext["dbg_gam"] = nc.dram_tensor("dbg_gam", [128, 8], F32, kind="ExternalOutput").ap()
        ext["dbg_wf"] = nc.dram_tensor("dbg_wf", [128, HID1], F32, kind="ExternalOutput").ap()
        ext["dbg_attc"] = nc.dram_tensor("dbg_attc", [1024, NFULL], F32, kind="ExternalOutput").ap()
        ext["dbg_q"] = nc.dram_tensor("dbg_q", [128, NFULL], F32, kind="ExternalOutput").ap()
        ext["dbg_h"] = nc.dram_tensor("dbg_h", [128, NH1 * 512], F32, kind="ExternalOutput").ap()
        ext["dbg_xnf"] = nc.dram_tensor("dbg_xnf", [128, NC * 512], F32, kind="ExternalOutput").ap()
        ext["dbg_atE"] = nc.dram_tensor("dbg_atE", [128, NC * 512], F32, kind="ExternalOutput").ap()
    ext["xn1c"] = nc.dram_tensor("xn1c", [D, NOWN], BF16, kind="Internal").ap()
    ext["xn1g"] = nc.dram_tensor("xn1g", [2 * D, NOWN], BF16, kind="Internal").ap()
    ext["attc"] = nc.dram_tensor("attc", [1024, NFULL], BF16, kind="Internal").ap()
    ext["attg"] = nc.dram_tensor("attg", [2048, NFULL], BF16, kind="Internal").ap()
    ext["whatfc"] = nc.dram_tensor("whatfc", [D, HID1], BF16, kind="Internal").ap()
    ext["whatpj"] = nc.dram_tensor("whatpj", [HID1, D], BF16, kind="Internal").ap()

    B = Buf(nc)
    with nc.psum_tensor("PS", [128, 4096], F32) as PS:
        S = Sched(nc)
        build_common(S, B, nc)
        build_l1(S, B, ext, PS[:])
        S.finalize()
    return nc

def _l0_inmaps(inputs):
    x = np.asarray(inputs["L0_attn_x"])
    C = np.ascontiguousarray
    wqT = C(np.asarray(inputs["L0_attn_Wq"]).T)
    wkT = C(np.asarray(inputs["L0_attn_Wk"]).T)
    wvT = C(np.asarray(inputs["L0_attn_Wv"]).T)
    woT = C(np.asarray(inputs["L0_attn_Wo"]).T)
    bq = np.asarray(inputs["L0_attn_bq"])
    bk = np.asarray(inputs["L0_attn_bk"])
    bv = np.asarray(inputs["L0_attn_bv"])
    bo = np.asarray(inputs["L0_attn_bo"])
    wqgT = C(np.asarray(inputs["L0_ffn_wq_gate"]).T)
    wquT = C(np.asarray(inputs["L0_ffn_wq_up"]).T)
    wqdT = C(np.asarray(inputs["L0_ffn_wq_down"]).T)
    sg = np.asarray(inputs["L0_ffn_s_gate"])
    su = np.asarray(inputs["L0_ffn_s_up"])
    sd = np.asarray(inputs["L0_ffn_s_down"])
    col = lambda v, n: C(v.reshape(n, 128).T.astype(np.float32))
    maps = []
    for core in range(8):
        b, h = core // 2, core % 2
        xT = x[b].T
        own = xT[:, h * 2048:(h + 1) * 2048]
        oth = xT[:, (1 - h) * 2048:(2 - h) * 2048]
        hs = h * 1024
        maps.append({
            "x0T": C(np.concatenate([own, oth], axis=1)),
            "pos2": np.array([[h * 2048.0, (1 - h) * 2048.0]], np.float32),
            "wqT": wqT, "bq_col": col(bq, 16),
            "wkT_oh": C(wkT[:, hs:hs + 1024]), "bk_row": C(bk[None, hs:hs + 1024]),
            "wvT_oh": C(wvT[:, hs:hs + 1024]), "bv_row": C(bv[None, hs:hs + 1024]),
            "woT": woT, "bo_col": col(bo, 16),
            "wqgT": wqgT, "sg_col": col(sg, 44),
            "wquT": wquT, "su_col": col(su, 44),
            "wqdT": wqdT, "sd_col": col(sd, 16),
        })
    return maps


def _build_l0_nc():
    nc = bass.Bass("TRN2", target_bir_lowering=False, debug=False)
    ext = {}
    EI = lambda n, s, d=F32: nc.dram_tensor(n, s, d, kind="ExternalInput").ap()
    ext["x0T"] = EI("x0T", [D, NFULL])
    ext["pos2"] = EI("pos2", [1, 2])
    ext["wqT"] = EI("wqT", [D, D])
    ext["bq_col"] = EI("bq_col", [128, 16])
    ext["wkT_oh"] = EI("wkT_oh", [D, 1024])
    ext["bk_row"] = EI("bk_row", [1, 1024])
    ext["wvT_oh"] = EI("wvT_oh", [D, 1024])
    ext["bv_row"] = EI("bv_row", [1, 1024])
    ext["woT"] = EI("woT", [D, D])
    ext["bo_col"] = EI("bo_col", [128, 16])
    ext["wqgT"] = EI("wqgT", [D, HID0], I32)
    ext["sg_col"] = EI("sg_col", [128, 44])
    ext["wquT"] = EI("wquT", [D, HID0], I32)
    ext["su_col"] = EI("su_col", [128, 44])
    ext["wqdT"] = EI("wqdT", [HID0, D], I32)
    ext["sd_col"] = EI("sd_col", [128, 16])
    ext["x1T"] = nc.dram_tensor("x1T", [D, NOWN], F32, kind="ExternalOutput").ap()
    ext["xn1T"] = nc.dram_tensor("xn1T", [D, NOWN], BF16, kind="ExternalOutput").ap()
    ext["dbg_kv"] = nc.dram_tensor("dbg_kv", [128, 2048], F32, kind="ExternalOutput").ap()
    ext["dbg_ks"] = nc.dram_tensor("dbg_ks", [1, 2048], F32, kind="ExternalOutput").ap()
    ext["dbg_at"] = nc.dram_tensor("dbg_at", [D, NOWN], F32, kind="ExternalOutput").ap()
    ext["dbg_f0"] = nc.dram_tensor("dbg_f0", [D, NOWN], F32, kind="ExternalOutput").ap()
    ext["dbg_q"] = nc.dram_tensor("dbg_q", [D, NOWN], F32, kind="ExternalOutput").ap()
    ext["kvc"] = nc.dram_tensor("kvc", [1032, 128], F32, kind="Internal").ap()
    ext["kvg"] = nc.dram_tensor("kvg", [2064, 128], F32, kind="Internal").ap()
    ext["f0s"] = nc.dram_tensor("f0s", [D, NOWN], BF16, kind="Internal").ap()

    ext["DEBUG"] = DEBUG_L0
    B = Buf(nc)
    with nc.psum_tensor("PS", [128, 4096], F32) as PS:
        S = Sched(nc)
        build_common(S, B, nc)
        build_l0(S, B, ext, PS[:])
        S.finalize()
    return nc


def _build_fused_nc():
    nc = bass.Bass("TRN2", target_bir_lowering=False, debug=False)
    ext = {}
    EI = lambda n, s, d=F32: nc.dram_tensor(n, s, d, kind="ExternalInput").ap()
    IN = lambda n, s, d=F32: nc.dram_tensor(n, s, d, kind="Internal").ap()
    ext["x0T"] = EI("x0T", [D, NFULL])
    ext["pos2"] = EI("pos2", [1, 2])
    ext["wqT"] = EI("wqT", [D, D])
    ext["bq_col"] = EI("bq_col", [128, 16])
    ext["wkT_oh"] = EI("wkT_oh", [D, 1024])
    ext["bk_row"] = EI("bk_row", [1, 1024])
    ext["wvT_oh"] = EI("wvT_oh", [D, 1024])
    ext["bv_row"] = EI("bv_row", [1, 1024])
    ext["woT"] = EI("woT", [D, D])
    ext["bo_col"] = EI("bo_col", [128, 16])
    ext["wqgT"] = EI("wqgT", [D, HID0], I32)
    ext["sg_col"] = EI("sg_col", [128, 44])
    ext["wquT"] = EI("wquT", [D, HID0], I32)
    ext["su_col"] = EI("su_col", [128, 44])
    ext["wqdT"] = EI("wqdT", [HID0, D], I32)
    ext["sd_col"] = EI("sd_col", [128, 16])
    ext["wq1T"] = EI("wq1T", [D, 1024])
    ext["b1q"] = EI("b1q", [128, 8])
    ext["wk1T"] = EI("wk1T", [D, 1024])
    ext["b1k"] = EI("b1k", [128, 8])
    ext["wv1T"] = EI("wv1T", [D, 1024])
    ext["b1v"] = EI("b1v", [128, 8])
    ext["wo1T"] = EI("wo1T", [D, D])
    ext["bo1_col"] = EI("bo1_col", [128, 16])
    ext["wfcT"] = EI("wfcT", [D, HID1])
    ext["wpjT"] = EI("wpjT", [HID1, D])
    ext["idxE"] = EI("idxE", [128, 64], I32)
    ext["x2T"] = nc.dram_tensor("x2T", [D, NOWN], F32, kind="ExternalOutput").ap()
    ext["x1T"] = IN("x1s", [D, NOWN])
    ext["xn1T"] = IN("xn1s", [D, NOWN], BF16)
    ext["kvc"] = IN("kvc", [1032, 128])
    ext["kvg"] = IN("kvg", [2064, 128])
    ext["f0s"] = IN("f0s", [D, NOWN], BF16)
    ext["gus"] = IN("gus", [HID0, NOWN], BF16)
    ext["xn1c"] = IN("xn1c", [D, NOWN], BF16)
    ext["xn1g"] = IN("xn1g", [2 * D, NOWN], BF16)
    ext["attc"] = IN("attc", [1024, NFULL], BF16)
    ext["attg"] = IN("attg", [2048, NFULL], BF16)
    ext["whatfc"] = IN("whatfc", [D, HID1], BF16)
    ext["whatpj"] = IN("whatpj", [HID1, D], BF16)
    ext["DEBUG"] = False

    B = Buf(nc)
    with nc.psum_tensor("PS", [128, 4096], F32) as PS:
        S = Sched(nc)
        build_common(S, B, nc)
        build_l0(S, B, ext, PS[:])
        build_l1(S, B, ext, PS[:])
        S.finalize()
    return nc


FUSED = True
_NC_CACHE = {}


def kernel(**inputs):
    maps0 = _l0_inmaps(inputs)
    if FUSED:
        maps1 = _l1_inmaps(inputs, None)
        maps = [dict(m0, **m1) for m0, m1 in zip(maps0, maps1)]
        if "fused" not in _NC_CACHE:
            _NC_CACHE["fused"] = _build_fused_nc()
        res1 = run_bass_kernel_spmd(_NC_CACHE["fused"], maps, core_ids=list(range(8)))
    else:
        nc0 = _build_l0_nc()
        res0 = run_bass_kernel_spmd(nc0, maps0, core_ids=list(range(8)))
        maps1 = _l1_inmaps(inputs, res0.results)
        nc1 = _build_l1_nc()
        res1 = run_bass_kernel_spmd(nc1, maps1, core_ids=list(range(8)))
    outs = []
    for b in range(NB):
        xa = np.asarray(res1.results[2 * b]["x2T"], np.float32)
        xb_ = np.asarray(res1.results[2 * b + 1]["x2T"], np.float32)
        outs.append(np.concatenate([xa, xb_], axis=1).T)
    return np.stack(outs).astype(np.float32)


if __name__ == "__main__":
    nc = _build_l0_nc()
    n = sum(len(v) for v in Sched(bass.Bass("TRN2", target_bir_lowering=False)).ops.values())
    print("L0 build OK")
